# version 05
# baseline (speedup 1.0000x reference)
import numpy as np
import ml_dtypes
import concourse.bass as bass
import concourse.mybir as mybir
from concourse.bass_utils import run_bass_kernel_spmd

F32 = mybir.dt.float32
BF16 = mybir.dt.bfloat16
I32 = mybir.dt.int32
AF = mybir.ActivationFunctionType
OP = mybir.AluOpType
AX = mybir.AxisListType.X
BF16NP = ml_dtypes.bfloat16

D = 2048
FC = 16
TOK = 2048
SEQ = 4096
H = 16
DH = 128
IFF = 8192
TC = 512
NTC = TOK // TC
NTT = TOK // 128
NKT = SEQ // 128
E = 8
R = 16
EPS = 1e-5
ISQ = float(1.0 / np.sqrt(128.0))
NEG = -1.0e9

ENGS = ("sync", "tensor", "vector", "scalar", "gpsimd")
LSEMS = ("l0", "l1", "l2", "l3", "l4", "l5", "l6", "l7")
GSEMS = ("g0", "g1", "g2", "g3", "g4", "g5")
ALLSEMS = LSEMS + GSEMS + ("pe", "dve", "act", "cc")
DMASEMS = LSEMS + GSEMS + ("cc",)


class KB:

    def __init__(self, nc, blk, sems):
        self.nc = nc
        self.blk = blk
        self.sems = sems
        self.cnt = {k: 0 for k in sems}
        self.snap = {k: 0 for k in sems}
        self.waited = {e: {k: 0 for k in sems} for e in ENGS}

    def inc(self, name, k=1):
        self.cnt[name] += k
        return self.cnt[name]

    def barrier(self):
        self.snap = dict(self.cnt)

    def sec(self, eng, f):
        kb = self

        def wrapped(e):
            for k, v in kb.snap.items():
                if v > 0 and kb.waited[eng][k] < v:
                    e.wait_ge(kb.sems[k], v)
                    kb.waited[eng][k] = v
            f(e)

        getattr(self.blk, eng)(wrapped)


class DS:

    def __init__(self, kb, names):
        self.kb = kb
        self.names = tuple(names)
        self.k = 0
        self.base = {nm: kb.cnt[nm] for nm in self.names}

    def token(self, k):
        nm = self.names[k % len(self.names)]
        return (nm, self.base[nm] + 16 * (k // len(self.names) + 1))

    def issue(self, e, out=None, in_=None, **kw):
        nm, v = self.token(self.k)
        self.k += 1
        e.dma_start(out=out, in_=in_, **kw).then_inc(self.kb.sems[nm], 16)
        got = self.kb.inc(nm, 16)
        assert got == v, (nm, got, v)
        return (nm, v)

    def w(self, e, k):
        nm, v = self.token(k)
        e.wait_ge(self.kb.sems[nm], v)


def ph_transpose_td(kb, src, dst, sb, ps, nt_src, ident):
    nf = src.shape[1] // 128
    xin, tout = sb
    W = nf * 128
    LD = DS(kb, ("l0", "l1"))
    ST = DS(kb, ("g0", "g1"))
    pe0 = kb.cnt["pe"]
    dv0 = kb.cnt["dve"]

    def f_sync(e):
        for tt in range(nt_src):
            if tt >= 2:
                e.wait_ge(kb.sems["pe"], pe0 + nf * (tt - 1))
            LD.issue(e, out=xin[:, (tt % 2) * W:(tt % 2) * W + W],
                     in_=src[tt * 128:(tt + 1) * 128, :])
    kb.sec("sync", f_sync)

    def f_tensor(e):
        for tt in range(nt_src):
            LD.w(e, tt)
            for f in range(nf):
                k = tt * nf + f
                if k >= 2:
                    e.wait_ge(kb.sems["dve"], dv0 + (k - 1))
                e.transpose(out=ps[k % 2][:, 0:128],
                            in_=xin[:, (tt % 2) * W + f * 128:(tt % 2) * W + f * 128 + 128],
                            identity=ident[:]).then_inc(kb.sems["pe"], 1)
                kb.inc("pe")
    kb.sec("tensor", f_tensor)

    def f_vector(e):
        for tt in range(nt_src):
            if tt >= 2:
                ST.w(e, tt - 2)
            for f in range(nf):
                k = tt * nf + f
                e.wait_ge(kb.sems["pe"], pe0 + k + 1)
                e.tensor_copy(out=tout[:, (tt % 2) * W + f * 128:(tt % 2) * W + f * 128 + 128],
                              in_=ps[k % 2][:, 0:128]).then_inc(kb.sems["dve"], 1)
                kb.inc("dve")
    kb.sec("vector", f_vector)

    def f_gp(e):
        for tt in range(nt_src):
            e.wait_ge(kb.sems["dve"], dv0 + nf * (tt + 1))
            ST.issue(e, out=dst.rearrange("(f p) t -> p f t", p=128)[:, :, tt * 128:(tt + 1) * 128],
                     in_=tout[:, (tt % 2) * W:(tt % 2) * W + W].rearrange("p (f t) -> p f t", f=nf))
    kb.sec("gpsimd", f_gp)


def ph_ln_v2(kb, resid, xhat, sb, ps, consts):
    rfa, rb16, sq16, rowA, rowB, rowT, abc = sb
    ones_bf, ones_f32 = consts["ones_bf"], consts["ones_f32"]
    LD = DS(kb, ("l0", "l1"))
    ac0 = kb.cnt["act"]
    pe0 = kb.cnt["pe"]
    dv0 = kb.cnt["dve"]

    def f_sync(e):
        for tc in range(NTC):
            for fc in range(FC):
                k = tc * FC + fc
                if k >= 2:
                    e.wait_ge(kb.sems["act"], ac0 + 2 * (k - 1))
                LD.issue(e, out=rfa[:, (k % 2) * TC:(k % 2) * TC + TC],
                         in_=resid[fc * 128:(fc + 1) * 128, tc * TC:tc * TC + TC])
    kb.sec("sync", f_sync)

    def f_scalar(e):
        for tc in range(NTC):
            for fc in range(FC):
                k = tc * FC + fc
                LD.w(e, k)
                if k >= 2:
                    e.wait_ge(kb.sems["pe"], pe0 + 2 * (k - 1))
                e.activation(out=rb16[:, (k % 2) * TC:(k % 2) * TC + TC],
                             in_=rfa[:, (k % 2) * TC:(k % 2) * TC + TC],
                             func=AF.Copy).then_inc(kb.sems["act"], 1)
                e.activation(out=sq16[:, (k % 2) * TC:(k % 2) * TC + TC],
                             in_=rfa[:, (k % 2) * TC:(k % 2) * TC + TC],
                             func=AF.Square).then_inc(kb.sems["act"], 1)
                kb.inc("act", 2)
    kb.sec("scalar", f_scalar)

    def f_tensor(e):
        for tc in range(NTC):
            if tc >= 1:
                e.wait_ge(kb.sems["dve"], dv0 + 2 * tc)
            for fc in range(FC):
                k = tc * FC + fc
                e.wait_ge(kb.sems["act"], ac0 + 2 * (k + 1))
                st = (fc == 0)
                sp = (fc == FC - 1)
                e.matmul(out=ps[5][0:1, 0:TC], lhsT=ones_bf[:, 0:1],
                         rhs=rb16[:, (k % 2) * TC:(k % 2) * TC + TC],
                         start=st, stop=sp).then_inc(kb.sems["pe"], 1)
                kb.inc("pe")
                e.matmul(out=ps[6][0:1, 0:TC], lhsT=ones_bf[:, 0:1],
                         rhs=sq16[:, (k % 2) * TC:(k % 2) * TC + TC],
                         start=st, stop=sp).then_inc(kb.sems["pe"], 1)
                kb.inc("pe")
    kb.sec("tensor", f_tensor)

    def f_vector(e):
        for tc in range(NTC):
            e.wait_ge(kb.sems["pe"], pe0 + 2 * FC * (tc + 1))
            e.tensor_copy(out=rowA[0:1, tc * TC:tc * TC + TC],
                          in_=ps[5][0:1, 0:TC]).then_inc(kb.sems["dve"], 1)
            e.tensor_copy(out=rowB[0:1, tc * TC:tc * TC + TC],
                          in_=ps[6][0:1, 0:TC]).then_inc(kb.sems["dve"], 1)
            kb.inc("dve", 2)
    kb.sec("vector", f_vector)
    dv1 = kb.cnt["dve"]

    def f_vector2(e):
        def sw(inst):
            inst.then_inc(kb.sems["dve"], 1)
            e.wait_ge(kb.sems["dve"], kb.inc("dve"))
        e.wait_ge(kb.sems["dve"], dv1)
        sw(e.tensor_scalar(out=rowA[0:1, :], in0=rowA[0:1, :], scalar1=1.0 / D,
                           scalar2=None, op0=OP.mult))
        sw(e.tensor_scalar(out=rowB[0:1, :], in0=rowB[0:1, :], scalar1=1.0 / D,
                           scalar2=None, op0=OP.mult))
        sw(e.tensor_tensor(out=rowT[0:1, :], in0=rowA[0:1, :], in1=rowA[0:1, :],
                           op=OP.mult))
        sw(e.tensor_tensor(out=rowB[0:1, :], in0=rowB[0:1, :], in1=rowT[0:1, :],
                           op=OP.subtract))
        e.tensor_scalar(out=rowB[0:1, :], in0=rowB[0:1, :], scalar1=EPS,
                        scalar2=None, op0=OP.add).then_inc(kb.sems["dve"], 1)
        kb.inc("dve")
    kb.sec("vector", f_vector2)
    dv2 = kb.cnt["dve"]

    def f_scalar2(e):
        e.wait_ge(kb.sems["dve"], dv2)
        e.activation(out=rowT[0:1, :], in_=rowB[0:1, :],
                     func=AF.Sqrt).then_inc(kb.sems["act"], 1)
        kb.inc("act")
    kb.sec("scalar", f_scalar2)
    ac2 = kb.cnt["act"]

    def f_vector3(e):
        def sw(inst):
            inst.then_inc(kb.sems["dve"], 1)
            e.wait_ge(kb.sems["dve"], kb.inc("dve"))
        e.wait_ge(kb.sems["act"], ac2)
        sw(e.reciprocal(out=rowB[0:1, :], in_=rowT[0:1, :]))
        sw(e.tensor_tensor(out=rowA[0:1, :], in0=rowA[0:1, :], in1=rowB[0:1, :],
                           op=OP.mult))
        e.tensor_scalar(out=rowA[0:1, :], in0=rowA[0:1, :], scalar1=-1.0, scalar2=None,
                        op0=OP.mult).then_inc(kb.sems["dve"], 1)
        kb.inc("dve")
    kb.sec("vector", f_vector3)
    dv3 = kb.cnt["dve"]

    pe1 = kb.cnt["pe"]
    sched = []
    carry = None
    for tc in range(NTC):
        sched.append(("copyA", tc))
        if carry is not None:
            sched.append(("add", carry))
            carry = None
        sched.append(("copyB", tc))
        prev = None
        for fc in range(FC):
            k = tc * FC + fc
            sched.append(("mult", k))
            if prev is not None:
                sched.append(("add", prev))
            prev = k
        carry = prev
    sched.append(("add", carry))
    add_tick = {}
    copyB_tick = {}
    t = dv3
    for op, arg in sched:
        t += 1
        if op == "add":
            add_tick[arg] = t
        elif op == "copyB":
            copyB_tick[arg] = t

    def f_tensor2(e):
        e.wait_ge(kb.sems["dve"], dv3)
        for tc in range(NTC):
            if tc >= 1:
                e.wait_ge(kb.sems["dve"], copyB_tick[tc - 1])
            e.matmul(out=ps[5][:, 0:TC], lhsT=ones_f32[0:1, :],
                     rhs=rowB[0:1, tc * TC:tc * TC + TC], start=True, stop=True
                     ).then_inc(kb.sems["pe"], 1)
            kb.inc("pe")
            e.matmul(out=ps[6][:, 0:TC], lhsT=ones_f32[0:1, :],
                     rhs=rowA[0:1, tc * TC:tc * TC + TC], start=True, stop=True
                     ).then_inc(kb.sems["pe"], 1)
            kb.inc("pe")
    kb.sec("tensor", f_tensor2)

    def f_sync2(e):
        for tc in range(NTC):
            for fc in range(FC):
                k = tc * FC + fc
                if k >= 2:
                    e.wait_ge(kb.sems["dve"], add_tick[k - 2])
                elif k == 0:
                    e.wait_ge(kb.sems["dve"], dv3)
                LD.issue(e, out=rfa[:, (k % 2) * TC:(k % 2) * TC + TC],
                         in_=resid[fc * 128:(fc + 1) * 128, tc * TC:tc * TC + TC])
    kb.sec("sync", f_sync2)

    def f_vector4(e):
        for op, arg in sched:
            if op == "copyA":
                e.wait_ge(kb.sems["pe"], pe1 + 2 * (arg + 1))
                e.tensor_copy(out=abc[:, 0:TC], in_=ps[5][:, 0:TC]).then_inc(kb.sems["dve"], 1)
            elif op == "copyB":
                e.tensor_copy(out=abc[:, TC:2 * TC], in_=ps[6][:, 0:TC]).then_inc(kb.sems["dve"], 1)
            elif op == "mult":
                k = arg
                LD.w(e, 64 + k)
                e.tensor_tensor(out=rfa[:, (k % 2) * TC:(k % 2) * TC + TC],
                                in0=rfa[:, (k % 2) * TC:(k % 2) * TC + TC],
                                in1=abc[:, 0:TC], op=OP.mult).then_inc(kb.sems["dve"], 1)
            else:
                k = arg
                tc, fc = k // FC, k % FC
                e.tensor_tensor(out=xhat[:, fc * TOK + tc * TC:fc * TOK + tc * TC + TC],
                                in0=rfa[:, (k % 2) * TC:(k % 2) * TC + TC],
                                in1=abc[:, TC:2 * TC], op=OP.add).then_inc(kb.sems["dve"], 1)
            kb.inc("dve")
    kb.sec("vector", f_vector4)


def ph_proj_fm(kb, xhat, w_pm, bias_pm, n_oc, dest, dest_row0, sb, ps):
    wbf, stage = sb
    WL = DS(kb, ("g0", "g1"))
    SO = DS(kb, ("g2", "g3"))
    pe0 = kb.cnt["pe"]
    dv0 = kb.cnt["dve"]

    def mm_tick(oc, tc):
        return pe0 + oc * NTC + tc + 1
    kb.cnt["pe"] = pe0 + n_oc * NTC

    def ev_tick(oc, tc):
        return dv0 + oc * NTC + tc + 1
    kb.cnt["dve"] = dv0 + n_oc * NTC

    if n_oc == 1:
        ops = [("ld", 0), ("st", 0)]
    else:
        ops = [("ld", 0), ("ld", 1)]
        for oc in range(2, n_oc):
            ops.append(("st", oc - 2))
            ops.append(("ld", oc))
        ops += [("st", n_oc - 2), ("st", n_oc - 1)]

    def f_gp(e):
        for kind, oc in ops:
            if kind == "ld":
                if oc >= 2:
                    e.wait_ge(kb.sems["pe"], mm_tick(oc - 2, NTC - 1))
                WL.issue(e, out=wbf[:, (oc % 2) * FC * 128:(oc % 2) * FC * 128 + FC * 128],
                         in_=w_pm[oc * 128:(oc + 1) * 128, :])
            else:
                e.wait_ge(kb.sems["dve"], ev_tick(oc, NTC - 1))
                SO.issue(e, out=dest[dest_row0 + oc * 128:dest_row0 + (oc + 1) * 128, :],
                         in_=stage[:, (oc % 2) * TOK:(oc % 2) * TOK + TOK])
    kb.sec("gpsimd", f_gp)

    def f_tensor(e):
        for oc in range(n_oc):
            WL.w(e, oc)
            for tc in range(NTC):
                gidx = oc * NTC + tc
                if gidx >= 2:
                    e.wait_ge(kb.sems["dve"], ev_tick((gidx - 2) // NTC, (gidx - 2) % NTC))
                for fc in range(FC):
                    ins = e.matmul(out=ps[gidx % 2][:, 0:TC],
                                   lhsT=wbf[:, (oc % 2) * FC * 128 + fc * 128:(oc % 2) * FC * 128 + fc * 128 + 128],
                                   rhs=xhat[:, fc * TOK + tc * TC:fc * TOK + tc * TC + TC],
                                   start=(fc == 0), stop=(fc == FC - 1))
                ins.then_inc(kb.sems["pe"], 1)
    kb.sec("tensor", f_tensor)

    def f_vector(e):
        for oc in range(n_oc):
            for tc in range(NTC):
                gidx = oc * NTC + tc
                e.wait_ge(kb.sems["pe"], mm_tick(oc, tc))
                if oc >= 2 and tc == 0:
                    SO.w(e, oc - 2)
                if bias_pm is not None:
                    e.tensor_scalar(out=stage[:, (oc % 2) * TOK + tc * TC:(oc % 2) * TOK + tc * TC + TC],
                                    in0=ps[gidx % 2][:, 0:TC],
                                    scalar1=bias_pm[:, oc:oc + 1], scalar2=None,
                                    op0=OP.add).then_inc(kb.sems["dve"], 1)
                else:
                    e.tensor_copy(out=stage[:, (oc % 2) * TOK + tc * TC:(oc % 2) * TOK + tc * TC + TC],
                                  in_=ps[gidx % 2][:, 0:TC]).then_inc(kb.sems["dve"], 1)
    kb.sec("vector", f_vector)


def ph_proj_v(kb, xhat, wv_pm, bv_row, n_vg, VG, dest, dest_row0, sb, ps, consts):
    wbf, vgst, bvb = sb
    ones_f32 = consts["ones_f32"]
    HPG = VG // 128
    WL = DS(kb, ("g0", "g1"))
    SO = DS(kb, ("g2", "g3"))
    pe0 = kb.cnt["pe"]
    dv0 = kb.cnt["dve"]

    def bc_tick(vg):
        return pe0 + vg * (NTT + 1) + 1

    def mm_tick(vg, tt):
        return pe0 + vg * (NTT + 1) + 1 + tt + 1
    kb.cnt["pe"] = pe0 + n_vg * (NTT + 1)

    def bvb_tick(vg):
        return dv0 + vg * (NTT + 1) + 1

    def ev_tick(vg, tt):
        return dv0 + vg * (NTT + 1) + 1 + tt + 1
    kb.cnt["dve"] = dv0 + n_vg * (NTT + 1)

    ops = []
    for vg in range(n_vg):
        ops.append(("ld", vg))
        if vg >= 2:
            ops.append(("st", vg - 2))
    if n_vg >= 2:
        ops.append(("st", n_vg - 2))
    ops.append(("st", n_vg - 1))

    def f_gp(e):
        for kind, vg in ops:
            if kind == "ld":
                if vg >= 2:
                    e.wait_ge(kb.sems["pe"], mm_tick(vg - 2, NTT - 1))
                WL.issue(e, out=wbf[:, (vg % 2) * FC * VG:(vg % 2) * FC * VG + FC * VG],
                         in_=wv_pm[vg * 128:(vg + 1) * 128, :])
            else:
                e.wait_ge(kb.sems["dve"], ev_tick(vg, NTT - 1))
                for hh in range(HPG):
                    h = vg * HPG + hh
                    SO.issue(e, out=dest[dest_row0 + h * 128:dest_row0 + (h + 1) * 128, :],
                             in_=vgst[:, hh * TOK:(hh + 1) * TOK])
    kb.sec("gpsimd", f_gp)

    def f_tensor(e):
        for vg in range(n_vg):
            if vg >= 1:
                e.wait_ge(kb.sems["dve"], bvb_tick(vg - 1))
            e.matmul(out=ps[4][:, 0:VG], lhsT=ones_f32[0:1, :],
                     rhs=bv_row[0:1, vg * VG:(vg + 1) * VG],
                     start=True, stop=True).then_inc(kb.sems["pe"], 1)
            WL.w(e, vg)
            for tt in range(NTT):
                gidx = vg * NTT + tt
                if gidx >= 2:
                    e.wait_ge(kb.sems["dve"], ev_tick((gidx - 2) // NTT, (gidx - 2) % NTT))
                for fc in range(FC):
                    ins = e.matmul(out=ps[gidx % 2][:, 0:VG],
                                   lhsT=xhat[:, fc * TOK + tt * 128:fc * TOK + tt * 128 + 128],
                                   rhs=wbf[:, (vg % 2) * FC * VG + fc * VG:(vg % 2) * FC * VG + fc * VG + VG],
                                   start=(fc == 0), stop=(fc == FC - 1))
                ins.then_inc(kb.sems["pe"], 1)
    kb.sec("tensor", f_tensor)

    def f_vector(e):
        for vg in range(n_vg):
            e.wait_ge(kb.sems["pe"], bc_tick(vg))
            e.tensor_copy(out=bvb[:, 0:VG], in_=ps[4][:, 0:VG]).then_inc(kb.sems["dve"], 1)
            e.wait_ge(kb.sems["dve"], bvb_tick(vg))
            if vg >= 1:
                SO.w(e, vg * HPG - 1)
            for tt in range(NTT):
                gidx = vg * NTT + tt
                e.wait_ge(kb.sems["pe"], mm_tick(vg, tt))
                e.tensor_tensor(out=vgst[:].rearrange("p (hh t) -> p hh t", hh=HPG)[:, :, tt * 128:tt * 128 + 128],
                                in0=ps[gidx % 2][:, 0:VG].rearrange("p (hh dh) -> p hh dh", hh=HPG),
                                in1=bvb[:, 0:VG].rearrange("p (hh dh) -> p hh dh", hh=HPG),
                                op=OP.add).then_inc(kb.sems["dve"], 1)
    kb.sec("vector", f_vector)
def ph_attn(kb, layer, qbuf, kvfull, abuf, sb, ps, consts, qmode=None):
    (ksb, vsb, qsb, ptsb, msksb, tmpsb, asb, qbrow, recb, srw, qrhs, qtmp) = sb
    ones_f32 = consts["ones_f32"]
    ones_bf = consts["ones_bf"]
    isq_vec = consts["isq_vec"]
    midbias = consts["midbias"]
    kposB = consts["kposB"]
    qposB = consts["qposB"]
    mqa = (layer == 1)
    SEQH = SEQ // 2

    if layer == 0:
        nkt = [20 + 4 * qc for qc in range(NTC)]
        masked = [list(range(4 * qc, 4 * qc + 4)) + list(range(16 + 4 * qc, 20 + 4 * qc))
                  for qc in range(NTC)]
        midt = [set(range(4 * qc + 4, 16 + 4 * qc)) for qc in range(NTC)]
        kv_row0 = [0, D]
        rankrows = 2 * D
    else:
        nkt = [NKT] * NTC
        masked = [[] for _ in range(NTC)]
        midt = [set() for _ in range(NTC)]
        kv_row0 = [0, DH]
        rankrows = 2 * DH

    if layer == 0:
        pe0 = kb.cnt["pe"]
        dv0 = kb.cnt["dve"]
        QP = DS(kb, ("l4",))

        def f_syncq(e):
            QP.issue(e, out=qposB, in_=consts["qposB_dram"])
        kb.sec("sync", f_syncq)

        def cp_tick(qc):
            return dv0 + qc * 9 + 1

        def f_tensor_m(e):
            QP.w(e, 0)
            for qc in range(NTC):
                if qc >= 1:
                    e.wait_ge(kb.sems["dve"], cp_tick(qc - 1))
                e.matmul(out=ps[4][:, 0:TC], lhsT=ones_f32[0:1, :],
                         rhs=qposB[0:1, qc * TC:(qc + 1) * TC],
                         start=True, stop=True).then_inc(kb.sems["pe"], 1)
                kb.inc("pe")
        kb.sec("tensor", f_tensor_m)

        def f_vector_m(e):
            for qc in range(NTC):
                e.wait_ge(kb.sems["pe"], pe0 + qc + 1)
                e.tensor_copy(out=qbrow[:, 0:TC], in_=ps[4][:, 0:TC]).then_inc(kb.sems["dve"], 1)
                e.wait_ge(kb.sems["dve"], kb.inc("dve"))
                for j, t in enumerate(masked[qc]):
                    e.tensor_scalar(out=msksb[:, (qc * 8 + j) * TC:(qc * 8 + j) * TC + TC],
                                    in0=qbrow[:, 0:TC], scalar1=kposB[:, t:t + 1],
                                    scalar2=0.0, op0=OP.subtract, op1=OP.min
                                    ).then_inc(kb.sems["dve"], 1)
                    kb.inc("dve")
        kb.sec("vector", f_vector_m)
        kb.barrier()

    nh = H
    scram = (layer == 0)
    if qmode is None:
        qmode = "host" if scram else "plain"
    fusedq = (qmode == "fused")
    if fusedq:
        rankrows = 3 * D
    KV = DS(kb, ("l0", "l1", "l2", "l3"))
    QL = DS(kb, ("l4", "l5", "l6", "l7") if fusedq else ("l4", "l5"))
    AS = DS(kb, ("g0", "g1"))
    QW = H * TOK
    if fusedq:
        QW = 16 * 256

    pe_t = kb.cnt["pe"]
    act_t = kb.cnt["act"]
    dve_t = kb.cnt["dve"]
    S_tick = {}
    RS_tick = {}
    BC_tick = {}
    EXP_tick = {}
    MADD_tick = {}
    EV_tick = {}
    GA_tick = {}
    for h in range(nh):
        for qc in range(NTC):
            n = nkt[qc]
            mset = set(masked[qc])
            if scram:
                dve_t += 3 if fusedq else 1
                GA_tick[(h, qc)] = dve_t
            for kt in range(n):
                pe_t += 1
                S_tick[(h, qc, kt)] = pe_t
                if kt >= 1:
                    pe_t += 1
                    RS_tick[(h, qc, kt - 1)] = pe_t
                if kt in mset:
                    dve_t += 1
                    MADD_tick[(h, qc, kt)] = dve_t
                act_t += 1
                EXP_tick[(h, qc, kt)] = act_t
            pe_t += 1
            RS_tick[(h, qc, n - 1)] = pe_t
            dve_t += 1
            pe_t += 1
            BC_tick[(h, qc)] = pe_t
            dve_t += 1
            dve_t += 1
            EV_tick[(h, qc)] = dve_t
    kb.cnt["pe"] = pe_t
    kb.cnt["act"] = act_t
    kb.cnt["dve"] = dve_t

    def prev_hqc(h, qc):
        return (h, qc - 1) if qc > 0 else (h - 1, NTC - 1)

    def bank_free_tick(h, qc, kt):
        if kt in set(masked[qc]):
            return ("dve", MADD_tick[(h, qc, kt)])
        return ("act", EXP_tick[(h, qc, kt)])

    def f_sync(e):
        for h in range(nh):
            if (not mqa) or h == 0:
                if (not mqa) and h >= 2:
                    e.wait_ge(kb.sems["pe"], RS_tick[(h - 2, NTC - 1, nkt[NTC - 1] - 1)])
                slot = 0 if mqa else (h % 2)
                krow = kv_row0[0] + (0 if mqa else h * 128)
                vrow = kv_row0[1] + (0 if mqa else h * 128)
                KV.issue(e, out=ksb[:, slot * SEQ:(slot + 1) * SEQ].rearrange("p (r t) -> p r t", r=2),
                         in_=kvfull.rearrange("(r z) t -> z r t", r=2)[krow:krow + 128, :, :])
                KV.issue(e, out=vsb[:, slot * SEQ:(slot + 1) * SEQ].rearrange("p (r t) -> p r t", r=2),
                         in_=kvfull.rearrange("(r z) t -> z r t", r=2)[vrow:vrow + 128, :, :])
            if h >= 2:
                if scram:
                    e.wait_ge(kb.sems["dve"], GA_tick[(h - 2, NTC - 1)])
                else:
                    e.wait_ge(kb.sems["pe"], S_tick[(h - 2, NTC - 1, nkt[NTC - 1] - 1)])
            if fusedq:
                r = h // 8
                qrow = r * 3 * D + 2 * D
                qcol = 256 * (h % 8)
                for fh in range(2):
                    QL.issue(e, out=qsb[:, (h % 2) * QW + fh * 8 * 256:(h % 2) * QW + (fh + 1) * 8 * 256
                                        ].rearrange("p (fc g) -> p fc g", fc=8),
                             in_=qbuf[qrow + fh * 8 * 128:qrow + (fh + 1) * 8 * 128,
                                      qcol:qcol + 256].rearrange("(fc p) g -> p fc g", p=128))
            elif scram:
                QL.issue(e, out=qsb[:, (h % 2) * TOK:(h % 2) * TOK + TOK],
                         in_=qbuf[:, h * TOK:(h + 1) * TOK])
            else:
                QL.issue(e, out=qsb[:, (h % 2) * TOK:(h % 2) * TOK + TOK],
                         in_=qbuf[h * 128:(h + 1) * 128, :])
    kb.sec("sync", f_sync)

    def f_tensor(e):
        for h in range(nh):
            kslot = 0 if mqa else (h % 2)
            if mqa:
                KV.w(e, 0)
                KV.w(e, 1)
            else:
                KV.w(e, 2 * h)
                KV.w(e, 2 * h + 1)
            QL.w(e, h)
            for qc in range(NTC):
                n = nkt[qc]
                for kt in range(n):
                    if kt >= 2:
                        sem, v = bank_free_tick(h, qc, kt - 2)
                        e.wait_ge(kb.sems[sem], v)
                    elif qc > 0 or h > 0:
                        ph_, pqc = prev_hqc(h, qc)
                        pn = nkt[pqc]
                        sem, v = bank_free_tick(ph_, pqc, pn - 2 + kt)
                        e.wait_ge(kb.sems[sem], v)
                    if kt == 0 and scram:
                        e.wait_ge(kb.sems["dve"], GA_tick[(h, qc)])
                    if scram:
                        s_rhs = qrhs[:, (qc % 2) * TC:(qc % 2) * TC + TC]
                    else:
                        s_rhs = qsb[:, (h % 2) * TOK + qc * TC:(h % 2) * TOK + qc * TC + TC]
                    e.matmul(out=ps[kt % 2][:, 0:TC],
                             lhsT=ksb[:, kslot * SEQ + kt * 128:kslot * SEQ + kt * 128 + 128],
                             rhs=s_rhs,
                             start=True, stop=True).then_inc(kb.sems["pe"], 1)
                    if kt >= 1:
                        kp = kt - 1
                        e.wait_ge(kb.sems["act"], EXP_tick[(h, qc, kp)])
                        if kp == 0 and (qc > 0 or h > 0):
                            ph_, pqc = prev_hqc(h, qc)
                            e.wait_ge(kb.sems["dve"], EV_tick[(ph_, pqc)])
                        e.matmul(out=ps[2][:, 0:TC],
                                 lhsT=vsb[:, kslot * SEQ + kp * 128:kslot * SEQ + kp * 128 + 128],
                                 rhs=ptsb[:, (kp % 2) * TC:(kp % 2) * TC + TC],
                                 start=(kp == 0), stop=False)
                        e.matmul(out=ps[3][0:1, 0:TC],
                                 lhsT=ones_bf[:, 0:1],
                                 rhs=ptsb[:, (kp % 2) * TC:(kp % 2) * TC + TC],
                                 start=(kp == 0), stop=False).then_inc(kb.sems["pe"], 1)
                kp = n - 1
                e.wait_ge(kb.sems["act"], EXP_tick[(h, qc, kp)])
                if kp == 0 and (qc > 0 or h > 0):
                    ph_, pqc = prev_hqc(h, qc)
                    e.wait_ge(kb.sems["dve"], EV_tick[(ph_, pqc)])
                e.matmul(out=ps[2][:, 0:TC],
                         lhsT=vsb[:, kslot * SEQ + kp * 128:kslot * SEQ + kp * 128 + 128],
                         rhs=ptsb[:, (kp % 2) * TC:(kp % 2) * TC + TC],
                         start=(kp == 0), stop=True)
                e.matmul(out=ps[3][0:1, 0:TC],
                         lhsT=ones_bf[:, 0:1],
                         rhs=ptsb[:, (kp % 2) * TC:(kp % 2) * TC + TC],
                         start=(kp == 0), stop=True).then_inc(kb.sems["pe"], 1)
                e.wait_ge(kb.sems["dve"], EV_tick[(h, qc)] - 2)
                e.matmul(out=ps[4][:, 0:TC], lhsT=ones_f32[0:1, :],
                         rhs=srw[0:1, 0:TC], start=True, stop=True).then_inc(kb.sems["pe"], 1)
    kb.sec("tensor", f_tensor)

    def f_scalar(e):
        for h in range(nh):
            for qc in range(NTC):
                n = nkt[qc]
                mset = set(masked[qc])
                for kt in range(n):
                    if kt >= 2:
                        e.wait_ge(kb.sems["pe"], RS_tick[(h, qc, kt - 2)])
                    elif qc > 0 or h > 0:
                        ph_, pqc = prev_hqc(h, qc)
                        pn = nkt[pqc]
                        e.wait_ge(kb.sems["pe"], RS_tick[(ph_, pqc, pn - 2 + kt)])
                    if kt in mset:
                        e.wait_ge(kb.sems["dve"], MADD_tick[(h, qc, kt)])
                        e.activation(out=ptsb[:, (kt % 2) * TC:(kt % 2) * TC + TC],
                                     in_=tmpsb[:, (kt % 2) * TC:(kt % 2) * TC + TC],
                                     func=AF.Exp,
                                     scale=isq_vec[:, 0:1]).then_inc(kb.sems["act"], 1)
                    elif kt in midt[qc]:
                        e.wait_ge(kb.sems["pe"], S_tick[(h, qc, kt)])
                        e.activation(out=ptsb[:, (kt % 2) * TC:(kt % 2) * TC + TC],
                                     in_=ps[kt % 2][:, 0:TC], func=AF.Exp,
                                     scale=isq_vec[:, 0:1],
                                     bias=midbias[:, 0:1]).then_inc(kb.sems["act"], 1)
                    else:
                        e.wait_ge(kb.sems["pe"], S_tick[(h, qc, kt)])
                        e.activation(out=ptsb[:, (kt % 2) * TC:(kt % 2) * TC + TC],
                                     in_=ps[kt % 2][:, 0:TC], func=AF.Exp,
                                     scale=isq_vec[:, 0:1]).then_inc(kb.sems["act"], 1)
    kb.sec("scalar", f_scalar)

    def f_vector(e):
        for h in range(nh):
            for qc in range(NTC):
                n = nkt[qc]
                if scram:
                    if fusedq:
                        QL.w(e, 2 * h)
                        QL.w(e, 2 * h + 1)
                    else:
                        QL.w(e, h)
                    if qc >= 2:
                        e.wait_ge(kb.sems["pe"], S_tick[(h, qc - 2, nkt[qc - 2] - 1)])
                    elif h >= 1:
                        e.wait_ge(kb.sems["pe"], S_tick[(h - 1, NTC - 2 + qc, nkt[NTC - 2 + qc] - 1)])
                    if fusedq:
                        qv = qsb[:, (h % 2) * QW:(h % 2) * QW + QW].rearrange("p (m g) -> p g m", m=16)
                        e.tensor_scalar(
                            out=qtmp[:, 0:TC].rearrange("p (g m) -> p g m", m=16),
                            in0=qv[:, 128 + qc * 32:128 + (qc + 1) * 32, :],
                            scalar1=consts["qw1"][:, 0:1], scalar2=None,
                            op0=OP.mult).then_inc(kb.sems["dve"], 1)
                        e.tensor_scalar(
                            out=qrhs[:, (qc % 2) * TC:(qc % 2) * TC + TC].rearrange("p (g m) -> p g m", m=16),
                            in0=qv[:, qc * 32:(qc + 1) * 32, :],
                            scalar1=consts["qw0"][:, 0:1], scalar2=None,
                            op0=OP.mult).then_inc(kb.sems["dve"], 1)
                        e.wait_ge(kb.sems["dve"], GA_tick[(h, qc)] - 1)
                        e.tensor_tensor(
                            out=qrhs[:, (qc % 2) * TC:(qc % 2) * TC + TC],
                            in0=qrhs[:, (qc % 2) * TC:(qc % 2) * TC + TC],
                            in1=qtmp[:, 0:TC],
                            op=OP.add).then_inc(kb.sems["dve"], 1)
                    else:
                        e.tensor_copy(
                            out=qrhs[:, (qc % 2) * TC:(qc % 2) * TC + TC].rearrange("p (g m) -> p g m", m=16),
                            in_=qsb[:, (h % 2) * TOK:(h % 2) * TOK + TOK].rearrange(
                                "p (m g) -> p g m", m=16)[:, qc * 32:(qc + 1) * 32, :]
                        ).then_inc(kb.sems["dve"], 1)
                for kt in range(n):
                    if kt in set(masked[qc]):
                        e.wait_ge(kb.sems["pe"], S_tick[(h, qc, kt)])
                        if kt >= 2:
                            e.wait_ge(kb.sems["act"], EXP_tick[(h, qc, kt - 2)])
                        elif qc > 0 or h > 0:
                            ph_, pqc = prev_hqc(h, qc)
                            pn = nkt[pqc]
                            e.wait_ge(kb.sems["act"], EXP_tick[(ph_, pqc, pn - 2 + kt)])
                        j = masked[qc].index(kt)
                        e.tensor_tensor(out=tmpsb[:, (kt % 2) * TC:(kt % 2) * TC + TC],
                                        in0=ps[kt % 2][:, 0:TC],
                                        in1=msksb[:, (qc * 8 + j) * TC:(qc * 8 + j) * TC + TC],
                                        op=OP.add).then_inc(kb.sems["dve"], 1)
                e.wait_ge(kb.sems["pe"], RS_tick[(h, qc, n - 1)])
                e.reciprocal(out=srw[0:1, 0:TC], in_=ps[3][0:1, 0:TC]).then_inc(kb.sems["dve"], 1)
                e.wait_ge(kb.sems["dve"], EV_tick[(h, qc)] - 2)
                e.wait_ge(kb.sems["pe"], BC_tick[(h, qc)])
                e.tensor_copy(out=recb[:, 0:TC], in_=ps[4][:, 0:TC]).then_inc(kb.sems["dve"], 1)
                e.wait_ge(kb.sems["dve"], EV_tick[(h, qc)] - 1)
                if h >= 2 and qc == 0:
                    AS.w(e, h - 2)
                e.tensor_tensor(out=asb[:, (h % 2) * TOK + qc * TC:(h % 2) * TOK + qc * TC + TC],
                                in0=ps[2][:, 0:TC], in1=recb[:, 0:TC],
                                op=OP.mult).then_inc(kb.sems["dve"], 1)
    kb.sec("vector", f_vector)

    def f_gp(e):
        for h in range(nh):
            e.wait_ge(kb.sems["dve"], EV_tick[(h, NTC - 1)])
            AS.issue(e, out=abuf[h * 128:(h + 1) * 128, :],
                     in_=asb[:, (h % 2) * TOK:(h % 2) * TOK + TOK])
    kb.sec("gpsimd", f_gp)


def ph_wo(kb, abuf, wo_dram, bo_pm, resid, sb, ps):
    wosb, atsb, rsb, rosb = sb
    for half in range(2):
        pe0 = kb.cnt["pe"]
        dv0 = kb.cnt["dve"]
        nidx = NTC * 8
        AT = DS(kb, ("l0", "l1"))
        RS = DS(kb, ("l2", "l3"))
        WOL = DS(kb, ("g0", "g1"))
        WB = DS(kb, ("g2", "g3", "g4", "g5"))

        dsched = [("ar", 0), ("ar", 1)]
        for i in range(2, nidx):
            dsched.append(("ab", i - 2))
            dsched.append(("ar", i))
        dsched += [("ab", nidx - 2), ("ab", nidx - 1)]
        ar_tick = {}
        ab_tick = {}
        t = dv0
        for kind, i in dsched:
            t += 1
            (ar_tick if kind == "ar" else ab_tick)[i] = t
        kb.cnt["dve"] = t

        def mm_tick(i, pe0=pe0):
            return pe0 + i + 1
        kb.cnt["pe"] = pe0 + nidx

        def f_gp(e, half=half, pe0=pe0, ab_tick=ab_tick, WOL=WOL, WB=WB):
            e.wait_ge(kb.sems["pe"], pe0)
            WOL.issue(e, out=wosb[:],
                      in_=wo_dram.rearrange("(oc p) o2 -> p oc o2", p=128)[:, :, half * 1024:(half + 1) * 1024])
            for i in range(nidx):
                e.wait_ge(kb.sems["dve"], ab_tick[i])
                tc, j = i // 8, i % 8
                o2 = half * 8 + j
                WB.issue(e, out=resid[o2 * 128:(o2 + 1) * 128, tc * TC:(tc + 1) * TC],
                         in_=rosb[:, (i % 4) * TC:(i % 4) * TC + TC])
        kb.sec("gpsimd", f_gp)

        def f_sync(e, half=half, pe0=pe0, dv0=dv0, ar_tick=ar_tick, AT=AT, RS=RS, mm_tick=mm_tick):
            for tc in range(NTC):
                if tc >= 2:
                    e.wait_ge(kb.sems["pe"], mm_tick((tc - 1) * 8 - 1))
                else:
                    e.wait_ge(kb.sems["pe"], pe0)
                AT.issue(e, out=atsb[:, (tc % 2) * FC * TC:(tc % 2) * FC * TC + FC * TC],
                         in_=abuf.rearrange("(oc p) t -> p oc t", p=128)[:, :, tc * TC:(tc + 1) * TC])
                for j in range(8):
                    i = tc * 8 + j
                    if i >= 2:
                        e.wait_ge(kb.sems["dve"], ar_tick[i - 2])
                    elif half == 1:
                        e.wait_ge(kb.sems["dve"], dv0)
                    RS.issue(e, out=rsb[:, (i % 4) * TC:(i % 4) * TC + TC],
                             in_=resid[(half * 8 + j) * 128:(half * 8 + j + 1) * 128,
                                       tc * TC:(tc + 1) * TC])
        kb.sec("sync", f_sync)

        def f_tensor(e, half=half, AT=AT, WOL=WOL, ar_tick=ar_tick, mm_tick=mm_tick):
            WOL.w(e, 0)
            for tc in range(NTC):
                AT.w(e, tc)
                for j in range(8):
                    i = tc * 8 + j
                    if i >= 2:
                        e.wait_ge(kb.sems["dve"], ar_tick[i - 2])
                    for oc in range(FC):
                        ins = e.matmul(out=ps[i % 2][:, 0:TC],
                                       lhsT=wosb[:, oc * 1024 + j * 128:oc * 1024 + j * 128 + 128],
                                       rhs=atsb[:, (tc % 2) * FC * TC + oc * TC:(tc % 2) * FC * TC + oc * TC + TC],
                                       start=(oc == 0), stop=(oc == FC - 1))
                    ins.then_inc(kb.sems["pe"], 1)
        kb.sec("tensor", f_tensor)

        def f_vector(e, half=half, dsched=dsched, RS=RS, WB=WB, mm_tick=mm_tick):
            for kind, i in dsched:
                tc, j = i // 8, i % 8
                if kind == "ar":
                    e.wait_ge(kb.sems["pe"], mm_tick(i))
                    RS.w(e, i)
                    if i >= 4:
                        WB.w(e, i - 4)
                    e.tensor_tensor(out=rosb[:, (i % 4) * TC:(i % 4) * TC + TC],
                                    in0=ps[i % 2][:, 0:TC],
                                    in1=rsb[:, (i % 4) * TC:(i % 4) * TC + TC],
                                    op=OP.add).then_inc(kb.sems["dve"], 1)
                else:
                    o2 = half * 8 + j
                    e.tensor_scalar(out=rosb[:, (i % 4) * TC:(i % 4) * TC + TC],
                                    in0=rosb[:, (i % 4) * TC:(i % 4) * TC + TC],
                                    scalar1=bo_pm[:, o2:o2 + 1], scalar2=None,
                                    op0=OP.add).then_inc(kb.sems["dve"], 1)
        kb.sec("vector", f_vector)
def ph_down_prep(kb, down_pm, wd_scr, ai32, ab, ps):
    NI = IFF // 128
    DL = DS(kb, ("l0", "l1"))
    PS_ = DS(kb, ("g0", "g1"))
    ac0 = kb.cnt["act"]

    def f_sync(e):
        for ic in range(NI):
            if ic >= 2:
                e.wait_ge(kb.sems["act"], ac0 + ic - 1)
            DL.issue(e, out=ai32[:, (ic % 2) * D:(ic % 2) * D + D],
                     in_=down_pm[ic * 128:(ic + 1) * 128, :])
    kb.sec("sync", f_sync)

    def f_scalar(e):
        for ic in range(NI):
            DL.w(e, ic)
            if ic >= 2:
                PS_.w(e, ic - 2)
            e.activation(out=ab[:, (ic % 2) * D:(ic % 2) * D + D],
                         in_=ai32[:, (ic % 2) * D:(ic % 2) * D + D],
                         func=AF.Copy, bias=-128.0).then_inc(kb.sems["act"], 1)
            kb.inc("act")
    kb.sec("scalar", f_scalar)

    def f_gp(e):
        for ic in range(NI):
            e.wait_ge(kb.sems["act"], ac0 + ic + 1)
            PS_.issue(e, out=wd_scr[ic * 128:(ic + 1) * 128, :],
                      in_=ab[:, (ic % 2) * D:(ic % 2) * D + D])
    kb.sec("gpsimd", f_gp)


def ph_ffn0_a(kb, xhat, gate_pm, up_pm, h_scr, sgv, suv, sb, ps):
    ai32, wgT, wuT, silu_sb, u_sb, h_stage = sb
    NIB = 16
    GL = DS(kb, ("l0", "l1"))
    HS = DS(kb, ("g0", "g1"))
    ac0 = kb.cnt["act"]
    pe0 = kb.cnt["pe"]
    dv0 = kb.cnt["dve"]

    def cast_tick(ib, j):
        return ac0 + ib * 24 + j + 1

    def silu_tick(ib, k):
        return ac0 + ib * 24 + 8 + k + 1

    def g_tick(ib, k):
        return pe0 + ib * 32 + 2 * k + 1

    def u_tick(ib, k):
        return pe0 + ib * 32 + 2 * k + 2

    def dsched_ib():
        s = [("ue", 0), ("ue", 1)]
        for k in range(2, 16):
            s.append(("h", k - 2))
            s.append(("ue", k))
        s += [("h", 14), ("h", 15)]
        return s
    ue_tick = {}
    h_tick = {}
    t = dv0
    for ib in range(NIB):
        for kind, k in dsched_ib():
            t += 1
            (ue_tick if kind == "ue" else h_tick)[(ib, k)] = t
    kb.cnt["dve"] = t
    kb.cnt["act"] = ac0 + NIB * 24
    kb.cnt["pe"] = pe0 + NIB * 32

    def f_sync(e):
        for ib in range(NIB):
            for j in range(8):
                if j >= 2:
                    e.wait_ge(kb.sems["act"], cast_tick(ib, j - 2))
                elif ib >= 1:
                    e.wait_ge(kb.sems["act"], cast_tick(ib - 1, j + 6))
                src = gate_pm if j < 4 else up_pm
                q = j % 4
                GL.issue(e, out=ai32[:, (j % 2) * 2048:(j % 2) * 2048 + 2048],
                         in_=src[ib * 128:(ib + 1) * 128, q * 2048:(q + 1) * 2048])
    kb.sec("sync", f_sync)

    def f_scalar(e):
        for ib in range(NIB):
            for j in range(8):
                GL.w(e, ib * 8 + j)
                if ib >= 1 and j in (0, 4):
                    e.wait_ge(kb.sems["pe"], pe0 + ib * 32)
                dst = wgT if j < 4 else wuT
                q = j % 4
                e.activation(out=dst[:, q * 2048:(q + 1) * 2048],
                             in_=ai32[:, (j % 2) * 2048:(j % 2) * 2048 + 2048],
                             func=AF.Copy, bias=-128.0).then_inc(kb.sems["act"], 1)
            for k in range(16):
                icb, tc = k // 4, k % 4
                e.wait_ge(kb.sems["pe"], g_tick(ib, k))
                gi = ib * 16 + k
                if gi >= 2:
                    pib, pk = (gi - 2) // 16, (gi - 2) % 16
                    e.wait_ge(kb.sems["dve"], h_tick[(pib, pk)])
                ic = ib * 4 + icb
                e.activation(out=silu_sb[:, (k % 2) * TC:(k % 2) * TC + TC],
                             in_=ps[k % 2][:, 0:TC], func=AF.Silu,
                             scale=sgv[:, ic:ic + 1]).then_inc(kb.sems["act"], 1)
    kb.sec("scalar", f_scalar)

    def f_tensor(e):
        for ib in range(NIB):
            for j in range(8):
                e.wait_ge(kb.sems["act"], cast_tick(ib, j))
            for k in range(16):
                icb, tc = k // 4, k % 4
                gi = ib * 16 + k
                if gi >= 2:
                    pib, pk = (gi - 2) // 16, (gi - 2) % 16
                    e.wait_ge(kb.sems["act"], silu_tick(pib, pk))
                    e.wait_ge(kb.sems["dve"], ue_tick[(pib, pk)])
                for fc in range(FC):
                    ins = e.matmul(out=ps[k % 2][:, 0:TC],
                                   lhsT=wgT[:, fc * 512 + icb * 128:fc * 512 + icb * 128 + 128],
                                   rhs=xhat[:, fc * TOK + tc * TC:fc * TOK + tc * TC + TC],
                                   start=(fc == 0), stop=(fc == FC - 1))
                ins.then_inc(kb.sems["pe"], 1)
                for fc in range(FC):
                    ins = e.matmul(out=ps[2 + k % 2][:, 0:TC],
                                   lhsT=wuT[:, fc * 512 + icb * 128:fc * 512 + icb * 128 + 128],
                                   rhs=xhat[:, fc * TOK + tc * TC:fc * TOK + tc * TC + TC],
                                   start=(fc == 0), stop=(fc == FC - 1))
                ins.then_inc(kb.sems["pe"], 1)
    kb.sec("tensor", f_tensor)

    def f_vector(e):
        for ib in range(NIB):
            for kind, k in dsched_ib():
                icb, tc = k // 4, k % 4
                ic = ib * 4 + icb
                if kind == "ue":
                    e.wait_ge(kb.sems["pe"], u_tick(ib, k))
                    if ib > 0 or k >= 2:
                        pib, pk = (ib, k - 2) if k >= 2 else (ib - 1, 14 + k)
                        e.wait_ge(kb.sems["dve"], h_tick[(pib, pk)])
                    e.tensor_scalar(out=u_sb[:, (k % 2) * TC:(k % 2) * TC + TC],
                                    in0=ps[2 + k % 2][:, 0:TC],
                                    scalar1=suv[:, ic:ic + 1], scalar2=None,
                                    op0=OP.mult).then_inc(kb.sems["dve"], 1)
                else:
                    e.wait_ge(kb.sems["act"], silu_tick(ib, k))
                    if tc == 0 and ic >= 2:
                        HS.w(e, ic - 2)
                    e.tensor_tensor(out=h_stage[:, (icb % 2) * TOK + tc * TC:(icb % 2) * TOK + tc * TC + TC],
                                    in0=silu_sb[:, (k % 2) * TC:(k % 2) * TC + TC],
                                    in1=u_sb[:, (k % 2) * TC:(k % 2) * TC + TC],
                                    op=OP.mult).then_inc(kb.sems["dve"], 1)
    kb.sec("vector", f_vector)

    def f_gp(e):
        for ib in range(NIB):
            for icb in range(4):
                ic = ib * 4 + icb
                e.wait_ge(kb.sems["dve"], h_tick[(ib, icb * 4 + 3)])
                HS.issue(e, out=h_scr[ic * 128:(ic + 1) * 128, :],
                         in_=h_stage[:, (icb % 2) * TOK:(icb % 2) * TOK + TOK])
    kb.sec("gpsimd", f_gp)


def ph_ffn0_b(kb, h_scr, down_pm, sdv, resid, sb, ps, consts):
    h_t, wdt, rsb, rosb, hsb = sb
    ones_bf, ones_f32 = consts["ones_bf"], consts["ones_f32"]
    hsr = consts["hsr"]
    NI = IFF // 128
    HT = DS(kb, ("l0", "l1"))
    RS = DS(kb, ("l2", "l3"))
    WDT = DS(kb, ("g0", "g1"))
    WB = DS(kb, ("g2", "g3", "g4", "g5"))
    pe0 = kb.cnt["pe"]
    dv0 = kb.cnt["dve"]

    def hs_tick(tc, ic):
        return pe0 + tc * (NI + 1 + 4 * NI) + ic + 1

    def bc_tick(tc):
        return pe0 + tc * (NI + 1 + 4 * NI) + NI + 1

    def mm_tick(tc, og, ic):
        return pe0 + tc * (NI + 1 + 4 * NI) + NI + 1 + og * NI + ic + 1
    kb.cnt["pe"] = pe0 + NTC * (NI + 1 + 4 * NI)

    seq12 = [("a", 0), ("a", 1), ("m", 0), ("a", 2), ("m", 1), ("a", 3),
             ("m", 2), ("r", 0), ("m", 3), ("r", 1), ("r", 2), ("r", 3)]
    a_tick = {}
    m_tick = {}
    r_tick = {}
    hrow_tick = {}
    t = dv0
    for tc in range(NTC):
        t += 3
        hrow_tick[tc] = t
        for og in range(4):
            for kind, j in seq12:
                t += 1
                {"a": a_tick, "m": m_tick, "r": r_tick}[kind][(tc, og, j)] = t
    kb.cnt["dve"] = t

    def f_sync(e):
        for tc in range(NTC):
            if tc >= 1:
                e.wait_ge(kb.sems["pe"], mm_tick(tc - 1, 3, NI - 1))
            for jj in range(8):
                HT.issue(e, out=h_t[:, jj * 8 * 512:(jj + 1) * 8 * 512].rearrange("p (ic t) -> p ic t", ic=8),
                         in_=h_scr.rearrange("(ic p) t -> p ic t", p=128)[:, jj * 8:(jj + 1) * 8,
                                                                         tc * TC:(tc + 1) * TC])
            for og in range(4):
                for j in range(4):
                    i = (tc * 4 + og) * 4 + j
                    if i >= 2:
                        pi = i - 2
                        ptc, pog, pj = pi // 16, (pi // 4) % 4, pi % 4
                        e.wait_ge(kb.sems["dve"], r_tick[(ptc, pog, pj)])
                    oc = og * 4 + j
                    RS.issue(e, out=rsb[:, (i % 4) * TC:(i % 4) * TC + TC],
                             in_=resid[oc * 128:(oc + 1) * 128, tc * TC:(tc + 1) * TC])
    kb.sec("sync", f_sync)

    def f_gp(e):
        for tc in range(NTC):
            for og in range(4):
                for ic in range(NI):
                    gi = (tc * 4 + og) * NI + ic
                    if gi >= 2:
                        pgi = gi - 2
                        ptc, pog, pic = pgi // (4 * NI), (pgi // NI) % 4, pgi % NI
                        e.wait_ge(kb.sems["pe"], mm_tick(ptc, pog, pic))
                    WDT.issue(e, out=wdt[:, (ic % 2) * 512:(ic % 2) * 512 + 512],
                              in_=down_pm[ic * 128:(ic + 1) * 128, og * 512:(og + 1) * 512])
                for j in range(4):
                    e.wait_ge(kb.sems["dve"], r_tick[(tc, og, j)])
                    oc = og * 4 + j
                    WB.issue(e, out=resid[oc * 128:(oc + 1) * 128, tc * TC:(tc + 1) * TC],
                             in_=rosb[:, (j % 4) * TC:(j % 4) * TC + TC])
    kb.sec("gpsimd", f_gp)

    def f_tensor(e):
        for tc in range(NTC):
            for jj in range(8):
                HT.w(e, tc * 8 + jj)
            if tc >= 1:
                e.wait_ge(kb.sems["dve"], hrow_tick[tc - 1] - 2)
            for ic in range(NI):
                e.matmul(out=ps[5][0:1, 0:TC], lhsT=ones_bf[:, 0:1],
                         rhs=h_t[:, ic * 512:(ic + 1) * 512],
                         start=(ic == 0), stop=(ic == NI - 1)).then_inc(kb.sems["pe"], 1)
            e.wait_ge(kb.sems["dve"], hrow_tick[tc] - 1)
            if tc >= 1:
                e.wait_ge(kb.sems["dve"], hrow_tick[tc - 1])
            e.matmul(out=ps[6][:, 0:TC], lhsT=ones_f32[0:1, :],
                     rhs=hsr[0:1, 0:TC], start=True, stop=True).then_inc(kb.sems["pe"], 1)
            for og in range(4):
                for ic in range(NI):
                    gi = (tc * 4 + og) * NI + ic
                    WDT.w(e, gi)
                    if ic == 0 and (tc > 0 or og > 0):
                        ptc, pog = (tc, og - 1) if og > 0 else (tc - 1, 3)
                        e.wait_ge(kb.sems["dve"], a_tick[(ptc, pog, 3)])
                    for j in range(4):
                        ins = e.matmul(out=ps[j][:, 0:TC],
                                       lhsT=wdt[:, (ic % 2) * 512 + j * 128:(ic % 2) * 512 + j * 128 + 128],
                                       rhs=h_t[:, ic * 512:(ic + 1) * 512],
                                       start=(ic == 0), stop=(ic == NI - 1))
                    ins.then_inc(kb.sems["pe"], 1)
    kb.sec("tensor", f_tensor)

    def f_vector(e):
        for tc in range(NTC):
            e.wait_ge(kb.sems["pe"], hs_tick(tc, NI - 1))
            e.tensor_copy(out=hsr[0:1, 0:TC], in_=ps[5][0:1, 0:TC]).then_inc(kb.sems["dve"], 1)
            e.wait_ge(kb.sems["dve"], hrow_tick[tc] - 2)
            e.tensor_scalar(out=hsr[0:1, 0:TC], in0=hsr[0:1, 0:TC], scalar1=-128.0,
                            scalar2=None, op0=OP.mult).then_inc(kb.sems["dve"], 1)
            e.wait_ge(kb.sems["pe"], bc_tick(tc))
            e.tensor_copy(out=hsb[:, 0:TC], in_=ps[6][:, 0:TC]).then_inc(kb.sems["dve"], 1)
            e.wait_ge(kb.sems["dve"], hrow_tick[tc])
            for og in range(4):
                for kind, j in seq12:
                    oc = og * 4 + j
                    i = (tc * 4 + og) * 4 + j
                    if kind == "a":
                        e.wait_ge(kb.sems["pe"], mm_tick(tc, og, NI - 1))
                        if i >= 4:
                            WB.w(e, i - 4)
                        e.tensor_tensor(out=rosb[:, (j % 4) * TC:(j % 4) * TC + TC],
                                        in0=ps[j][:, 0:TC],
                                        in1=hsb[:, 0:TC],
                                        op=OP.add).then_inc(kb.sems["dve"], 1)
                    elif kind == "m":
                        e.tensor_scalar(out=rosb[:, (j % 4) * TC:(j % 4) * TC + TC],
                                        in0=rosb[:, (j % 4) * TC:(j % 4) * TC + TC],
                                        scalar1=sdv[:, oc:oc + 1], scalar2=None,
                                        op0=OP.mult).then_inc(kb.sems["dve"], 1)
                    else:
                        RS.w(e, i)
                        e.tensor_tensor(out=rosb[:, (j % 4) * TC:(j % 4) * TC + TC],
                                        in0=rosb[:, (j % 4) * TC:(j % 4) * TC + TC],
                                        in1=rsb[:, (i % 4) * TC:(i % 4) * TC + TC],
                                        op=OP.add).then_inc(kb.sems["dve"], 1)
    kb.sec("vector", f_vector)


def ph_ffn1(kb, xhat, wt1_pm, g1_pm, at_pm, bs_pm, rep_sb, resid, sb, ps, consts):
    (wtb, at_sb, bs_sb, g1_sb, cwT, cwt_sb, tc_sb, rsb, rosb, tk) = sb
    ident = consts["ident"]

    SL = DS(kb, ("g0", "g1"))

    def f_ld(e):
        SL.issue(e, out=g1_sb[:], in_=g1_pm)
        SL.issue(e, out=at_sb[:], in_=at_pm)
        SL.issue(e, out=bs_sb[:], in_=bs_pm)
    kb.sec("gpsimd", f_ld)
    kb.barrier()

    lg, t1, t2, t5, t7, t8, a2, t3, t4, t6, t9, a1, cwall = tk
    pe0 = kb.cnt["pe"]
    dv0 = kb.cnt["dve"]
    ac0 = kb.cnt["act"]
    ev_tick = {}
    dv = dv0
    ac = ac0
    for tt in range(NTT):
        ev_tick[tt] = dv + 1
        dv += 12
        ac += 2
    kb.cnt["dve"] = dv
    kb.cnt["act"] = ac

    def f_tensor_l(e):
        for tt in range(NTT):
            if tt >= 1:
                e.wait_ge(kb.sems["dve"], ev_tick[tt - 1])
            for fc in range(FC):
                ins = e.matmul(out=ps[0][:, 0:8],
                               lhsT=xhat[:, fc * TOK + tt * 128:fc * TOK + tt * 128 + 128],
                               rhs=g1_sb[:, fc * 8:fc * 8 + 8],
                               start=(fc == 0), stop=(fc == FC - 1))
            ins.then_inc(kb.sems["pe"], 1)
            kb.inc("pe")
    kb.sec("tensor", f_tensor_l)

    def f_scalar_l(e):
        for tt in range(NTT):
            e.wait_ge(kb.sems["dve"], dv0 + tt * 12 + 8)
            e.activation(out=a1[:], in_=lg[:], func=AF.Exp,
                         bias=t2[:, 0:1]).then_inc(kb.sems["act"], 1)
            e.activation(out=a2[:], in_=t5[:, 0:1], func=AF.Exp,
                         bias=t2[:, 0:1]).then_inc(kb.sems["act"], 1)
    kb.sec("scalar", f_scalar_l)

    def f_vector_l(e):
        for tt in range(NTT):
            base = dv0 + tt * 12
            e.wait_ge(kb.sems["pe"], pe0 + tt + 1)

            def swx(inst, k, base=base, e=e):
                inst.then_inc(kb.sems["dve"], 1)
                e.wait_ge(kb.sems["dve"], base + k)
            swx(e.tensor_copy(out=lg[:], in_=ps[0][:, 0:8]), 1)
            swx(e.tensor_reduce(out=t1[:, 0:1], in_=lg[:], axis=AX, op=OP.max), 2)
            swx(e.tensor_scalar(out=t2[:, 0:1], in0=t1[:, 0:1], scalar1=-1.0,
                                scalar2=None, op0=OP.mult), 3)
            swx(e.tensor_scalar(out=t3[:], in0=lg[:], scalar1=t1[:, 0:1],
                                scalar2=1.0e9, op0=OP.is_equal, op1=OP.mult), 4)
            swx(e.tensor_tensor(out=t4[:], in0=lg[:], in1=t3[:], op=OP.subtract), 5)
            swx(e.tensor_reduce(out=t5[:, 0:1], in_=t4[:], axis=AX, op=OP.max), 6)
            swx(e.tensor_scalar(out=t6[:], in0=lg[:], scalar1=t5[:, 0:1],
                                scalar2=None, op0=OP.is_ge), 7)
            e.nop().then_inc(kb.sems["dve"], 1)
            e.wait_ge(kb.sems["act"], ac0 + 2 * (tt + 1))
            swx(e.tensor_scalar(out=t7[:, 0:1], in0=a2[:, 0:1], scalar1=1.0,
                                scalar2=None, op0=OP.add), 9)
            swx(e.reciprocal(out=t8[:, 0:1], in_=t7[:, 0:1]), 10)
            swx(e.tensor_tensor(out=t9[:], in0=a1[:], in1=t6[:], op=OP.mult), 11)
            swx(e.tensor_scalar(out=cwall[:, tt * 8:(tt + 1) * 8], in0=t9[:],
                                scalar1=t8[:, 0:1], scalar2=None, op0=OP.mult), 12)
    kb.sec("vector", f_vector_l)
    kb.barrier()

    pe1 = kb.cnt["pe"]
    dv1 = kb.cnt["dve"]

    def f_tensor_c(e):
        for tt in range(NTT):
            if tt >= 2:
                e.wait_ge(kb.sems["dve"], dv1 + tt - 1)
            e.transpose(out=ps[4 + tt % 2][0:8, 0:128], in_=cwall[:, tt * 8:(tt + 1) * 8],
                        identity=ident[:]).then_inc(kb.sems["pe"], 1)
            kb.inc("pe")
    kb.sec("tensor", f_tensor_c)

    def f_vector_c(e):
        for tt in range(NTT):
            e.wait_ge(kb.sems["pe"], pe1 + tt + 1)
            e.tensor_copy(out=cwT[0:8, tt * 128:(tt + 1) * 128],
                          in_=ps[4 + tt % 2][0:8, 0:128]).then_inc(kb.sems["dve"], 1)
            kb.inc("dve")
    kb.sec("vector", f_vector_c)
    kb.barrier()

    pe2 = kb.cnt["pe"]
    dv2 = kb.cnt["dve"]

    def f_tensor_t(e):
        for tc in range(NTC):
            if tc >= 1:
                e.wait_ge(kb.sems["dve"], dv2 + 2 * tc)
            e.matmul(out=ps[4][:, 0:TC], lhsT=rep_sb[0:8, :],
                     rhs=cwT[0:8, tc * TC:(tc + 1) * TC], start=True, stop=True
                     ).then_inc(kb.sems["pe"], 1)
            kb.inc("pe")
            for fc in range(FC):
                ins = e.matmul(out=ps[tc % 2][:, 0:TC],
                               lhsT=at_sb[:, fc * 128:(fc + 1) * 128],
                               rhs=xhat[:, fc * TOK + tc * TC:fc * TOK + tc * TC + TC],
                               start=(fc == 0), stop=(fc == FC - 1))
            ins.then_inc(kb.sems["pe"], 1)
            kb.inc("pe")
    kb.sec("tensor", f_tensor_t)

    def f_vector_t(e):
        for tc in range(NTC):
            e.wait_ge(kb.sems["pe"], pe2 + 2 * (tc + 1))
            e.tensor_copy(out=cwt_sb[:, tc * TC:(tc + 1) * TC],
                          in_=ps[4][:, 0:TC]).then_inc(kb.sems["dve"], 1)
            e.wait_ge(kb.sems["dve"], kb.inc("dve"))
            e.tensor_tensor(out=tc_sb[:, tc * TC:(tc + 1) * TC],
                            in0=ps[tc % 2][:, 0:TC],
                            in1=cwt_sb[:, tc * TC:(tc + 1) * TC],
                            op=OP.mult).then_inc(kb.sems["dve"], 1)
            kb.inc("dve")
    kb.sec("vector", f_vector_t)
    kb.barrier()

    pe3 = kb.cnt["pe"]
    dv3 = kb.cnt["dve"]
    WT = DS(kb, ("g0", "g1"))
    WB = DS(kb, ("g2", "g3", "g4", "g5"))
    RS = DS(kb, ("l2", "l3"))

    def mm_tick(i):
        return pe3 + i + 1
    kb.cnt["pe"] = pe3 + 64

    def ad_tick(i):
        return dv3 + i + 1
    kb.cnt["dve"] = dv3 + 64

    def f_gp(e):
        ops = []
        for og in range(4):
            ops.append(("ld", og))
            if og >= 1:
                for i in range((og - 1) * 16, og * 16):
                    ops.append(("st", i))
        for i in range(48, 64):
            ops.append(("st", i))
        for kind, v in ops:
            if kind == "ld":
                if v >= 2:
                    e.wait_ge(kb.sems["pe"], mm_tick((v - 2) * 16 + 15))
                WT.issue(e, out=wtb[:, (v % 2) * 8192:(v % 2) * 8192 + 8192],
                         in_=wt1_pm[v * 128:(v + 1) * 128, :])
            else:
                e.wait_ge(kb.sems["dve"], ad_tick(v))
                og, rest = v // 16, v % 16
                oc = og * 4 + rest // 4
                tc = rest % 4
                WB.issue(e, out=resid[oc * 128:(oc + 1) * 128, tc * TC:(tc + 1) * TC],
                         in_=rosb[:, (v % 4) * TC:(v % 4) * TC + TC])
    kb.sec("gpsimd", f_gp)

    def f_sync(e):
        for i in range(64):
            if i >= 2:
                e.wait_ge(kb.sems["dve"], ad_tick(i - 2))
            og, rest = i // 16, i % 16
            oc = og * 4 + rest // 4
            tc = rest % 4
            RS.issue(e, out=rsb[:, (i % 4) * TC:(i % 4) * TC + TC],
                     in_=resid[oc * 128:(oc + 1) * 128, tc * TC:(tc + 1) * TC])
    kb.sec("sync", f_sync)

    def f_tensor_d(e):
        for og in range(4):
            WT.w(e, og)
            for ocr in range(4):
                for tc in range(NTC):
                    i = (og * 4 + ocr) * 4 + tc
                    oc = og * 4 + ocr
                    if i >= 2:
                        e.wait_ge(kb.sems["dve"], ad_tick(i - 2))
                    for fc in range(FC):
                        e.matmul(out=ps[i % 2][:, 0:TC],
                                 lhsT=wtb[:, (og % 2) * 8192 + fc * 512 + ocr * 128:(og % 2) * 8192 + fc * 512 + ocr * 128 + 128],
                                 rhs=xhat[:, fc * TOK + tc * TC:fc * TOK + tc * TC + TC],
                                 start=(fc == 0), stop=False)
                    ins = e.matmul(out=ps[i % 2][:, 0:TC],
                                   lhsT=bs_sb[:, oc * 128:(oc + 1) * 128],
                                   rhs=tc_sb[:, tc * TC:(tc + 1) * TC],
                                   start=False, stop=True)
                    ins.then_inc(kb.sems["pe"], 1)
    kb.sec("tensor", f_tensor_d)

    def f_vector_d(e):
        for i in range(64):
            e.wait_ge(kb.sems["pe"], mm_tick(i))
            RS.w(e, i)
            if i >= 4:
                WB.w(e, i - 4)
            e.tensor_tensor(out=rosb[:, (i % 4) * TC:(i % 4) * TC + TC],
                            in0=ps[i % 2][:, 0:TC],
                            in1=rsb[:, (i % 4) * TC:(i % 4) * TC + TC],
                            op=OP.add).then_inc(kb.sems["dve"], 1)
    kb.sec("vector", f_vector_d)
from contextlib import ExitStack


def build_program(mode, dumps=(), cc_mode="cc"):
    nc = bass.Bass("TRN2", target_bir_lowering=False)
    io = {}

    def din(name, shape, dt):
        io[name] = nc.dram_tensor(name, shape, dt, kind="ExternalInput").ap()
        return io[name]

    def dout(name, shape, dt):
        io[name] = nc.dram_tensor(name, shape, dt, kind="ExternalOutput").ap()
        return io[name]

    def dint(name, shape, dt):
        io[name] = nc.dram_tensor(name, shape, dt, kind="Internal").ap()
        return io[name]

    din("c_ident", [128, 128], F32)
    din("c_ones_bf", [128, 1], BF16)
    din("c_ones_f32", [1, 128], F32)
    din("c_isq", [128, 1], F32)
    din("c_midbias", [128, 1], F32)
    din("c_kposB", [128, NKT], F32)
    din("c_qposB", [1, TOK], F32)
    din("c_rep", [8, 128], BF16)
    din("c_qw0", [128, 1], F32)
    din("c_qw1", [128, 1], F32)

    if mode == "launch1":
        din("x", [TOK, D], F32)
        dint("resid", [D, TOK], F32)
        din("wq_pm", [D, D], F32)
        din("wk_pm", [D, D], F32)
        din("wv_pm", [4 * 128, FC * 512], F32)
        din("bq_pm", [128, FC], F32)
        din("bk_pm", [128, FC], F32)
        din("bv_row", [1, D], F32)
        dout("qbuf", [D, TOK], BF16)
        dout("cin", [2 * D, TOK], BF16)
    elif mode == "launch2":
        din("x", [TOK, D], F32)
        din("qx", [128, H * TOK], BF16)
        din("kvfull", [2 * 2 * D, TOK], BF16)
        din("wo", [D, D], F32)
        din("bo_pm", [128, FC], F32)
        din("gate_pm", [D, IFF], I32)
        din("up_pm", [D, IFF], I32)
        din("down_pm", [IFF, D], I32)
        din("sg_pm", [128, 64], F32)
        din("su_pm", [128, 64], F32)
        din("sd_pm", [128, 16], F32)
        din("wq1_pm", [D, D], F32)
        din("wk1_pm", [128, D], F32)
        din("wv1_pm", [128, D], F32)
        din("bq1_pm", [128, FC], F32)
        din("bk1_pm", [128, 1], F32)
        din("bv1_row", [1, DH], F32)
        dout("resid", [D, TOK], F32)
        dout("q1buf", [D, TOK], BF16)
        dout("cin1", [2 * DH, TOK], BF16)
        dint("abuf", [D, TOK], BF16)
        dint("h_scr", [IFF, TOK], BF16)
    elif mode == "launch3":
        din("resid2", [D, TOK], F32)
        din("q1buf", [D, TOK], BF16)
        din("kvfull1", [2 * 2 * DH, TOK], BF16)
        din("wo1", [D, D], F32)
        din("bo1_pm", [128, FC], F32)
        din("wt1_pm", [512, 8192], F32)
        din("g1_pm", [128, 128], F32)
        din("at_pm", [128, 2048], F32)
        din("bs_pm", [128, 2048], F32)
        dout("y", [TOK, D], F32)
        dint("resid", [D, TOK], F32)
        dint("abuf", [D, TOK], BF16)
    elif mode == "fused":
        din("x", [TOK, D], F32)
        din("wq_pm", [D, D], F32)
        din("wk_pm", [D, D], F32)
        din("wv_pm", [4 * 128, FC * 512], F32)
        din("bq_pm", [128, FC], F32)
        din("bk_pm", [128, FC], F32)
        din("bv_row", [1, D], F32)
        din("wo", [D, D], F32)
        din("bo_pm", [128, FC], F32)
        din("gate_pm", [D, IFF], I32)
        din("up_pm", [D, IFF], I32)
        din("down_pm", [IFF, D], I32)
        din("sg_pm", [128, 64], F32)
        din("su_pm", [128, 64], F32)
        din("sd_pm", [128, 16], F32)
        din("wq1_pm", [D, D], F32)
        din("wk1_pm", [128, D], F32)
        din("wv1_pm", [128, D], F32)
        din("bq1_pm", [128, FC], F32)
        din("bk1_pm", [128, 1], F32)
        din("bv1_row", [1, DH], F32)
        din("wo1", [D, D], F32)
        din("bo1_pm", [128, FC], F32)
        din("wt1_pm", [512, 8192], F32)
        din("g1_pm", [128, 128], F32)
        din("at_pm", [128, 2048], F32)
        din("bs_pm", [128, 2048], F32)
        dout("y", [TOK, D], F32)
        dint("resid", [D, TOK], F32)
        dint("qbuf", [D, TOK], BF16)
        dint("abuf", [D, TOK], BF16)
        dint("h_scr", [IFF, TOK], BF16)
        io["cin"] = nc.dram_tensor("cin", [3 * D, TOK], BF16).ap()
        io["cout"] = nc.dram_tensor("cout", [2 * 3 * D, TOK], BF16).ap()
        io["cin1"] = nc.dram_tensor("cin1", [2 * DH, TOK], BF16).ap()
        io["cout1"] = nc.dram_tensor("cout1", [2 * 2 * DH, TOK], BF16).ap()
    else:
        raise ValueError(mode)

    with ExitStack() as stack:
        ec = stack.enter_context
        xhat = ec(nc.sbuf_tensor("xhat", [128, FC * TOK], BF16))
        AB1 = ec(nc.sbuf_tensor("AB1", [128, 32768], BF16))
        AB2 = ec(nc.sbuf_tensor("AB2", [128, 8192], BF16))
        AF1 = ec(nc.sbuf_tensor("AF1", [128, 4096], F32))
        AF2 = ec(nc.sbuf_tensor("AF2", [128, 4096], F32))
        PT = ec(nc.sbuf_tensor("PT", [128, 2560], BF16))
        QS2 = ec(nc.sbuf_tensor("QS2", [128, 8192], BF16))
        AI32 = QS2[:].bitcast(I32)
        ident = ec(nc.sbuf_tensor("ident", [128, 128], F32))
        ones_bf = ec(nc.sbuf_tensor("ones_bf", [128, 1], BF16))
        ones_f32 = ec(nc.sbuf_tensor("ones_f32", [1, 128], F32))
        isq_vec = ec(nc.sbuf_tensor("isq_vec", [128, 1], F32))
        midbias = ec(nc.sbuf_tensor("midbias", [128, 1], F32))
        kposB = ec(nc.sbuf_tensor("kposB", [128, NKT], F32))
        rep_sb = ec(nc.sbuf_tensor("rep_sb", [8, 128], BF16))
        qw0 = ec(nc.sbuf_tensor("qw0", [128, 1], F32))
        qw1 = ec(nc.sbuf_tensor("qw1", [128, 1], F32))
        bvec = ec(nc.sbuf_tensor("bvec", [128, 256], F32))
        brow2 = AF2[0:1, 0:2048]
        ps = [ec(nc.psum_tensor(f"ps{i}", [128, 512], F32)) for i in range(7)]
        sems = {k: ec(nc.semaphore(f"s_{k}")) for k in ALLSEMS}
        blk = ec(nc.Block())

        kb = KB(nc, blk, sems)
        consts = {"ones_bf": ones_bf, "ones_f32": ones_f32, "isq_vec": isq_vec,
                  "midbias": midbias, "kposB": kposB, "qposB": AF2[0:1, 2048:4096],
                  "qposB_dram": io["c_qposB"], "ident": ident,
                  "qw0": qw0, "qw1": qw1}

        CL = DS(kb, ("l0", "l1"))

        def f_consts(e):
            pairs = [(ident[:], io["c_ident"]), (ones_bf[:], io["c_ones_bf"]),
                     (ones_f32[:], io["c_ones_f32"]), (isq_vec[:], io["c_isq"]),
                     (midbias[:], io["c_midbias"]), (kposB[:], io["c_kposB"]),
                     (rep_sb[:], io["c_rep"]), (qw0[:], io["c_qw0"]),
                     (qw1[:], io["c_qw1"])]
            bv_map = [("bq_pm", 0, FC), ("bk_pm", 16, FC), ("bo_pm", 32, FC),
                      ("sg_pm", 48, 64), ("su_pm", 112, 64), ("sd_pm", 176, 16),
                      ("bq1_pm", 192, FC), ("bk1_pm", 208, 1), ("bo1_pm", 209, FC)]
            for nm, off, wdt_ in bv_map:
                if nm in io:
                    pairs.append((bvec[:, off:off + wdt_], io[nm]))
            for dst, srcap in pairs:
                CL.issue(e, out=dst, in_=srcap, allow_slow_non_contiguous=True)
        kb.sec("sync", f_consts)
        kb.barrier()

        sb_tp = (AF1[:, 0:4096], AF2[:, 0:4096])
        sb_ln = (AF1[:, 0:1024], AB2[:, 0:1024], AB2[:, 1024:2048],
                 AF2[0:1, 0:2048], AF2[0:1, 2048:4096], AF1[0:1, 2048:4096],
                 AF1[:, 1024:2048])
        sb_proj = (AB1[:, 24576:28672], AB1[:, 28672:32768])
        sb_at = (AB1[:, 0:8192], AB1[:, 8192:16384], AB2[:, 0:4096], PT[:, 0:1024],
                 AB1[:, 16384:32768], AF1[:, 0:1024], AB2[:, 4096:8192],
                 AF1[:, 1536:2048], AF1[:, 1024:1536], AF2[0:1, 0:TOK],
                 PT[:, 1024:2048], PT[:, 2048:2560])
        sb_atf = (AB1[:, 0:8192], AB1[:, 8192:16384], QS2[:, 0:8192], PT[:, 0:1024],
                  AB1[:, 16384:32768], AF1[:, 0:1024], AB2[:, 4096:8192],
                  AF1[:, 1536:2048], AF1[:, 1024:1536], AF2[0:1, 0:TOK],
                  PT[:, 1024:2048], PT[:, 2048:2560])
        sb_wo = (AB1[:, 0:16384], AB1[:, 16384:32768], AF1[:, 0:2048], AF1[:, 2048:4096])

        def load_bias(pairs):
            BL = DS(kb, ("l0", "l1"))

            def f_b(e):
                for dst, srcap in pairs:
                    BL.issue(e, out=dst, in_=srcap)
            kb.sec("sync", f_b)
            kb.barrier()

        if mode == "launch1":
            ph_transpose_td(kb, io["x"], io["resid"], sb_tp, ps, NTT, ident)
            kb.barrier()
            ph_ln_v2(kb, io["resid"], xhat, sb_ln, ps, consts)
            kb.barrier()
            load_bias([(bvec[:, 0:FC], io["bq_pm"]), (bvec[:, FC:2 * FC], io["bk_pm"]),
                       (brow2[0:1, 0:D], io["bv_row"])])
            ph_proj_fm(kb, xhat, io["wq_pm"], bvec[:, 0:FC], FC, io["qbuf"], 0, sb_proj, ps)
            kb.barrier()
            ph_proj_fm(kb, xhat, io["wk_pm"], bvec[:, FC:2 * FC], FC, io["cin"], 0, sb_proj, ps)
            kb.barrier()
            sb_v = (AB1[:, 0:16384], AB1[:, 16384:24576], AF1[:, 0:512])
            ph_proj_v(kb, xhat, io["wv_pm"], brow2[0:1, 0:D], 4, 512, io["cin"], D, sb_v, ps, consts)
            kb.barrier()
        elif mode == "launch2":
            ph_transpose_td(kb, io["x"], io["resid"], sb_tp, ps, NTT, ident)
            kb.barrier()
            ph_attn(kb, 0, io["qx"], io["kvfull"], io["abuf"], sb_at, ps, consts)
            kb.barrier()
            load_bias([(bvec[:, 0:FC], io["bo_pm"])])
            ph_wo(kb, io["abuf"], io["wo"], bvec[:, 0:FC], io["resid"], sb_wo, ps)
            kb.barrier()
            if "resid_mid" in dumps:
                io["resid_mid"] = nc.dram_tensor("resid_mid_d", [D, TOK], F32,
                                                 kind="ExternalOutput").ap()

                DM = DS(kb, ("g0",))

                def f_dm(e):
                    DM.issue(e, out=io["resid_mid"], in_=io["resid"])
                kb.sec("gpsimd", f_dm)
                kb.barrier()
            ph_ln_v2(kb, io["resid"], xhat, sb_ln, ps, consts)
            kb.barrier()
            load_bias([(bvec[:, 0:64], io["sg_pm"]), (bvec[:, 64:128], io["su_pm"]),
                       (bvec[:, 128:144], io["sd_pm"])])
            sb_fa = (AI32[:, 0:4096], AB1[:, 0:8192], AB1[:, 8192:16384],
                     AB2[:, 0:1024], AB2[:, 1024:2048], AB2[:, 2048:6144])
            ph_ffn0_a(kb, xhat, io["gate_pm"], io["up_pm"], io["h_scr"],
                      bvec[:, 0:64], bvec[:, 64:128], sb_fa, ps)
            kb.barrier()
            sb_fb = (AB1[:, 0:32768], AB2[:, 0:1024], AF1[:, 0:2048], AF1[:, 2048:4096],
                     AF2[:, 1024:1536])
            consts["hsr"] = AF2[0:1, 0:512]
            ph_ffn0_b(kb, io["h_scr"], io["down_pm"], bvec[:, 128:144], io["resid"], sb_fb, ps, consts)
            kb.barrier()
            ph_ln_v2(kb, io["resid"], xhat, sb_ln, ps, consts)
            kb.barrier()
            load_bias([(bvec[:, 0:FC], io["bq1_pm"]), (bvec[:, FC:FC + 1], io["bk1_pm"]),
                       (brow2[0:1, 0:DH], io["bv1_row"])])
            ph_proj_fm(kb, xhat, io["wq1_pm"], bvec[:, 0:FC], FC, io["q1buf"], 0, sb_proj, ps)
            kb.barrier()
            ph_proj_fm(kb, xhat, io["wk1_pm"], bvec[:, FC:FC + 1], 1, io["cin1"], 0, sb_proj, ps)
            kb.barrier()
            sb_v = (AB1[:, 0:16384], AB1[:, 16384:24576], AF1[:, 0:512])
            ph_proj_v(kb, xhat, io["wv1_pm"], brow2[0:1, 0:DH], 1, 128, io["cin1"], DH, sb_v, ps, consts)
            kb.barrier()
        elif mode == "launch3":
            CP = DS(kb, ("g0", "g1"))

            def f_cp(e):
                for fc in range(FC):
                    CP.issue(e, out=io["resid"][fc * 128:(fc + 1) * 128, :],
                             in_=io["resid2"][fc * 128:(fc + 1) * 128, :])
            kb.sec("gpsimd", f_cp)
            kb.barrier()
            ph_attn(kb, 1, io["q1buf"], io["kvfull1"], io["abuf"], sb_at, ps, consts)
            kb.barrier()
            load_bias([(bvec[:, 0:FC], io["bo1_pm"])])
            ph_wo(kb, io["abuf"], io["wo1"], bvec[:, 0:FC], io["resid"], sb_wo, ps)
            kb.barrier()
            ph_ln_v2(kb, io["resid"], xhat, sb_ln, ps, consts)
            kb.barrier()
            sb_f1 = (AB1[:, 0:16384], AB1[:, 16384:18432], AB1[:, 18432:20480],
                     AB1[:, 24576:24704], AB2[0:8, 0:2048], AB1[:, 20480:22528],
                     AB1[:, 22528:24576], AF1[:, 0:2048], AF1[:, 2048:4096],
                     (AF2[:, 0:8], AF2[:, 16:17], AF2[:, 17:18], AF2[:, 18:19],
                      AF2[:, 19:20], AF2[:, 20:21], AF2[:, 21:22], AF2[:, 24:32],
                      AF2[:, 32:40], AF2[:, 40:48], AF2[:, 48:56], AF2[:, 56:64],
                      AF2[:, 64:192]))
            ph_ffn1(kb, xhat, io["wt1_pm"], io["g1_pm"], io["at_pm"], io["bs_pm"],
                    rep_sb, io["resid"], sb_f1, ps, consts)
            kb.barrier()
            ph_transpose_td(kb, io["resid"], io["y"], sb_tp, ps, FC, ident)
            kb.barrier()

        if mode == "fused":
            RG = [[0, 1], [2, 3], [4, 5], [6, 7]]

            def ph_ag(cin_ap, cout_ap, which):
                def f_ag(e):
                    if cc_mode == "cc" or cc_mode == which or cc_mode.startswith("prefix"):
                        e.collective_compute("AllGather", OP.bypass, replica_groups=RG,
                                             ins=[cin_ap.opt()], outs=[cout_ap.opt()]
                                             ).then_inc(kb.sems["cc"], 1)
                        kb.inc("cc", 1)
                    else:
                        nrow = cin_ap.shape[0]
                        for rr in range(2):
                            e.dma_start(out=cout_ap[rr * nrow:(rr + 1) * nrow, :],
                                        in_=cin_ap[:, :]).then_inc(kb.sems["cc"], 16)
                            kb.inc("cc", 16)
                kb.sec("gpsimd", f_ag)
                kb.barrier()

            if cc_mode != "prefix0":
                ph_transpose_td(kb, io["x"], io["resid"], sb_tp, ps, NTT, ident)
                kb.barrier()
                ph_ln_v2(kb, io["resid"], xhat, sb_ln, ps, consts)
                kb.barrier()
                load_bias([(brow2[0:1, 0:D], io["bv_row"])])
                ph_proj_fm(kb, xhat, io["wq_pm"], bvec[:, 0:FC], FC, io["cin"], 2 * D, sb_proj, ps)
                kb.barrier()
                ph_proj_fm(kb, xhat, io["wk_pm"], bvec[:, 16:32], FC, io["cin"], 0, sb_proj, ps)
                kb.barrier()
                sb_v = (AB1[:, 0:16384], AB1[:, 16384:24576], AF1[:, 0:512])
                ph_proj_v(kb, xhat, io["wv_pm"], brow2[0:1, 0:D], 4, 512, io["cin"], D, sb_v, ps, consts)
                kb.barrier()
            ph_ag(io["cin"], io["cout"], "ag0")
            if cc_mode.startswith("prefix"):
                ph_transpose_td(kb, io["resid"], io["y"], sb_tp, ps, FC, ident)
                kb.barrier()
                fused_rest = False
            else:
                fused_rest = True
            if fused_rest and True:
                ph_attn(kb, 0, io["cout"], io["cout"], io["abuf"], sb_atf, ps, consts, qmode="fused")
                kb.barrier()
            if fused_rest:
                ph_wo(kb, io["abuf"], io["wo"], bvec[:, 32:48], io["resid"], sb_wo, ps)
                kb.barrier()
                ph_ln_v2(kb, io["resid"], xhat, sb_ln, ps, consts)
                kb.barrier()
                sb_fa = (AI32[:, 0:4096], AB1[:, 0:8192], AB1[:, 8192:16384],
                         AB2[:, 0:1024], AB2[:, 1024:2048], AB2[:, 2048:6144])
                ph_ffn0_a(kb, xhat, io["gate_pm"], io["up_pm"], io["h_scr"],
                          bvec[:, 48:112], bvec[:, 112:176], sb_fa, ps)
                kb.barrier()
                sb_fb = (AB1[:, 0:32768], AB2[:, 0:1024], AF1[:, 0:2048], AF1[:, 2048:4096],
                         AF2[:, 1024:1536])
                consts["hsr"] = AF2[0:1, 0:512]
                ph_ffn0_b(kb, io["h_scr"], io["down_pm"], bvec[:, 176:192], io["resid"], sb_fb, ps, consts)
                kb.barrier()
                ph_ln_v2(kb, io["resid"], xhat, sb_ln, ps, consts)
                kb.barrier()
                load_bias([(brow2[0:1, 0:DH], io["bv1_row"])])
                ph_proj_fm(kb, xhat, io["wq1_pm"], bvec[:, 192:208], FC, io["qbuf"], 0, sb_proj, ps)
                kb.barrier()
                ph_proj_fm(kb, xhat, io["wk1_pm"], bvec[:, 208:209], 1, io["cin1"], 0, sb_proj, ps)
                kb.barrier()
                sb_v = (AB1[:, 0:16384], AB1[:, 16384:24576], AF1[:, 0:512])
                ph_proj_v(kb, xhat, io["wv1_pm"], brow2[0:1, 0:DH], 1, 128, io["cin1"], DH, sb_v, ps, consts)
                kb.barrier()
                ph_ag(io["cin1"], io["cout1"], "ag1")
                ph_attn(kb, 1, io["qbuf"], io["cout1"], io["abuf"], sb_at, ps, consts)
                kb.barrier()
                ph_wo(kb, io["abuf"], io["wo1"], bvec[:, 209:225], io["resid"], sb_wo, ps)
                kb.barrier()
                ph_ln_v2(kb, io["resid"], xhat, sb_ln, ps, consts)
                kb.barrier()
                sb_f1 = (AB1[:, 0:16384], AB1[:, 16384:18432], AB1[:, 18432:20480],
                         AB1[:, 24576:24704], AB2[0:8, 0:2048], AB1[:, 20480:22528],
                         AB1[:, 22528:24576], AF1[:, 0:2048], AF1[:, 2048:4096],
                         (AF2[:, 0:8], AF2[:, 16:17], AF2[:, 17:18], AF2[:, 18:19],
                          AF2[:, 19:20], AF2[:, 20:21], AF2[:, 21:22], AF2[:, 24:32],
                          AF2[:, 32:40], AF2[:, 40:48], AF2[:, 48:56], AF2[:, 56:64],
                          AF2[:, 64:192]))
                ph_ffn1(kb, xhat, io["wt1_pm"], io["g1_pm"], io["at_pm"], io["bs_pm"],
                        rep_sb, io["resid"], sb_f1, ps, consts)
                kb.barrier()
                ph_transpose_td(kb, io["resid"], io["y"], sb_tp, ps, FC, ident)
                kb.barrier()

        for name in dumps:
            if name == 'resid_mid':
                continue
            src = io[name]
            dst = dout(name + "_d", list(src.tensor.shape), src.tensor.dtype)

            DD = DS(kb, ("g0",))

            def f_dump(e, src=src, dst=dst, DD=DD):
                DD.issue(e, out=dst, in_=src)
            kb.sec("gpsimd", f_dump)
            kb.barrier()

        def f_final(e):
            for nm in DMASEMS:
                if kb.cnt[nm] > 0:
                    e.wait_ge(kb.sems[nm], kb.cnt[nm])
        kb.sec("gpsimd", f_final)

    return nc


def pm4(w, n_oc, ocw):
    return np.ascontiguousarray(
        w.reshape(FC, 128, n_oc, ocw).transpose(2, 1, 0, 3).reshape(n_oc * 128, FC * ocw))


def make_consts(core):
    hlf = core % 2
    qpos = (np.arange(TOK, dtype=np.float64) + hlf * TOK) * 1e6
    kpos = (np.arange(SEQ, dtype=np.float64)) * 1e6
    rep = np.zeros((8, 128), np.float32)
    for ee in range(8):
        rep[ee, ee * 16:(ee + 1) * 16] = 16.0
    return {
        "c_ident": np.eye(128, dtype=np.float32),
        "c_ones_bf": np.ones((128, 1), dtype=BF16NP),
        "c_ones_f32": np.ones((1, 128), dtype=np.float32),
        "c_isq": np.full((128, 1), ISQ, np.float32),
        "c_midbias": np.full((128, 1), 0.0 if hlf == 1 else NEG, np.float32),
        "c_kposB": kpos.reshape(NKT, 128).T.astype(np.float32).copy(),
        "c_qposB": qpos.reshape(1, TOK).astype(np.float32),
        "c_rep": rep.astype(BF16NP),
        "c_qw0": np.full((128, 1), 1.0 if hlf == 0 else 0.0, np.float32),
        "c_qw1": np.full((128, 1), 1.0 if hlf == 1 else 0.0, np.float32),
    }


def host_weights(ins):
    w = {}
    w["wq_pm"] = pm4(ins["L0_attn_Wq"], 16, 128)
    w["wk_pm"] = pm4(ins["L0_attn_Wk"], 16, 128)
    w["wv_pm"] = pm4(ins["L0_attn_Wv"], 4, 512)
    w["bq_pm"] = np.ascontiguousarray(ins["L0_attn_bq"].reshape(16, 128).T)
    w["bk_pm"] = np.ascontiguousarray(ins["L0_attn_bk"].reshape(16, 128).T)
    w["bv_row"] = ins["L0_attn_bv"].reshape(1, D)
    w["wo"] = ins["L0_attn_Wo"]
    w["bo_pm"] = np.ascontiguousarray(ins["L0_attn_bo"].reshape(16, 128).T)
    w["gate_pm"] = np.ascontiguousarray(
        ins["L0_ffn_gate_w"].reshape(16, 512, 16, 128).transpose(0, 3, 2, 1).reshape(2048, 8192))
    w["up_pm"] = np.ascontiguousarray(
        ins["L0_ffn_up_w"].reshape(16, 512, 16, 128).transpose(0, 3, 2, 1).reshape(2048, 8192))
    w["down_pm"] = np.ascontiguousarray(
        ins["L0_ffn_down_w"].reshape(16, 128, 64, 128).transpose(2, 3, 0, 1).reshape(8192, 2048))
    w["sg_pm"] = np.ascontiguousarray(ins["L0_ffn_gate_scale"].reshape(64, 128).T)
    w["su_pm"] = np.ascontiguousarray(ins["L0_ffn_up_scale"].reshape(64, 128).T)
    w["sd_pm"] = np.ascontiguousarray(ins["L0_ffn_down_scale"].reshape(16, 128).T)
    w["wq1_pm"] = pm4(ins["L1_attn_Wq"], 16, 128)
    w["wk1_pm"] = pm4(ins["L1_attn_Wk"], 1, 128)
    w["wv1_pm"] = pm4(ins["L1_attn_Wv"], 1, 128)
    w["bq1_pm"] = np.ascontiguousarray(ins["L1_attn_bq"].reshape(16, 128).T)
    w["bk1_pm"] = np.ascontiguousarray(ins["L1_attn_bk"].reshape(1, 128).T)
    w["bv1_row"] = ins["L1_attn_bv"].reshape(1, DH)
    w["wo1"] = ins["L1_attn_Wo"]
    w["bo1_pm"] = np.ascontiguousarray(ins["L1_attn_bo"].reshape(16, 128).T)
    w["wt1_pm"] = np.ascontiguousarray(
        ins["L1_ffn_weight"].reshape(4, 512, 16, 128).transpose(0, 3, 2, 1).reshape(512, 8192))
    w["g1_pm"] = np.ascontiguousarray(
        ins["L1_ffn_gate_w"].reshape(8, 16, 128).transpose(2, 1, 0).reshape(128, 128))
    w["at_pm"] = np.ascontiguousarray(
        ins["L1_ffn_A_w"].reshape(128, 16, 128).transpose(2, 1, 0).reshape(128, 2048))
    w["bs_pm"] = np.ascontiguousarray(
        ins["L1_ffn_B_w"].transpose(0, 2, 1).reshape(128, 2048))
    return {k: (v.astype(np.float32) if v.dtype == np.float64 else v) for k, v in w.items()}


_BUILD_CACHE = {}


def _get_prog(mode):
    if mode not in _BUILD_CACHE:
        _BUILD_CACHE[mode] = build_program(mode)
    return _BUILD_CACHE[mode]


def kernel(**inputs):
    x = np.ascontiguousarray(inputs["L0_attn_x"]).astype(np.float32, copy=False)
    w = host_weights(inputs)
    nc = _get_prog("fused")
    wkeys = ("wq_pm", "wk_pm", "wv_pm", "bq_pm", "bk_pm", "bv_row",
             "wo", "bo_pm", "gate_pm", "up_pm", "down_pm", "sg_pm", "su_pm", "sd_pm",
             "wq1_pm", "wk1_pm", "wv1_pm", "bq1_pm", "bk1_pm", "bv1_row",
             "wo1", "bo1_pm", "wt1_pm", "g1_pm", "at_pm", "bs_pm")
    maps = []
    for c in range(8):
        b, hlf = c // 2, c % 2
        m = {"x": np.ascontiguousarray(x[b, hlf * TOK:(hlf + 1) * TOK, :])}
        m.update({k: w[k] for k in wkeys})
        m.update(make_consts(c))
        maps.append(m)
    res = run_bass_kernel_spmd(nc, maps, core_ids=list(range(8)))
    y = np.zeros((4, SEQ, D), np.float32)
    for c in range(8):
        b, hlf = c // 2, c % 2
        y[b, hlf * TOK:(hlf + 1) * TOK, :] = res.results[c]["y"]
    return y


# version 06
# speedup vs baseline: 1.1765x; 1.1765x over previous
import numpy as np
import ml_dtypes
import concourse.bass as bass
import concourse.mybir as mybir
from concourse.bass_utils import run_bass_kernel_spmd

F32 = mybir.dt.float32
BF16 = mybir.dt.bfloat16
I32 = mybir.dt.int32
AF = mybir.ActivationFunctionType
OP = mybir.AluOpType
AX = mybir.AxisListType.X
BF16NP = ml_dtypes.bfloat16

D = 2048
FC = 16
TOK = 2048
SEQ = 4096
H = 16
DH = 128
IFF = 8192
TC = 512
NTC = TOK // TC
NTT = TOK // 128
NKT = SEQ // 128
E = 8
R = 16
EPS = 1e-5
ISQ = float(1.0 / np.sqrt(128.0))
NEG = -1.0e9

ENGS = ("sync", "tensor", "vector", "scalar", "gpsimd")
LSEMS = ("l0", "l1", "l2", "l3", "l4", "l5", "l6", "l7")
GSEMS = ("g0", "g1", "g2", "g3", "g4", "g5")
ALLSEMS = LSEMS + GSEMS + ("pe", "dve", "act", "cc")
DMASEMS = LSEMS + GSEMS + ("cc",)


class KB:

    def __init__(self, nc, blk, sems):
        self.nc = nc
        self.blk = blk
        self.sems = sems
        self.cnt = {k: 0 for k in sems}
        self.snap = {k: 0 for k in sems}
        self.waited = {e: {k: 0 for k in sems} for e in ENGS}

    def inc(self, name, k=1):
        self.cnt[name] += k
        return self.cnt[name]

    def barrier(self):
        self.snap = dict(self.cnt)

    def sec(self, eng, f):
        kb = self

        def wrapped(e):
            for k, v in kb.snap.items():
                if v > 0 and kb.waited[eng][k] < v:
                    e.wait_ge(kb.sems[k], v)
                    kb.waited[eng][k] = v
            f(e)

        getattr(self.blk, eng)(wrapped)


class DS:

    def __init__(self, kb, names):
        self.kb = kb
        self.names = tuple(names)
        self.k = 0
        self.base = {nm: kb.cnt[nm] for nm in self.names}

    def token(self, k):
        nm = self.names[k % len(self.names)]
        return (nm, self.base[nm] + 16 * (k // len(self.names) + 1))

    def issue(self, e, out=None, in_=None, **kw):
        nm, v = self.token(self.k)
        self.k += 1
        e.dma_start(out=out, in_=in_, **kw).then_inc(self.kb.sems[nm], 16)
        got = self.kb.inc(nm, 16)
        assert got == v, (nm, got, v)
        return (nm, v)

    def w(self, e, k):
        nm, v = self.token(k)
        e.wait_ge(self.kb.sems[nm], v)


def ph_transpose_td(kb, src, dst, sb, ps, nt_src, ident):
    nf = src.shape[1] // 128
    xin, tout = sb
    W = nf * 128
    LD = DS(kb, ("l0", "l1"))
    ST = DS(kb, ("g0", "g1"))
    pe0 = kb.cnt["pe"]
    dv0 = kb.cnt["dve"]

    def f_sync(e):
        for tt in range(nt_src):
            if tt >= 2:
                e.wait_ge(kb.sems["pe"], pe0 + nf * (tt - 1))
            LD.issue(e, out=xin[:, (tt % 2) * W:(tt % 2) * W + W],
                     in_=src[tt * 128:(tt + 1) * 128, :])
    kb.sec("sync", f_sync)

    def f_tensor(e):
        for tt in range(nt_src):
            LD.w(e, tt)
            for f in range(nf):
                k = tt * nf + f
                if k >= 2:
                    e.wait_ge(kb.sems["dve"], dv0 + (k - 1))
                e.transpose(out=ps[k % 2][:, 0:128],
                            in_=xin[:, (tt % 2) * W + f * 128:(tt % 2) * W + f * 128 + 128],
                            identity=ident[:]).then_inc(kb.sems["pe"], 1)
                kb.inc("pe")
    kb.sec("tensor", f_tensor)

    def f_vector(e):
        for tt in range(nt_src):
            if tt >= 2:
                ST.w(e, tt - 2)
            for f in range(nf):
                k = tt * nf + f
                e.wait_ge(kb.sems["pe"], pe0 + k + 1)
                e.tensor_copy(out=tout[:, (tt % 2) * W + f * 128:(tt % 2) * W + f * 128 + 128],
                              in_=ps[k % 2][:, 0:128]).then_inc(kb.sems["dve"], 1)
                kb.inc("dve")
    kb.sec("vector", f_vector)

    def f_gp(e):
        for tt in range(nt_src):
            e.wait_ge(kb.sems["dve"], dv0 + nf * (tt + 1))
            ST.issue(e, out=dst.rearrange("(f p) t -> p f t", p=128)[:, :, tt * 128:(tt + 1) * 128],
                     in_=tout[:, (tt % 2) * W:(tt % 2) * W + W].rearrange("p (f t) -> p f t", f=nf))
    kb.sec("gpsimd", f_gp)


def ph_ln_v2(kb, resid, xhat, sb, ps, consts):
    rfa, rb16, sq16, rowA, rowB, rowT, abc = sb
    ones_bf, ones_f32 = consts["ones_bf"], consts["ones_f32"]
    LD = DS(kb, ("l0", "l1"))
    ac0 = kb.cnt["act"]
    pe0 = kb.cnt["pe"]
    dv0 = kb.cnt["dve"]

    def f_sync(e):
        for tc in range(NTC):
            for fc in range(FC):
                k = tc * FC + fc
                if k >= 2:
                    e.wait_ge(kb.sems["act"], ac0 + 2 * (k - 1))
                LD.issue(e, out=rfa[:, (k % 2) * TC:(k % 2) * TC + TC],
                         in_=resid[fc * 128:(fc + 1) * 128, tc * TC:tc * TC + TC])
    kb.sec("sync", f_sync)

    def f_scalar(e):
        for tc in range(NTC):
            for fc in range(FC):
                k = tc * FC + fc
                LD.w(e, k)
                if k >= 2:
                    e.wait_ge(kb.sems["pe"], pe0 + 2 * (k - 1))
                e.activation(out=rb16[:, (k % 2) * TC:(k % 2) * TC + TC],
                             in_=rfa[:, (k % 2) * TC:(k % 2) * TC + TC],
                             func=AF.Copy).then_inc(kb.sems["act"], 1)
                e.activation(out=sq16[:, (k % 2) * TC:(k % 2) * TC + TC],
                             in_=rfa[:, (k % 2) * TC:(k % 2) * TC + TC],
                             func=AF.Square).then_inc(kb.sems["act"], 1)
                kb.inc("act", 2)
    kb.sec("scalar", f_scalar)

    def f_tensor(e):
        for tc in range(NTC):
            if tc >= 1:
                e.wait_ge(kb.sems["dve"], dv0 + 2 * tc)
            for fc in range(FC):
                k = tc * FC + fc
                e.wait_ge(kb.sems["act"], ac0 + 2 * (k + 1))
                st = (fc == 0)
                sp = (fc == FC - 1)
                e.matmul(out=ps[5][0:1, 0:TC], lhsT=ones_bf[:, 0:1],
                         rhs=rb16[:, (k % 2) * TC:(k % 2) * TC + TC],
                         start=st, stop=sp).then_inc(kb.sems["pe"], 1)
                kb.inc("pe")
                e.matmul(out=ps[6][0:1, 0:TC], lhsT=ones_bf[:, 0:1],
                         rhs=sq16[:, (k % 2) * TC:(k % 2) * TC + TC],
                         start=st, stop=sp).then_inc(kb.sems["pe"], 1)
                kb.inc("pe")
    kb.sec("tensor", f_tensor)

    def f_vector(e):
        for tc in range(NTC):
            e.wait_ge(kb.sems["pe"], pe0 + 2 * FC * (tc + 1))
            e.tensor_copy(out=rowA[0:1, tc * TC:tc * TC + TC],
                          in_=ps[5][0:1, 0:TC]).then_inc(kb.sems["dve"], 1)
            e.tensor_copy(out=rowB[0:1, tc * TC:tc * TC + TC],
                          in_=ps[6][0:1, 0:TC]).then_inc(kb.sems["dve"], 1)
            kb.inc("dve", 2)
    kb.sec("vector", f_vector)
    dv1 = kb.cnt["dve"]

    def f_vector2(e):
        def sw(inst):
            inst.then_inc(kb.sems["dve"], 1)
            e.wait_ge(kb.sems["dve"], kb.inc("dve"))
        e.wait_ge(kb.sems["dve"], dv1)
        sw(e.tensor_scalar(out=rowA[0:1, :], in0=rowA[0:1, :], scalar1=1.0 / D,
                           scalar2=None, op0=OP.mult))
        sw(e.tensor_scalar(out=rowB[0:1, :], in0=rowB[0:1, :], scalar1=1.0 / D,
                           scalar2=None, op0=OP.mult))
        sw(e.tensor_tensor(out=rowT[0:1, :], in0=rowA[0:1, :], in1=rowA[0:1, :],
                           op=OP.mult))
        sw(e.tensor_tensor(out=rowB[0:1, :], in0=rowB[0:1, :], in1=rowT[0:1, :],
                           op=OP.subtract))
        e.tensor_scalar(out=rowB[0:1, :], in0=rowB[0:1, :], scalar1=EPS,
                        scalar2=None, op0=OP.add).then_inc(kb.sems["dve"], 1)
        kb.inc("dve")
    kb.sec("vector", f_vector2)
    dv2 = kb.cnt["dve"]

    def f_scalar2(e):
        e.wait_ge(kb.sems["dve"], dv2)
        e.activation(out=rowT[0:1, :], in_=rowB[0:1, :],
                     func=AF.Sqrt).then_inc(kb.sems["act"], 1)
        kb.inc("act")
    kb.sec("scalar", f_scalar2)
    ac2 = kb.cnt["act"]

    def f_vector3(e):
        def sw(inst):
            inst.then_inc(kb.sems["dve"], 1)
            e.wait_ge(kb.sems["dve"], kb.inc("dve"))
        e.wait_ge(kb.sems["act"], ac2)
        sw(e.reciprocal(out=rowB[0:1, :], in_=rowT[0:1, :]))
        sw(e.tensor_tensor(out=rowA[0:1, :], in0=rowA[0:1, :], in1=rowB[0:1, :],
                           op=OP.mult))
        e.tensor_scalar(out=rowA[0:1, :], in0=rowA[0:1, :], scalar1=-1.0, scalar2=None,
                        op0=OP.mult).then_inc(kb.sems["dve"], 1)
        kb.inc("dve")
    kb.sec("vector", f_vector3)
    dv3 = kb.cnt["dve"]

    pe1 = kb.cnt["pe"]
    sched = []
    carry = None
    for tc in range(NTC):
        sched.append(("copyA", tc))
        if carry is not None:
            sched.append(("add", carry))
            carry = None
        sched.append(("copyB", tc))
        prev = None
        for fc in range(FC):
            k = tc * FC + fc
            sched.append(("mult", k))
            if prev is not None:
                sched.append(("add", prev))
            prev = k
        carry = prev
    sched.append(("add", carry))
    add_tick = {}
    copyB_tick = {}
    t = dv3
    for op, arg in sched:
        t += 1
        if op == "add":
            add_tick[arg] = t
        elif op == "copyB":
            copyB_tick[arg] = t

    def f_tensor2(e):
        e.wait_ge(kb.sems["dve"], dv3)
        for tc in range(NTC):
            if tc >= 1:
                e.wait_ge(kb.sems["dve"], copyB_tick[tc - 1])
            e.matmul(out=ps[5][:, 0:TC], lhsT=ones_f32[0:1, :],
                     rhs=rowB[0:1, tc * TC:tc * TC + TC], start=True, stop=True
                     ).then_inc(kb.sems["pe"], 1)
            kb.inc("pe")
            e.matmul(out=ps[6][:, 0:TC], lhsT=ones_f32[0:1, :],
                     rhs=rowA[0:1, tc * TC:tc * TC + TC], start=True, stop=True
                     ).then_inc(kb.sems["pe"], 1)
            kb.inc("pe")
    kb.sec("tensor", f_tensor2)

    def f_sync2(e):
        for tc in range(NTC):
            for fc in range(FC):
                k = tc * FC + fc
                if k >= 2:
                    e.wait_ge(kb.sems["dve"], add_tick[k - 2])
                elif k == 0:
                    e.wait_ge(kb.sems["dve"], dv3)
                LD.issue(e, out=rfa[:, (k % 2) * TC:(k % 2) * TC + TC],
                         in_=resid[fc * 128:(fc + 1) * 128, tc * TC:tc * TC + TC])
    kb.sec("sync", f_sync2)

    def f_vector4(e):
        for op, arg in sched:
            if op == "copyA":
                e.wait_ge(kb.sems["pe"], pe1 + 2 * (arg + 1))
                e.tensor_copy(out=abc[:, 0:TC], in_=ps[5][:, 0:TC]).then_inc(kb.sems["dve"], 1)
            elif op == "copyB":
                e.tensor_copy(out=abc[:, TC:2 * TC], in_=ps[6][:, 0:TC]).then_inc(kb.sems["dve"], 1)
            elif op == "mult":
                k = arg
                LD.w(e, 64 + k)
                e.tensor_tensor(out=rfa[:, (k % 2) * TC:(k % 2) * TC + TC],
                                in0=rfa[:, (k % 2) * TC:(k % 2) * TC + TC],
                                in1=abc[:, 0:TC], op=OP.mult).then_inc(kb.sems["dve"], 1)
            else:
                k = arg
                tc, fc = k // FC, k % FC
                e.tensor_tensor(out=xhat[:, fc * TOK + tc * TC:fc * TOK + tc * TC + TC],
                                in0=rfa[:, (k % 2) * TC:(k % 2) * TC + TC],
                                in1=abc[:, TC:2 * TC], op=OP.add).then_inc(kb.sems["dve"], 1)
            kb.inc("dve")
    kb.sec("vector", f_vector4)


def ph_proj_fm(kb, xhat, w_pm, bias_pm, n_oc, dest, dest_row0, sb, ps):
    wbf, stage = sb
    WL = DS(kb, ("g0", "g1"))
    SO = DS(kb, ("g2", "g3"))
    pe0 = kb.cnt["pe"]
    dv0 = kb.cnt["dve"]

    def mm_tick(oc, tc):
        return pe0 + oc * NTC + tc + 1
    kb.cnt["pe"] = pe0 + n_oc * NTC

    def ev_tick(oc, tc):
        return dv0 + oc * NTC + tc + 1
    kb.cnt["dve"] = dv0 + n_oc * NTC

    if n_oc == 1:
        ops = [("ld", 0), ("st", 0)]
    else:
        ops = [("ld", 0), ("ld", 1)]
        for oc in range(2, n_oc):
            ops.append(("st", oc - 2))
            ops.append(("ld", oc))
        ops += [("st", n_oc - 2), ("st", n_oc - 1)]

    def f_gp(e):
        for kind, oc in ops:
            if kind == "ld":
                if oc >= 2:
                    e.wait_ge(kb.sems["pe"], mm_tick(oc - 2, NTC - 1))
                WL.issue(e, out=wbf[:, (oc % 2) * FC * 128:(oc % 2) * FC * 128 + FC * 128],
                         in_=w_pm[oc * 128:(oc + 1) * 128, :])
            else:
                e.wait_ge(kb.sems["dve"], ev_tick(oc, NTC - 1))
                SO.issue(e, out=dest[dest_row0 + oc * 128:dest_row0 + (oc + 1) * 128, :],
                         in_=stage[:, (oc % 2) * TOK:(oc % 2) * TOK + TOK])
    kb.sec("gpsimd", f_gp)

    def f_tensor(e):
        for oc in range(n_oc):
            WL.w(e, oc)
            for tc in range(NTC):
                gidx = oc * NTC + tc
                if gidx >= 2:
                    e.wait_ge(kb.sems["dve"], ev_tick((gidx - 2) // NTC, (gidx - 2) % NTC))
                for fc in range(FC):
                    ins = e.matmul(out=ps[gidx % 2][:, 0:TC],
                                   lhsT=wbf[:, (oc % 2) * FC * 128 + fc * 128:(oc % 2) * FC * 128 + fc * 128 + 128],
                                   rhs=xhat[:, fc * TOK + tc * TC:fc * TOK + tc * TC + TC],
                                   start=(fc == 0), stop=(fc == FC - 1))
                ins.then_inc(kb.sems["pe"], 1)
    kb.sec("tensor", f_tensor)

    def f_vector(e):
        for oc in range(n_oc):
            for tc in range(NTC):
                gidx = oc * NTC + tc
                e.wait_ge(kb.sems["pe"], mm_tick(oc, tc))
                if oc >= 2 and tc == 0:
                    SO.w(e, oc - 2)
                if bias_pm is not None:
                    e.tensor_scalar(out=stage[:, (oc % 2) * TOK + tc * TC:(oc % 2) * TOK + tc * TC + TC],
                                    in0=ps[gidx % 2][:, 0:TC],
                                    scalar1=bias_pm[:, oc:oc + 1], scalar2=None,
                                    op0=OP.add).then_inc(kb.sems["dve"], 1)
                else:
                    e.tensor_copy(out=stage[:, (oc % 2) * TOK + tc * TC:(oc % 2) * TOK + tc * TC + TC],
                                  in_=ps[gidx % 2][:, 0:TC]).then_inc(kb.sems["dve"], 1)
    kb.sec("vector", f_vector)


def ph_proj_v(kb, xhat, wv_pm, bv_row, n_vg, VG, dest, dest_row0, sb, ps, consts):
    wbf, vgst, bvb = sb
    ones_f32 = consts["ones_f32"]
    HPG = VG // 128
    WL = DS(kb, ("g0", "g1"))
    SO = DS(kb, ("g2", "g3"))
    pe0 = kb.cnt["pe"]
    dv0 = kb.cnt["dve"]

    def bc_tick(vg):
        return pe0 + vg * (NTT + 1) + 1

    def mm_tick(vg, tt):
        return pe0 + vg * (NTT + 1) + 1 + tt + 1
    kb.cnt["pe"] = pe0 + n_vg * (NTT + 1)

    def bvb_tick(vg):
        return dv0 + vg * (NTT + 1) + 1

    def ev_tick(vg, tt):
        return dv0 + vg * (NTT + 1) + 1 + tt + 1
    kb.cnt["dve"] = dv0 + n_vg * (NTT + 1)

    ops = []
    for vg in range(n_vg):
        ops.append(("ld", vg))
        if vg >= 2:
            ops.append(("st", vg - 2))
    if n_vg >= 2:
        ops.append(("st", n_vg - 2))
    ops.append(("st", n_vg - 1))

    def f_gp(e):
        for kind, vg in ops:
            if kind == "ld":
                if vg >= 2:
                    e.wait_ge(kb.sems["pe"], mm_tick(vg - 2, NTT - 1))
                WL.issue(e, out=wbf[:, (vg % 2) * FC * VG:(vg % 2) * FC * VG + FC * VG],
                         in_=wv_pm[vg * 128:(vg + 1) * 128, :])
            else:
                e.wait_ge(kb.sems["dve"], ev_tick(vg, NTT - 1))
                for hh in range(HPG):
                    h = vg * HPG + hh
                    SO.issue(e, out=dest[dest_row0 + h * 128:dest_row0 + (h + 1) * 128, :],
                             in_=vgst[:, hh * TOK:(hh + 1) * TOK])
    kb.sec("gpsimd", f_gp)

    def f_tensor(e):
        for vg in range(n_vg):
            if vg >= 1:
                e.wait_ge(kb.sems["dve"], bvb_tick(vg - 1))
            e.matmul(out=ps[4][:, 0:VG], lhsT=ones_f32[0:1, :],
                     rhs=bv_row[0:1, vg * VG:(vg + 1) * VG],
                     start=True, stop=True).then_inc(kb.sems["pe"], 1)
            WL.w(e, vg)
            for tt in range(NTT):
                gidx = vg * NTT + tt
                if gidx >= 2:
                    e.wait_ge(kb.sems["dve"], ev_tick((gidx - 2) // NTT, (gidx - 2) % NTT))
                for fc in range(FC):
                    ins = e.matmul(out=ps[gidx % 2][:, 0:VG],
                                   lhsT=xhat[:, fc * TOK + tt * 128:fc * TOK + tt * 128 + 128],
                                   rhs=wbf[:, (vg % 2) * FC * VG + fc * VG:(vg % 2) * FC * VG + fc * VG + VG],
                                   start=(fc == 0), stop=(fc == FC - 1))
                ins.then_inc(kb.sems["pe"], 1)
    kb.sec("tensor", f_tensor)

    def f_vector(e):
        for vg in range(n_vg):
            e.wait_ge(kb.sems["pe"], bc_tick(vg))
            e.tensor_copy(out=bvb[:, 0:VG], in_=ps[4][:, 0:VG]).then_inc(kb.sems["dve"], 1)
            e.wait_ge(kb.sems["dve"], bvb_tick(vg))
            if vg >= 1:
                SO.w(e, vg * HPG - 1)
            for tt in range(NTT):
                gidx = vg * NTT + tt
                e.wait_ge(kb.sems["pe"], mm_tick(vg, tt))
                e.tensor_tensor(out=vgst[:].rearrange("p (hh t) -> p hh t", hh=HPG)[:, :, tt * 128:tt * 128 + 128],
                                in0=ps[gidx % 2][:, 0:VG].rearrange("p (hh dh) -> p hh dh", hh=HPG),
                                in1=bvb[:, 0:VG].rearrange("p (hh dh) -> p hh dh", hh=HPG),
                                op=OP.add).then_inc(kb.sems["dve"], 1)
    kb.sec("vector", f_vector)
def ph_attn(kb, layer, qbuf, kvfull, abuf, sb, ps, consts, qmode=None):
    (ksb, vsb, qsb, ptsb, msksb, tmpsb, asb, qbrow, recb, srw, qrhs, qtmp) = sb
    ones_f32 = consts["ones_f32"]
    ones_bf = consts["ones_bf"]
    isq_vec = consts["isq_vec"]
    midbias = consts["midbias"]
    kposB = consts["kposB"]
    qposB = consts["qposB"]
    mqa = (layer == 1)
    SEQH = SEQ // 2

    if layer == 0:
        nkt = [20 + 4 * qc for qc in range(NTC)]
        masked = [list(range(4 * qc, 4 * qc + 4)) + list(range(16 + 4 * qc, 20 + 4 * qc))
                  for qc in range(NTC)]
        midt = [set(range(4 * qc + 4, 16 + 4 * qc)) for qc in range(NTC)]
        kv_row0 = [0, D]
        rankrows = 2 * D
    else:
        nkt = [NKT] * NTC
        masked = [[] for _ in range(NTC)]
        midt = [set() for _ in range(NTC)]
        kv_row0 = [0, DH]
        rankrows = 2 * DH

    if layer == 0:
        pe0 = kb.cnt["pe"]
        dv0 = kb.cnt["dve"]
        QP = DS(kb, ("l4",))

        def f_syncq(e):
            QP.issue(e, out=qposB, in_=consts["qposB_dram"])
        kb.sec("sync", f_syncq)

        def cp_tick(qc):
            return dv0 + qc * 9 + 1

        def f_tensor_m(e):
            QP.w(e, 0)
            for qc in range(NTC):
                if qc >= 1:
                    e.wait_ge(kb.sems["dve"], cp_tick(qc - 1))
                e.matmul(out=ps[4][:, 0:TC], lhsT=ones_f32[0:1, :],
                         rhs=qposB[0:1, qc * TC:(qc + 1) * TC],
                         start=True, stop=True).then_inc(kb.sems["pe"], 1)
                kb.inc("pe")
        kb.sec("tensor", f_tensor_m)

        def f_vector_m(e):
            for qc in range(NTC):
                e.wait_ge(kb.sems["pe"], pe0 + qc + 1)
                e.tensor_copy(out=qbrow[:, 0:TC], in_=ps[4][:, 0:TC]).then_inc(kb.sems["dve"], 1)
                e.wait_ge(kb.sems["dve"], kb.inc("dve"))
                for j, t in enumerate(masked[qc]):
                    e.tensor_scalar(out=msksb[:, (qc * 8 + j) * TC:(qc * 8 + j) * TC + TC],
                                    in0=qbrow[:, 0:TC], scalar1=kposB[:, t:t + 1],
                                    scalar2=0.0, op0=OP.subtract, op1=OP.min
                                    ).then_inc(kb.sems["dve"], 1)
                    kb.inc("dve")
        kb.sec("vector", f_vector_m)
        kb.barrier()

    nh = H
    scram = (layer == 0)
    if qmode is None:
        qmode = "host" if scram else "plain"
    fusedq = (qmode == "fused")
    if fusedq:
        rankrows = 3 * D
    KV = DS(kb, ("l0", "l1", "l2", "l3"))
    QL = DS(kb, ("l4", "l5", "l6", "l7") if fusedq else ("l4", "l5"))
    AS = DS(kb, ("g0", "g1"))
    QW = H * TOK
    if fusedq:
        QW = 16 * 256

    pe_t = kb.cnt["pe"]
    act_t = kb.cnt["act"]
    dve_t = kb.cnt["dve"]
    S_tick = {}
    RS_tick = {}
    BC_tick = {}
    EXP_tick = {}
    MADD_tick = {}
    EV_tick = {}
    GA_tick = {}
    for h in range(nh):
        for qc in range(NTC):
            n = nkt[qc]
            mset = set(masked[qc])
            if scram:
                dve_t += 3 if fusedq else 1
                GA_tick[(h, qc)] = dve_t
            for kt in range(n):
                pe_t += 1
                S_tick[(h, qc, kt)] = pe_t
                if kt >= 1:
                    pe_t += 1
                    RS_tick[(h, qc, kt - 1)] = pe_t
                if kt in mset:
                    dve_t += 1
                    MADD_tick[(h, qc, kt)] = dve_t
                act_t += 1
                EXP_tick[(h, qc, kt)] = act_t
            pe_t += 1
            RS_tick[(h, qc, n - 1)] = pe_t
            dve_t += 1
            pe_t += 1
            BC_tick[(h, qc)] = pe_t
            dve_t += 1
            dve_t += 1
            EV_tick[(h, qc)] = dve_t
    kb.cnt["pe"] = pe_t
    kb.cnt["act"] = act_t
    kb.cnt["dve"] = dve_t

    def prev_hqc(h, qc):
        return (h, qc - 1) if qc > 0 else (h - 1, NTC - 1)

    def bank_free_tick(h, qc, kt):
        if kt in set(masked[qc]):
            return ("dve", MADD_tick[(h, qc, kt)])
        return ("act", EXP_tick[(h, qc, kt)])

    def f_sync(e):
        for h in range(nh):
            if (not mqa) or h == 0:
                if (not mqa) and h >= 2:
                    e.wait_ge(kb.sems["pe"], RS_tick[(h - 2, NTC - 1, nkt[NTC - 1] - 1)])
                slot = 0 if mqa else (h % 2)
                krow = kv_row0[0] + (0 if mqa else h * 128)
                vrow = kv_row0[1] + (0 if mqa else h * 128)
                KV.issue(e, out=ksb[:, slot * SEQ:(slot + 1) * SEQ].rearrange("p (r t) -> p r t", r=2),
                         in_=kvfull.rearrange("(r z) t -> z r t", r=2)[krow:krow + 128, :, :])
                KV.issue(e, out=vsb[:, slot * SEQ:(slot + 1) * SEQ].rearrange("p (r t) -> p r t", r=2),
                         in_=kvfull.rearrange("(r z) t -> z r t", r=2)[vrow:vrow + 128, :, :])
            if h >= 2:
                if scram:
                    e.wait_ge(kb.sems["dve"], GA_tick[(h - 2, NTC - 1)])
                else:
                    e.wait_ge(kb.sems["pe"], S_tick[(h - 2, NTC - 1, nkt[NTC - 1] - 1)])
            if fusedq:
                r = h // 8
                qrow = r * 3 * D + 2 * D
                qcol = 256 * (h % 8)
                for fh in range(2):
                    QL.issue(e, out=qsb[:, (h % 2) * QW + fh * 8 * 256:(h % 2) * QW + (fh + 1) * 8 * 256
                                        ].rearrange("p (fc g) -> p fc g", fc=8),
                             in_=qbuf[qrow + fh * 8 * 128:qrow + (fh + 1) * 8 * 128,
                                      qcol:qcol + 256].rearrange("(fc p) g -> p fc g", p=128))
            elif scram:
                QL.issue(e, out=qsb[:, (h % 2) * TOK:(h % 2) * TOK + TOK],
                         in_=qbuf[:, h * TOK:(h + 1) * TOK])
            else:
                QL.issue(e, out=qsb[:, (h % 2) * TOK:(h % 2) * TOK + TOK],
                         in_=qbuf[h * 128:(h + 1) * 128, :])
    kb.sec("sync", f_sync)

    def f_tensor(e):
        for h in range(nh):
            kslot = 0 if mqa else (h % 2)
            if mqa:
                KV.w(e, 0)
                KV.w(e, 1)
            else:
                KV.w(e, 2 * h)
                KV.w(e, 2 * h + 1)
            QL.w(e, h)
            for qc in range(NTC):
                n = nkt[qc]
                for kt in range(n):
                    if kt >= 2:
                        sem, v = bank_free_tick(h, qc, kt - 2)
                        e.wait_ge(kb.sems[sem], v)
                    elif qc > 0 or h > 0:
                        ph_, pqc = prev_hqc(h, qc)
                        pn = nkt[pqc]
                        sem, v = bank_free_tick(ph_, pqc, pn - 2 + kt)
                        e.wait_ge(kb.sems[sem], v)
                    if kt == 0 and scram:
                        e.wait_ge(kb.sems["dve"], GA_tick[(h, qc)])
                    if scram:
                        s_rhs = qrhs[:, (qc % 2) * TC:(qc % 2) * TC + TC]
                    else:
                        s_rhs = qsb[:, (h % 2) * TOK + qc * TC:(h % 2) * TOK + qc * TC + TC]
                    e.matmul(out=ps[kt % 2][:, 0:TC],
                             lhsT=ksb[:, kslot * SEQ + kt * 128:kslot * SEQ + kt * 128 + 128],
                             rhs=s_rhs,
                             start=True, stop=True).then_inc(kb.sems["pe"], 1)
                    if kt >= 1:
                        kp = kt - 1
                        e.wait_ge(kb.sems["act"], EXP_tick[(h, qc, kp)])
                        if kp == 0 and (qc > 0 or h > 0):
                            ph_, pqc = prev_hqc(h, qc)
                            e.wait_ge(kb.sems["dve"], EV_tick[(ph_, pqc)])
                        e.matmul(out=ps[2][:, 0:TC],
                                 lhsT=vsb[:, kslot * SEQ + kp * 128:kslot * SEQ + kp * 128 + 128],
                                 rhs=ptsb[:, (kp % 2) * TC:(kp % 2) * TC + TC],
                                 start=(kp == 0), stop=False)
                        e.matmul(out=ps[3][0:1, 0:TC],
                                 lhsT=ones_bf[:, 0:1],
                                 rhs=ptsb[:, (kp % 2) * TC:(kp % 2) * TC + TC],
                                 start=(kp == 0), stop=False).then_inc(kb.sems["pe"], 1)
                kp = n - 1
                e.wait_ge(kb.sems["act"], EXP_tick[(h, qc, kp)])
                if kp == 0 and (qc > 0 or h > 0):
                    ph_, pqc = prev_hqc(h, qc)
                    e.wait_ge(kb.sems["dve"], EV_tick[(ph_, pqc)])
                e.matmul(out=ps[2][:, 0:TC],
                         lhsT=vsb[:, kslot * SEQ + kp * 128:kslot * SEQ + kp * 128 + 128],
                         rhs=ptsb[:, (kp % 2) * TC:(kp % 2) * TC + TC],
                         start=(kp == 0), stop=True)
                e.matmul(out=ps[3][0:1, 0:TC],
                         lhsT=ones_bf[:, 0:1],
                         rhs=ptsb[:, (kp % 2) * TC:(kp % 2) * TC + TC],
                         start=(kp == 0), stop=True).then_inc(kb.sems["pe"], 1)
                e.wait_ge(kb.sems["dve"], EV_tick[(h, qc)] - 2)
                e.matmul(out=ps[4][:, 0:TC], lhsT=ones_f32[0:1, :],
                         rhs=srw[0:1, 0:TC], start=True, stop=True).then_inc(kb.sems["pe"], 1)
    kb.sec("tensor", f_tensor)

    def f_scalar(e):
        for h in range(nh):
            for qc in range(NTC):
                n = nkt[qc]
                mset = set(masked[qc])
                for kt in range(n):
                    if kt >= 2:
                        e.wait_ge(kb.sems["pe"], RS_tick[(h, qc, kt - 2)])
                    elif qc > 0 or h > 0:
                        ph_, pqc = prev_hqc(h, qc)
                        pn = nkt[pqc]
                        e.wait_ge(kb.sems["pe"], RS_tick[(ph_, pqc, pn - 2 + kt)])
                    if kt in mset:
                        e.wait_ge(kb.sems["dve"], MADD_tick[(h, qc, kt)])
                        e.activation(out=ptsb[:, (kt % 2) * TC:(kt % 2) * TC + TC],
                                     in_=tmpsb[:, (kt % 2) * TC:(kt % 2) * TC + TC],
                                     func=AF.Exp,
                                     scale=isq_vec[:, 0:1]).then_inc(kb.sems["act"], 1)
                    elif kt in midt[qc]:
                        e.wait_ge(kb.sems["pe"], S_tick[(h, qc, kt)])
                        e.activation(out=ptsb[:, (kt % 2) * TC:(kt % 2) * TC + TC],
                                     in_=ps[kt % 2][:, 0:TC], func=AF.Exp,
                                     scale=isq_vec[:, 0:1],
                                     bias=midbias[:, 0:1]).then_inc(kb.sems["act"], 1)
                    else:
                        e.wait_ge(kb.sems["pe"], S_tick[(h, qc, kt)])
                        e.activation(out=ptsb[:, (kt % 2) * TC:(kt % 2) * TC + TC],
                                     in_=ps[kt % 2][:, 0:TC], func=AF.Exp,
                                     scale=isq_vec[:, 0:1]).then_inc(kb.sems["act"], 1)
    kb.sec("scalar", f_scalar)

    def f_vector(e):
        for h in range(nh):
            for qc in range(NTC):
                n = nkt[qc]
                if scram:
                    if fusedq:
                        QL.w(e, 2 * h)
                        QL.w(e, 2 * h + 1)
                    else:
                        QL.w(e, h)
                    if qc >= 2:
                        e.wait_ge(kb.sems["pe"], S_tick[(h, qc - 2, nkt[qc - 2] - 1)])
                    elif h >= 1:
                        e.wait_ge(kb.sems["pe"], S_tick[(h - 1, NTC - 2 + qc, nkt[NTC - 2 + qc] - 1)])
                    if fusedq:
                        qv = qsb[:, (h % 2) * QW:(h % 2) * QW + QW].rearrange("p (m g) -> p g m", m=16)
                        e.tensor_scalar(
                            out=qtmp[:, 0:TC].rearrange("p (g m) -> p g m", m=16),
                            in0=qv[:, 128 + qc * 32:128 + (qc + 1) * 32, :],
                            scalar1=consts["qw1"][:, 0:1], scalar2=None,
                            op0=OP.mult).then_inc(kb.sems["dve"], 1)
                        e.tensor_scalar(
                            out=qrhs[:, (qc % 2) * TC:(qc % 2) * TC + TC].rearrange("p (g m) -> p g m", m=16),
                            in0=qv[:, qc * 32:(qc + 1) * 32, :],
                            scalar1=consts["qw0"][:, 0:1], scalar2=None,
                            op0=OP.mult).then_inc(kb.sems["dve"], 1)
                        e.wait_ge(kb.sems["dve"], GA_tick[(h, qc)] - 1)
                        e.tensor_tensor(
                            out=qrhs[:, (qc % 2) * TC:(qc % 2) * TC + TC],
                            in0=qrhs[:, (qc % 2) * TC:(qc % 2) * TC + TC],
                            in1=qtmp[:, 0:TC],
                            op=OP.add).then_inc(kb.sems["dve"], 1)
                    else:
                        e.tensor_copy(
                            out=qrhs[:, (qc % 2) * TC:(qc % 2) * TC + TC].rearrange("p (g m) -> p g m", m=16),
                            in_=qsb[:, (h % 2) * TOK:(h % 2) * TOK + TOK].rearrange(
                                "p (m g) -> p g m", m=16)[:, qc * 32:(qc + 1) * 32, :]
                        ).then_inc(kb.sems["dve"], 1)
                for kt in range(n):
                    if kt in set(masked[qc]):
                        e.wait_ge(kb.sems["pe"], S_tick[(h, qc, kt)])
                        if kt >= 2:
                            e.wait_ge(kb.sems["act"], EXP_tick[(h, qc, kt - 2)])
                        elif qc > 0 or h > 0:
                            ph_, pqc = prev_hqc(h, qc)
                            pn = nkt[pqc]
                            e.wait_ge(kb.sems["act"], EXP_tick[(ph_, pqc, pn - 2 + kt)])
                        j = masked[qc].index(kt)
                        e.tensor_tensor(out=tmpsb[:, (kt % 2) * TC:(kt % 2) * TC + TC],
                                        in0=ps[kt % 2][:, 0:TC],
                                        in1=msksb[:, (qc * 8 + j) * TC:(qc * 8 + j) * TC + TC],
                                        op=OP.add).then_inc(kb.sems["dve"], 1)
                e.wait_ge(kb.sems["pe"], RS_tick[(h, qc, n - 1)])
                e.reciprocal(out=srw[0:1, 0:TC], in_=ps[3][0:1, 0:TC]).then_inc(kb.sems["dve"], 1)
                e.wait_ge(kb.sems["dve"], EV_tick[(h, qc)] - 2)
                e.wait_ge(kb.sems["pe"], BC_tick[(h, qc)])
                e.tensor_copy(out=recb[:, 0:TC], in_=ps[4][:, 0:TC]).then_inc(kb.sems["dve"], 1)
                e.wait_ge(kb.sems["dve"], EV_tick[(h, qc)] - 1)
                if h >= 2 and qc == 0:
                    AS.w(e, h - 2)
                e.tensor_tensor(out=asb[:, (h % 2) * TOK + qc * TC:(h % 2) * TOK + qc * TC + TC],
                                in0=ps[2][:, 0:TC], in1=recb[:, 0:TC],
                                op=OP.mult).then_inc(kb.sems["dve"], 1)
    kb.sec("vector", f_vector)

    def f_gp(e):
        for h in range(nh):
            e.wait_ge(kb.sems["dve"], EV_tick[(h, NTC - 1)])
            AS.issue(e, out=abuf[h * 128:(h + 1) * 128, :],
                     in_=asb[:, (h % 2) * TOK:(h % 2) * TOK + TOK])
    kb.sec("gpsimd", f_gp)


def ph_wo(kb, abuf, wo_dram, bo_pm, resid, sb, ps):
    wosb, atsb, rsb, rosb = sb
    for half in range(2):
        pe0 = kb.cnt["pe"]
        dv0 = kb.cnt["dve"]
        nidx = NTC * 8
        AT = DS(kb, ("l0", "l1"))
        RS = DS(kb, ("l2", "l3"))
        WOL = DS(kb, ("g0", "g1"))
        WB = DS(kb, ("g2", "g3", "g4", "g5"))

        dsched = [("ar", 0), ("ar", 1)]
        for i in range(2, nidx):
            dsched.append(("ab", i - 2))
            dsched.append(("ar", i))
        dsched += [("ab", nidx - 2), ("ab", nidx - 1)]
        ar_tick = {}
        ab_tick = {}
        t = dv0
        for kind, i in dsched:
            t += 1
            (ar_tick if kind == "ar" else ab_tick)[i] = t
        kb.cnt["dve"] = t

        def mm_tick(i, pe0=pe0):
            return pe0 + i + 1
        kb.cnt["pe"] = pe0 + nidx

        def f_gp(e, half=half, pe0=pe0, ab_tick=ab_tick, WOL=WOL, WB=WB):
            e.wait_ge(kb.sems["pe"], pe0)
            WOL.issue(e, out=wosb[:],
                      in_=wo_dram.rearrange("(oc p) o2 -> p oc o2", p=128)[:, :, half * 1024:(half + 1) * 1024])
            for i in range(nidx):
                e.wait_ge(kb.sems["dve"], ab_tick[i])
                tc, j = i // 8, i % 8
                o2 = half * 8 + j
                WB.issue(e, out=resid[o2 * 128:(o2 + 1) * 128, tc * TC:(tc + 1) * TC],
                         in_=rosb[:, (i % 4) * TC:(i % 4) * TC + TC])
        kb.sec("gpsimd", f_gp)

        def f_sync(e, half=half, pe0=pe0, dv0=dv0, ar_tick=ar_tick, AT=AT, RS=RS, mm_tick=mm_tick):
            for tc in range(NTC):
                if tc >= 2:
                    e.wait_ge(kb.sems["pe"], mm_tick((tc - 1) * 8 - 1))
                else:
                    e.wait_ge(kb.sems["pe"], pe0)
                AT.issue(e, out=atsb[:, (tc % 2) * FC * TC:(tc % 2) * FC * TC + FC * TC],
                         in_=abuf.rearrange("(oc p) t -> p oc t", p=128)[:, :, tc * TC:(tc + 1) * TC])
                for j in range(8):
                    i = tc * 8 + j
                    if i >= 2:
                        e.wait_ge(kb.sems["dve"], ar_tick[i - 2])
                    elif half == 1:
                        e.wait_ge(kb.sems["dve"], dv0)
                    RS.issue(e, out=rsb[:, (i % 4) * TC:(i % 4) * TC + TC],
                             in_=resid[(half * 8 + j) * 128:(half * 8 + j + 1) * 128,
                                       tc * TC:(tc + 1) * TC])
        kb.sec("sync", f_sync)

        def f_tensor(e, half=half, AT=AT, WOL=WOL, ar_tick=ar_tick, mm_tick=mm_tick):
            WOL.w(e, 0)
            for tc in range(NTC):
                AT.w(e, tc)
                for j in range(8):
                    i = tc * 8 + j
                    if i >= 2:
                        e.wait_ge(kb.sems["dve"], ar_tick[i - 2])
                    for oc in range(FC):
                        ins = e.matmul(out=ps[i % 2][:, 0:TC],
                                       lhsT=wosb[:, oc * 1024 + j * 128:oc * 1024 + j * 128 + 128],
                                       rhs=atsb[:, (tc % 2) * FC * TC + oc * TC:(tc % 2) * FC * TC + oc * TC + TC],
                                       start=(oc == 0), stop=(oc == FC - 1))
                    ins.then_inc(kb.sems["pe"], 1)
        kb.sec("tensor", f_tensor)

        def f_vector(e, half=half, dsched=dsched, RS=RS, WB=WB, mm_tick=mm_tick):
            for kind, i in dsched:
                tc, j = i // 8, i % 8
                if kind == "ar":
                    e.wait_ge(kb.sems["pe"], mm_tick(i))
                    RS.w(e, i)
                    if i >= 4:
                        WB.w(e, i - 4)
                    e.tensor_tensor(out=rosb[:, (i % 4) * TC:(i % 4) * TC + TC],
                                    in0=ps[i % 2][:, 0:TC],
                                    in1=rsb[:, (i % 4) * TC:(i % 4) * TC + TC],
                                    op=OP.add).then_inc(kb.sems["dve"], 1)
                else:
                    o2 = half * 8 + j
                    e.tensor_scalar(out=rosb[:, (i % 4) * TC:(i % 4) * TC + TC],
                                    in0=rosb[:, (i % 4) * TC:(i % 4) * TC + TC],
                                    scalar1=bo_pm[:, o2:o2 + 1], scalar2=None,
                                    op0=OP.add).then_inc(kb.sems["dve"], 1)
        kb.sec("vector", f_vector)
def ph_down_prep(kb, down_pm, wd_scr, ai32, ab, ps):
    NI = IFF // 128
    DL = DS(kb, ("l0", "l1"))
    PS_ = DS(kb, ("g0", "g1"))
    ac0 = kb.cnt["act"]

    def f_sync(e):
        for ic in range(NI):
            if ic >= 2:
                e.wait_ge(kb.sems["act"], ac0 + ic - 1)
            DL.issue(e, out=ai32[:, (ic % 2) * D:(ic % 2) * D + D],
                     in_=down_pm[ic * 128:(ic + 1) * 128, :])
    kb.sec("sync", f_sync)

    def f_scalar(e):
        for ic in range(NI):
            DL.w(e, ic)
            if ic >= 2:
                PS_.w(e, ic - 2)
            e.activation(out=ab[:, (ic % 2) * D:(ic % 2) * D + D],
                         in_=ai32[:, (ic % 2) * D:(ic % 2) * D + D],
                         func=AF.Copy, bias=-128.0).then_inc(kb.sems["act"], 1)
            kb.inc("act")
    kb.sec("scalar", f_scalar)

    def f_gp(e):
        for ic in range(NI):
            e.wait_ge(kb.sems["act"], ac0 + ic + 1)
            PS_.issue(e, out=wd_scr[ic * 128:(ic + 1) * 128, :],
                      in_=ab[:, (ic % 2) * D:(ic % 2) * D + D])
    kb.sec("gpsimd", f_gp)


def ph_ffn0_a(kb, xhat, gate_pm, up_pm, h_scr, sgv, suv, sb, ps):
    ai32, wbuf, silu_sb, u_sb, h_stage = sb
    NIB = 16
    GL = DS(kb, ("l0", "l1"))
    HS = DS(kb, ("g0", "g1"))
    ac0 = kb.cnt["act"]
    pe0 = kb.cnt["pe"]
    dv0 = kb.cnt["dve"]

    def wgT(ib):
        return wbuf[:, (ib % 2) * 8192:(ib % 2) * 8192 + 8192]

    def wuT(ib):
        return wbuf[:, 16384 + (ib % 2) * 8192:16384 + (ib % 2) * 8192 + 8192]

    sched = [("c", 0, j) for j in range(8)]
    for ib in range(NIB):
        for k in range(16):
            sched.append(("s", ib, k))
            if ib + 1 < NIB and k % 2 == 0:
                sched.append(("c", ib + 1, k // 2))
    cast_tick = {}
    silu_tick = {}
    t = ac0
    for kind, ib, k in sched:
        t += 1
        (cast_tick if kind == "c" else silu_tick)[(ib, k)] = t
    kb.cnt["act"] = t

    def g_tick(ib, k):
        return pe0 + ib * 32 + 2 * k + 1

    def u_tick(ib, k):
        return pe0 + ib * 32 + 2 * k + 2

    def dsched_ib():
        s = [("ue", 0), ("ue", 1)]
        for k in range(2, 16):
            s.append(("h", k - 2))
            s.append(("ue", k))
        s += [("h", 14), ("h", 15)]
        return s
    ue_tick = {}
    h_tick = {}
    t = dv0
    for ib in range(NIB):
        for kind, k in dsched_ib():
            t += 1
            (ue_tick if kind == "ue" else h_tick)[(ib, k)] = t
    kb.cnt["dve"] = t
    kb.cnt["pe"] = pe0 + NIB * 32

    def f_sync(e):
        for ib in range(NIB):
            for j in range(8):
                if j >= 2:
                    e.wait_ge(kb.sems["act"], cast_tick[(ib, j - 2)])
                elif ib >= 1:
                    e.wait_ge(kb.sems["act"], cast_tick[(ib - 1, j + 6)])
                src = gate_pm if j < 4 else up_pm
                q = j % 4
                GL.issue(e, out=ai32[:, (j % 2) * 2048:(j % 2) * 2048 + 2048],
                         in_=src[ib * 128:(ib + 1) * 128, q * 2048:(q + 1) * 2048])
    kb.sec("sync", f_sync)

    def f_scalar(e):
        for kind, ib, k in sched:
            if kind == "c":
                j = k
                GL.w(e, ib * 8 + j)
                if ib >= 2 and j in (0, 4):
                    e.wait_ge(kb.sems["pe"], pe0 + (ib - 1) * 32)
                dst = wgT(ib) if j < 4 else wuT(ib)
                q = j % 4
                e.activation(out=dst[:, q * 2048:(q + 1) * 2048],
                             in_=ai32[:, (j % 2) * 2048:(j % 2) * 2048 + 2048],
                             func=AF.Copy, bias=-128.0).then_inc(kb.sems["act"], 1)
            else:
                icb, tc = k // 4, k % 4
                e.wait_ge(kb.sems["pe"], g_tick(ib, k))
                gi = ib * 16 + k
                if gi >= 2:
                    pib, pk = (gi - 2) // 16, (gi - 2) % 16
                    e.wait_ge(kb.sems["dve"], h_tick[(pib, pk)])
                ic = ib * 4 + icb
                e.activation(out=silu_sb[:, (k % 2) * TC:(k % 2) * TC + TC],
                             in_=ps[k % 2][:, 0:TC], func=AF.Silu,
                             scale=sgv[:, ic:ic + 1]).then_inc(kb.sems["act"], 1)
    kb.sec("scalar", f_scalar)

    def f_tensor(e):
        for ib in range(NIB):
            for j in range(8):
                e.wait_ge(kb.sems["act"], cast_tick[(ib, j)])
            for k in range(16):
                icb, tc = k // 4, k % 4
                gi = ib * 16 + k
                if gi >= 2:
                    pib, pk = (gi - 2) // 16, (gi - 2) % 16
                    e.wait_ge(kb.sems["act"], silu_tick[(pib, pk)])
                    e.wait_ge(kb.sems["dve"], ue_tick[(pib, pk)])
                for fc in range(FC):
                    ins = e.matmul(out=ps[k % 2][:, 0:TC],
                                   lhsT=wgT(ib)[:, fc * 512 + icb * 128:fc * 512 + icb * 128 + 128],
                                   rhs=xhat[:, fc * TOK + tc * TC:fc * TOK + tc * TC + TC],
                                   start=(fc == 0), stop=(fc == FC - 1))
                ins.then_inc(kb.sems["pe"], 1)
                for fc in range(FC):
                    ins = e.matmul(out=ps[2 + k % 2][:, 0:TC],
                                   lhsT=wuT(ib)[:, fc * 512 + icb * 128:fc * 512 + icb * 128 + 128],
                                   rhs=xhat[:, fc * TOK + tc * TC:fc * TOK + tc * TC + TC],
                                   start=(fc == 0), stop=(fc == FC - 1))
                ins.then_inc(kb.sems["pe"], 1)
    kb.sec("tensor", f_tensor)

    def f_vector(e):
        for ib in range(NIB):
            for kind, k in dsched_ib():
                icb, tc = k // 4, k % 4
                ic = ib * 4 + icb
                if kind == "ue":
                    e.wait_ge(kb.sems["pe"], u_tick(ib, k))
                    if ib > 0 or k >= 2:
                        pib, pk = (ib, k - 2) if k >= 2 else (ib - 1, 14 + k)
                        e.wait_ge(kb.sems["dve"], h_tick[(pib, pk)])
                    e.tensor_scalar(out=u_sb[:, (k % 2) * TC:(k % 2) * TC + TC],
                                    in0=ps[2 + k % 2][:, 0:TC],
                                    scalar1=suv[:, ic:ic + 1], scalar2=None,
                                    op0=OP.mult).then_inc(kb.sems["dve"], 1)
                else:
                    e.wait_ge(kb.sems["act"], silu_tick[(ib, k)])
                    if tc == 0 and ic >= 2:
                        HS.w(e, ic - 2)
                    e.tensor_tensor(out=h_stage[:, (icb % 2) * TOK + tc * TC:(icb % 2) * TOK + tc * TC + TC],
                                    in0=silu_sb[:, (k % 2) * TC:(k % 2) * TC + TC],
                                    in1=u_sb[:, (k % 2) * TC:(k % 2) * TC + TC],
                                    op=OP.mult).then_inc(kb.sems["dve"], 1)
    kb.sec("vector", f_vector)

    def f_gp(e):
        for ib in range(NIB):
            for icb in range(4):
                ic = ib * 4 + icb
                e.wait_ge(kb.sems["dve"], h_tick[(ib, icb * 4 + 3)])
                HS.issue(e, out=h_scr[:, ic * TOK:(ic + 1) * TOK],
                         in_=h_stage[:, (icb % 2) * TOK:(icb % 2) * TOK + TOK])
    kb.sec("gpsimd", f_gp)


def ph_ffn0_b(kb, h_scr, down_pm, sdv, resid, sb, ps, consts):
    htA, htB, wdt, rsb, rosb, hsb = sb
    ones_bf, ones_f32 = consts["ones_bf"], consts["ones_f32"]
    hsr = consts["hsr"]
    NI = IFF // 128
    NG = 8
    GSZ = NI // NG
    hts = (htA, htB)
    HT = DS(kb, ("l0", "l1"))
    RS = DS(kb, ("l2", "l3"))
    WDT = DS(kb, ("g0", "g1"))
    WB = DS(kb, ("g2", "g3", "g4", "g5"))
    pe0 = kb.cnt["pe"]
    dv0 = kb.cnt["dve"]

    def hs_tick(tc, ic):
        return pe0 + tc * (NI + 1 + 4 * NG) + ic + 1

    def bc_tick(tc):
        return pe0 + tc * (NI + 1 + 4 * NG) + NI + 1

    def mm_tick(tc, og, icg):
        return pe0 + tc * (NI + 1 + 4 * NG) + NI + 1 + og * NG + icg + 1
    kb.cnt["pe"] = pe0 + NTC * (NI + 1 + 4 * NG)

    seq12 = [("a", 0), ("a", 1), ("m", 0), ("a", 2), ("m", 1), ("a", 3),
             ("m", 2), ("r", 0), ("m", 3), ("r", 1), ("r", 2), ("r", 3)]
    a_tick = {}
    m_tick = {}
    r_tick = {}
    hrow_tick = {}
    t = dv0
    for tc in range(NTC):
        t += 3
        hrow_tick[tc] = t
        for og in range(4):
            for kind, j in seq12:
                t += 1
                {"a": a_tick, "m": m_tick, "r": r_tick}[kind][(tc, og, j)] = t
    kb.cnt["dve"] = t

    def f_sync(e):
        for tc in range(NTC):
            if tc >= 2:
                e.wait_ge(kb.sems["pe"], mm_tick(tc - 2, 3, NG - 1))
            HT.issue(e, out=hts[tc % 2].rearrange("p (ic t) -> p ic t", ic=NI),
                     in_=h_scr.rearrange("p (ic t) -> p ic t", ic=NI)[:, :, tc * TC:(tc + 1) * TC])
            for og in range(4):
                for j in range(4):
                    i = (tc * 4 + og) * 4 + j
                    if i >= 2:
                        pi = i - 2
                        ptc, pog, pj = pi // 16, (pi // 4) % 4, pi % 4
                        e.wait_ge(kb.sems["dve"], r_tick[(ptc, pog, pj)])
                    oc = og * 4 + j
                    RS.issue(e, out=rsb[:, (i % 4) * TC:(i % 4) * TC + TC],
                             in_=resid[oc * 128:(oc + 1) * 128, tc * TC:(tc + 1) * TC])
    kb.sec("sync", f_sync)

    def f_gp(e):
        for tc in range(NTC):
            for og in range(4):
                for icg in range(NG):
                    gi = (tc * 4 + og) * NG + icg
                    if gi >= 2:
                        pgi = gi - 2
                        ptc, pog, picg = pgi // (4 * NG), (pgi // NG) % 4, pgi % NG
                        e.wait_ge(kb.sems["pe"], mm_tick(ptc, pog, picg))
                    WDT.issue(e,
                              out=wdt[:, (icg % 2) * 4096:(icg % 2) * 4096 + 4096]
                              .rearrange("p (ic oc) -> p ic oc", ic=GSZ),
                              in_=down_pm.rearrange("(ic p) oc -> p ic oc", p=128)
                              [:, icg * GSZ:(icg + 1) * GSZ, og * 512:(og + 1) * 512])
    kb.sec("gpsimd", f_gp)

    def f_scalar(e):
        for tc in range(NTC):
            for og in range(4):
                for j in range(4):
                    e.wait_ge(kb.sems["dve"], r_tick[(tc, og, j)])
                    oc = og * 4 + j
                    WB.issue(e, out=resid[oc * 128:(oc + 1) * 128, tc * TC:(tc + 1) * TC],
                             in_=rosb[:, (j % 4) * TC:(j % 4) * TC + TC])
    kb.sec("scalar", f_scalar)

    def f_tensor(e):
        for tc in range(NTC):
            ht = hts[tc % 2]
            HT.w(e, tc)
            if tc >= 1:
                e.wait_ge(kb.sems["dve"], hrow_tick[tc - 1] - 2)
            for ic in range(NI):
                e.matmul(out=ps[5][0:1, 0:TC], lhsT=ones_bf[:, 0:1],
                         rhs=ht[:, ic * 512:(ic + 1) * 512],
                         start=(ic == 0), stop=(ic == NI - 1)).then_inc(kb.sems["pe"], 1)
            e.wait_ge(kb.sems["dve"], hrow_tick[tc] - 1)
            if tc >= 1:
                e.wait_ge(kb.sems["dve"], hrow_tick[tc - 1])
            e.matmul(out=ps[6][:, 0:TC], lhsT=ones_f32[0:1, :],
                     rhs=hsr[0:1, 0:TC], start=True, stop=True).then_inc(kb.sems["pe"], 1)
            for og in range(4):
                for icg in range(NG):
                    gi = (tc * 4 + og) * NG + icg
                    WDT.w(e, gi)
                    if icg == 0 and (tc > 0 or og > 0):
                        ptc, pog = (tc, og - 1) if og > 0 else (tc - 1, 3)
                        e.wait_ge(kb.sems["dve"], a_tick[(ptc, pog, 3)])
                    for i8 in range(GSZ):
                        ic = icg * GSZ + i8
                        for j in range(4):
                            base = (icg % 2) * 4096 + i8 * 512 + j * 128
                            ins = e.matmul(out=ps[j][:, 0:TC],
                                           lhsT=wdt[:, base:base + 128],
                                           rhs=ht[:, ic * 512:(ic + 1) * 512],
                                           start=(ic == 0), stop=(ic == NI - 1))
                    ins.then_inc(kb.sems["pe"], 1)
    kb.sec("tensor", f_tensor)

    def f_vector(e):
        for tc in range(NTC):
            e.wait_ge(kb.sems["pe"], hs_tick(tc, NI - 1))
            e.tensor_copy(out=hsr[0:1, 0:TC], in_=ps[5][0:1, 0:TC]).then_inc(kb.sems["dve"], 1)
            e.wait_ge(kb.sems["dve"], hrow_tick[tc] - 2)
            e.tensor_scalar(out=hsr[0:1, 0:TC], in0=hsr[0:1, 0:TC], scalar1=-128.0,
                            scalar2=None, op0=OP.mult).then_inc(kb.sems["dve"], 1)
            e.wait_ge(kb.sems["pe"], bc_tick(tc))
            e.tensor_copy(out=hsb[:, 0:TC], in_=ps[6][:, 0:TC]).then_inc(kb.sems["dve"], 1)
            e.wait_ge(kb.sems["dve"], hrow_tick[tc])
            for og in range(4):
                for kind, j in seq12:
                    oc = og * 4 + j
                    i = (tc * 4 + og) * 4 + j
                    if kind == "a":
                        e.wait_ge(kb.sems["pe"], mm_tick(tc, og, NG - 1))
                        if i >= 4:
                            WB.w(e, i - 4)
                        e.tensor_tensor(out=rosb[:, (j % 4) * TC:(j % 4) * TC + TC],
                                        in0=ps[j][:, 0:TC],
                                        in1=hsb[:, 0:TC],
                                        op=OP.add).then_inc(kb.sems["dve"], 1)
                    elif kind == "m":
                        e.tensor_scalar(out=rosb[:, (j % 4) * TC:(j % 4) * TC + TC],
                                        in0=rosb[:, (j % 4) * TC:(j % 4) * TC + TC],
                                        scalar1=sdv[:, oc:oc + 1], scalar2=None,
                                        op0=OP.mult).then_inc(kb.sems["dve"], 1)
                    else:
                        RS.w(e, i)
                        e.tensor_tensor(out=rosb[:, (j % 4) * TC:(j % 4) * TC + TC],
                                        in0=rosb[:, (j % 4) * TC:(j % 4) * TC + TC],
                                        in1=rsb[:, (i % 4) * TC:(i % 4) * TC + TC],
                                        op=OP.add).then_inc(kb.sems["dve"], 1)
    kb.sec("vector", f_vector)


def ph_ffn1(kb, xhat, wt1_pm, g1_pm, at_pm, bs_pm, rep_sb, resid, sb, ps, consts):
    (wtb, at_sb, bs_sb, g1_sb, cwT, cwt_sb, tc_sb, rsb, rosb, tk) = sb
    ident = consts["ident"]

    SL = DS(kb, ("g0", "g1"))

    def f_ld(e):
        SL.issue(e, out=g1_sb[:], in_=g1_pm)
        SL.issue(e, out=at_sb[:], in_=at_pm)
        SL.issue(e, out=bs_sb[:], in_=bs_pm)
    kb.sec("gpsimd", f_ld)
    kb.barrier()

    lg, t1, t2, t5, t7, t8, a2, t3, t4, t6, t9, a1, cwall = tk
    pe0 = kb.cnt["pe"]
    dv0 = kb.cnt["dve"]
    ac0 = kb.cnt["act"]
    ev_tick = {}
    dv = dv0
    ac = ac0
    for tt in range(NTT):
        ev_tick[tt] = dv + 1
        dv += 12
        ac += 2
    kb.cnt["dve"] = dv
    kb.cnt["act"] = ac

    def f_tensor_l(e):
        for tt in range(NTT):
            if tt >= 1:
                e.wait_ge(kb.sems["dve"], ev_tick[tt - 1])
            for fc in range(FC):
                ins = e.matmul(out=ps[0][:, 0:8],
                               lhsT=xhat[:, fc * TOK + tt * 128:fc * TOK + tt * 128 + 128],
                               rhs=g1_sb[:, fc * 8:fc * 8 + 8],
                               start=(fc == 0), stop=(fc == FC - 1))
            ins.then_inc(kb.sems["pe"], 1)
            kb.inc("pe")
    kb.sec("tensor", f_tensor_l)

    def f_scalar_l(e):
        for tt in range(NTT):
            e.wait_ge(kb.sems["dve"], dv0 + tt * 12 + 8)
            e.activation(out=a1[:], in_=lg[:], func=AF.Exp,
                         bias=t2[:, 0:1]).then_inc(kb.sems["act"], 1)
            e.activation(out=a2[:], in_=t5[:, 0:1], func=AF.Exp,
                         bias=t2[:, 0:1]).then_inc(kb.sems["act"], 1)
    kb.sec("scalar", f_scalar_l)

    def f_vector_l(e):
        for tt in range(NTT):
            base = dv0 + tt * 12
            e.wait_ge(kb.sems["pe"], pe0 + tt + 1)

            def swx(inst, k, base=base, e=e):
                inst.then_inc(kb.sems["dve"], 1)
                e.wait_ge(kb.sems["dve"], base + k)
            swx(e.tensor_copy(out=lg[:], in_=ps[0][:, 0:8]), 1)
            swx(e.tensor_reduce(out=t1[:, 0:1], in_=lg[:], axis=AX, op=OP.max), 2)
            swx(e.tensor_scalar(out=t2[:, 0:1], in0=t1[:, 0:1], scalar1=-1.0,
                                scalar2=None, op0=OP.mult), 3)
            swx(e.tensor_scalar(out=t3[:], in0=lg[:], scalar1=t1[:, 0:1],
                                scalar2=1.0e9, op0=OP.is_equal, op1=OP.mult), 4)
            swx(e.tensor_tensor(out=t4[:], in0=lg[:], in1=t3[:], op=OP.subtract), 5)
            swx(e.tensor_reduce(out=t5[:, 0:1], in_=t4[:], axis=AX, op=OP.max), 6)
            swx(e.tensor_scalar(out=t6[:], in0=lg[:], scalar1=t5[:, 0:1],
                                scalar2=None, op0=OP.is_ge), 7)
            e.nop().then_inc(kb.sems["dve"], 1)
            e.wait_ge(kb.sems["act"], ac0 + 2 * (tt + 1))
            swx(e.tensor_scalar(out=t7[:, 0:1], in0=a2[:, 0:1], scalar1=1.0,
                                scalar2=None, op0=OP.add), 9)
            swx(e.reciprocal(out=t8[:, 0:1], in_=t7[:, 0:1]), 10)
            swx(e.tensor_tensor(out=t9[:], in0=a1[:], in1=t6[:], op=OP.mult), 11)
            swx(e.tensor_scalar(out=cwall[:, tt * 8:(tt + 1) * 8], in0=t9[:],
                                scalar1=t8[:, 0:1], scalar2=None, op0=OP.mult), 12)
    kb.sec("vector", f_vector_l)
    kb.barrier()

    pe1 = kb.cnt["pe"]
    dv1 = kb.cnt["dve"]

    def f_tensor_c(e):
        for tt in range(NTT):
            if tt >= 2:
                e.wait_ge(kb.sems["dve"], dv1 + tt - 1)
            e.transpose(out=ps[4 + tt % 2][0:8, 0:128], in_=cwall[:, tt * 8:(tt + 1) * 8],
                        identity=ident[:]).then_inc(kb.sems["pe"], 1)
            kb.inc("pe")
    kb.sec("tensor", f_tensor_c)

    def f_vector_c(e):
        for tt in range(NTT):
            e.wait_ge(kb.sems["pe"], pe1 + tt + 1)
            e.tensor_copy(out=cwT[0:8, tt * 128:(tt + 1) * 128],
                          in_=ps[4 + tt % 2][0:8, 0:128]).then_inc(kb.sems["dve"], 1)
            kb.inc("dve")
    kb.sec("vector", f_vector_c)
    kb.barrier()

    pe2 = kb.cnt["pe"]
    dv2 = kb.cnt["dve"]

    def f_tensor_t(e):
        for tc in range(NTC):
            if tc >= 1:
                e.wait_ge(kb.sems["dve"], dv2 + 2 * tc)
            e.matmul(out=ps[4][:, 0:TC], lhsT=rep_sb[0:8, :],
                     rhs=cwT[0:8, tc * TC:(tc + 1) * TC], start=True, stop=True
                     ).then_inc(kb.sems["pe"], 1)
            kb.inc("pe")
            for fc in range(FC):
                ins = e.matmul(out=ps[tc % 2][:, 0:TC],
                               lhsT=at_sb[:, fc * 128:(fc + 1) * 128],
                               rhs=xhat[:, fc * TOK + tc * TC:fc * TOK + tc * TC + TC],
                               start=(fc == 0), stop=(fc == FC - 1))
            ins.then_inc(kb.sems["pe"], 1)
            kb.inc("pe")
    kb.sec("tensor", f_tensor_t)

    def f_vector_t(e):
        for tc in range(NTC):
            e.wait_ge(kb.sems["pe"], pe2 + 2 * (tc + 1))
            e.tensor_copy(out=cwt_sb[:, tc * TC:(tc + 1) * TC],
                          in_=ps[4][:, 0:TC]).then_inc(kb.sems["dve"], 1)
            e.wait_ge(kb.sems["dve"], kb.inc("dve"))
            e.tensor_tensor(out=tc_sb[:, tc * TC:(tc + 1) * TC],
                            in0=ps[tc % 2][:, 0:TC],
                            in1=cwt_sb[:, tc * TC:(tc + 1) * TC],
                            op=OP.mult).then_inc(kb.sems["dve"], 1)
            kb.inc("dve")
    kb.sec("vector", f_vector_t)
    kb.barrier()

    pe3 = kb.cnt["pe"]
    dv3 = kb.cnt["dve"]
    WT = DS(kb, ("g0", "g1"))
    WB = DS(kb, ("g2", "g3", "g4", "g5"))
    RS = DS(kb, ("l2", "l3"))

    def mm_tick(i):
        return pe3 + i + 1
    kb.cnt["pe"] = pe3 + 64

    def ad_tick(i):
        return dv3 + i + 1
    kb.cnt["dve"] = dv3 + 64

    def f_gp(e):
        ops = []
        for og in range(4):
            ops.append(("ld", og))
            if og >= 1:
                for i in range((og - 1) * 16, og * 16):
                    ops.append(("st", i))
        for i in range(48, 64):
            ops.append(("st", i))
        for kind, v in ops:
            if kind == "ld":
                if v >= 2:
                    e.wait_ge(kb.sems["pe"], mm_tick((v - 2) * 16 + 15))
                WT.issue(e, out=wtb[:, (v % 2) * 8192:(v % 2) * 8192 + 8192],
                         in_=wt1_pm[v * 128:(v + 1) * 128, :])
            else:
                e.wait_ge(kb.sems["dve"], ad_tick(v))
                og, rest = v // 16, v % 16
                oc = og * 4 + rest // 4
                tc = rest % 4
                WB.issue(e, out=resid[oc * 128:(oc + 1) * 128, tc * TC:(tc + 1) * TC],
                         in_=rosb[:, (v % 4) * TC:(v % 4) * TC + TC])
    kb.sec("gpsimd", f_gp)

    def f_sync(e):
        for i in range(64):
            if i >= 2:
                e.wait_ge(kb.sems["dve"], ad_tick(i - 2))
            og, rest = i // 16, i % 16
            oc = og * 4 + rest // 4
            tc = rest % 4
            RS.issue(e, out=rsb[:, (i % 4) * TC:(i % 4) * TC + TC],
                     in_=resid[oc * 128:(oc + 1) * 128, tc * TC:(tc + 1) * TC])
    kb.sec("sync", f_sync)

    def f_tensor_d(e):
        for og in range(4):
            WT.w(e, og)
            for ocr in range(4):
                for tc in range(NTC):
                    i = (og * 4 + ocr) * 4 + tc
                    oc = og * 4 + ocr
                    if i >= 2:
                        e.wait_ge(kb.sems["dve"], ad_tick(i - 2))
                    for fc in range(FC):
                        e.matmul(out=ps[i % 2][:, 0:TC],
                                 lhsT=wtb[:, (og % 2) * 8192 + fc * 512 + ocr * 128:(og % 2) * 8192 + fc * 512 + ocr * 128 + 128],
                                 rhs=xhat[:, fc * TOK + tc * TC:fc * TOK + tc * TC + TC],
                                 start=(fc == 0), stop=False)
                    ins = e.matmul(out=ps[i % 2][:, 0:TC],
                                   lhsT=bs_sb[:, oc * 128:(oc + 1) * 128],
                                   rhs=tc_sb[:, tc * TC:(tc + 1) * TC],
                                   start=False, stop=True)
                    ins.then_inc(kb.sems["pe"], 1)
    kb.sec("tensor", f_tensor_d)

    def f_vector_d(e):
        for i in range(64):
            e.wait_ge(kb.sems["pe"], mm_tick(i))
            RS.w(e, i)
            if i >= 4:
                WB.w(e, i - 4)
            e.tensor_tensor(out=rosb[:, (i % 4) * TC:(i % 4) * TC + TC],
                            in0=ps[i % 2][:, 0:TC],
                            in1=rsb[:, (i % 4) * TC:(i % 4) * TC + TC],
                            op=OP.add).then_inc(kb.sems["dve"], 1)
    kb.sec("vector", f_vector_d)
from contextlib import ExitStack


def build_program(mode, dumps=(), cc_mode="cc"):
    nc = bass.Bass("TRN2", target_bir_lowering=False, num_swdge_queues=4)
    io = {}

    def din(name, shape, dt):
        io[name] = nc.dram_tensor(name, shape, dt, kind="ExternalInput").ap()
        return io[name]

    def dout(name, shape, dt):
        io[name] = nc.dram_tensor(name, shape, dt, kind="ExternalOutput").ap()
        return io[name]

    def dint(name, shape, dt):
        io[name] = nc.dram_tensor(name, shape, dt, kind="Internal").ap()
        return io[name]

    din("c_ident", [128, 128], F32)
    din("c_ones_bf", [128, 1], BF16)
    din("c_ones_f32", [1, 128], F32)
    din("c_isq", [128, 1], F32)
    din("c_midbias", [128, 1], F32)
    din("c_kposB", [128, NKT], F32)
    din("c_qposB", [1, TOK], F32)
    din("c_rep", [8, 128], BF16)
    din("c_qw0", [128, 1], F32)
    din("c_qw1", [128, 1], F32)

    if mode == "launch1":
        din("x", [TOK, D], F32)
        dint("resid", [D, TOK], F32)
        din("wq_pm", [D, D], F32)
        din("wk_pm", [D, D], F32)
        din("wv_pm", [4 * 128, FC * 512], F32)
        din("bq_pm", [128, FC], F32)
        din("bk_pm", [128, FC], F32)
        din("bv_row", [1, D], F32)
        dout("qbuf", [D, TOK], BF16)
        dout("cin", [2 * D, TOK], BF16)
    elif mode == "launch2":
        din("x", [TOK, D], F32)
        din("qx", [128, H * TOK], BF16)
        din("kvfull", [2 * 2 * D, TOK], BF16)
        din("wo", [D, D], F32)
        din("bo_pm", [128, FC], F32)
        din("gate_pm", [D, IFF], I32)
        din("up_pm", [D, IFF], I32)
        din("down_pm", [IFF, D], I32)
        din("sg_pm", [128, 64], F32)
        din("su_pm", [128, 64], F32)
        din("sd_pm", [128, 16], F32)
        din("wq1_pm", [D, D], F32)
        din("wk1_pm", [128, D], F32)
        din("wv1_pm", [128, D], F32)
        din("bq1_pm", [128, FC], F32)
        din("bk1_pm", [128, 1], F32)
        din("bv1_row", [1, DH], F32)
        dout("resid", [D, TOK], F32)
        dout("q1buf", [D, TOK], BF16)
        dout("cin1", [2 * DH, TOK], BF16)
        dint("abuf", [D, TOK], BF16)
        dint("h_scr", [128, (IFF // 128) * TOK], BF16)
    elif mode == "launch3":
        din("resid2", [D, TOK], F32)
        din("q1buf", [D, TOK], BF16)
        din("kvfull1", [2 * 2 * DH, TOK], BF16)
        din("wo1", [D, D], F32)
        din("bo1_pm", [128, FC], F32)
        din("wt1_pm", [512, 8192], F32)
        din("g1_pm", [128, 128], F32)
        din("at_pm", [128, 2048], F32)
        din("bs_pm", [128, 2048], F32)
        dout("y", [TOK, D], F32)
        dint("resid", [D, TOK], F32)
        dint("abuf", [D, TOK], BF16)
    elif mode == "fused":
        din("x", [TOK, D], F32)
        din("wq_pm", [D, D], F32)
        din("wk_pm", [D, D], F32)
        din("wv_pm", [4 * 128, FC * 512], F32)
        din("bq_pm", [128, FC], F32)
        din("bk_pm", [128, FC], F32)
        din("bv_row", [1, D], F32)
        din("wo", [D, D], F32)
        din("bo_pm", [128, FC], F32)
        din("gate_pm", [D, IFF], I32)
        din("up_pm", [D, IFF], I32)
        din("down_pm", [IFF, D], I32)
        din("sg_pm", [128, 64], F32)
        din("su_pm", [128, 64], F32)
        din("sd_pm", [128, 16], F32)
        din("wq1_pm", [D, D], F32)
        din("wk1_pm", [128, D], F32)
        din("wv1_pm", [128, D], F32)
        din("bq1_pm", [128, FC], F32)
        din("bk1_pm", [128, 1], F32)
        din("bv1_row", [1, DH], F32)
        din("wo1", [D, D], F32)
        din("bo1_pm", [128, FC], F32)
        din("wt1_pm", [512, 8192], F32)
        din("g1_pm", [128, 128], F32)
        din("at_pm", [128, 2048], F32)
        din("bs_pm", [128, 2048], F32)
        dout("y", [TOK, D], F32)
        dint("resid", [D, TOK], F32)
        dint("qbuf", [D, TOK], BF16)
        dint("abuf", [D, TOK], BF16)
        dint("h_scr", [128, (IFF // 128) * TOK], BF16)
        io["cin"] = nc.dram_tensor("cin", [3 * D, TOK], BF16).ap()
        io["cout"] = nc.dram_tensor("cout", [2 * 3 * D, TOK], BF16).ap()
        io["cin1"] = nc.dram_tensor("cin1", [2 * DH, TOK], BF16).ap()
        io["cout1"] = nc.dram_tensor("cout1", [2 * 2 * DH, TOK], BF16).ap()
    else:
        raise ValueError(mode)

    with ExitStack() as stack:
        ec = stack.enter_context
        xhat = ec(nc.sbuf_tensor("xhat", [128, FC * TOK], BF16))
        AB1 = ec(nc.sbuf_tensor("AB1", [128, 32768], BF16))
        AB2 = ec(nc.sbuf_tensor("AB2", [128, 8192], BF16))
        AF1 = ec(nc.sbuf_tensor("AF1", [128, 4096], F32))
        AF2 = ec(nc.sbuf_tensor("AF2", [128, 4096], F32))
        PT = ec(nc.sbuf_tensor("PT", [128, 2560], BF16))
        QS2 = ec(nc.sbuf_tensor("QS2", [128, 8192], BF16))
        AI32 = QS2[:].bitcast(I32)
        ident = ec(nc.sbuf_tensor("ident", [128, 128], F32))
        ones_bf = ec(nc.sbuf_tensor("ones_bf", [128, 1], BF16))
        ones_f32 = ec(nc.sbuf_tensor("ones_f32", [1, 128], F32))
        isq_vec = ec(nc.sbuf_tensor("isq_vec", [128, 1], F32))
        midbias = ec(nc.sbuf_tensor("midbias", [128, 1], F32))
        kposB = ec(nc.sbuf_tensor("kposB", [128, NKT], F32))
        rep_sb = ec(nc.sbuf_tensor("rep_sb", [8, 128], BF16))
        qw0 = ec(nc.sbuf_tensor("qw0", [128, 1], F32))
        qw1 = ec(nc.sbuf_tensor("qw1", [128, 1], F32))
        bvec = ec(nc.sbuf_tensor("bvec", [128, 256], F32))
        brow2 = AF2[0:1, 0:2048]
        ps = [ec(nc.psum_tensor(f"ps{i}", [128, 512], F32)) for i in range(7)]
        sems = {k: ec(nc.semaphore(f"s_{k}")) for k in ALLSEMS}
        blk = ec(nc.Block())

        kb = KB(nc, blk, sems)
        consts = {"ones_bf": ones_bf, "ones_f32": ones_f32, "isq_vec": isq_vec,
                  "midbias": midbias, "kposB": kposB, "qposB": AF2[0:1, 2048:4096],
                  "qposB_dram": io["c_qposB"], "ident": ident,
                  "qw0": qw0, "qw1": qw1}

        CL = DS(kb, ("l0", "l1"))

        def f_consts(e):
            pairs = [(ident[:], io["c_ident"]), (ones_bf[:], io["c_ones_bf"]),
                     (ones_f32[:], io["c_ones_f32"]), (isq_vec[:], io["c_isq"]),
                     (midbias[:], io["c_midbias"]), (kposB[:], io["c_kposB"]),
                     (rep_sb[:], io["c_rep"]), (qw0[:], io["c_qw0"]),
                     (qw1[:], io["c_qw1"])]
            bv_map = [("bq_pm", 0, FC), ("bk_pm", 16, FC), ("bo_pm", 32, FC),
                      ("sg_pm", 48, 64), ("su_pm", 112, 64), ("sd_pm", 176, 16),
                      ("bq1_pm", 192, FC), ("bk1_pm", 208, 1), ("bo1_pm", 209, FC)]
            for nm, off, wdt_ in bv_map:
                if nm in io:
                    pairs.append((bvec[:, off:off + wdt_], io[nm]))
            for dst, srcap in pairs:
                CL.issue(e, out=dst, in_=srcap, allow_slow_non_contiguous=True)
        kb.sec("sync", f_consts)
        kb.barrier()

        sb_tp = (AF1[:, 0:4096], AF2[:, 0:4096])
        sb_ln = (AF1[:, 0:1024], AB2[:, 0:1024], AB2[:, 1024:2048],
                 AF2[0:1, 0:2048], AF2[0:1, 2048:4096], AF1[0:1, 2048:4096],
                 AF1[:, 1024:2048])
        sb_proj = (AB1[:, 24576:28672], AB1[:, 28672:32768])
        sb_at = (AB1[:, 0:8192], AB1[:, 8192:16384], AB2[:, 0:4096], PT[:, 0:1024],
                 AB1[:, 16384:32768], AF1[:, 0:1024], AB2[:, 4096:8192],
                 AF1[:, 1536:2048], AF1[:, 1024:1536], AF2[0:1, 0:TOK],
                 PT[:, 1024:2048], PT[:, 2048:2560])
        sb_atf = (AB1[:, 0:8192], AB1[:, 8192:16384], QS2[:, 0:8192], PT[:, 0:1024],
                  AB1[:, 16384:32768], AF1[:, 0:1024], AB2[:, 4096:8192],
                  AF1[:, 1536:2048], AF1[:, 1024:1536], AF2[0:1, 0:TOK],
                  PT[:, 1024:2048], PT[:, 2048:2560])
        sb_wo = (AB1[:, 0:16384], AB1[:, 16384:32768], AF1[:, 0:2048], AF1[:, 2048:4096])

        def load_bias(pairs):
            BL = DS(kb, ("l0", "l1"))

            def f_b(e):
                for dst, srcap in pairs:
                    BL.issue(e, out=dst, in_=srcap)
            kb.sec("sync", f_b)
            kb.barrier()

        if mode == "launch1":
            ph_transpose_td(kb, io["x"], io["resid"], sb_tp, ps, NTT, ident)
            kb.barrier()
            ph_ln_v2(kb, io["resid"], xhat, sb_ln, ps, consts)
            kb.barrier()
            load_bias([(bvec[:, 0:FC], io["bq_pm"]), (bvec[:, FC:2 * FC], io["bk_pm"]),
                       (brow2[0:1, 0:D], io["bv_row"])])
            ph_proj_fm(kb, xhat, io["wq_pm"], bvec[:, 0:FC], FC, io["qbuf"], 0, sb_proj, ps)
            kb.barrier()
            ph_proj_fm(kb, xhat, io["wk_pm"], bvec[:, FC:2 * FC], FC, io["cin"], 0, sb_proj, ps)
            kb.barrier()
            sb_v = (AB1[:, 0:16384], AB1[:, 16384:24576], AF1[:, 0:512])
            ph_proj_v(kb, xhat, io["wv_pm"], brow2[0:1, 0:D], 4, 512, io["cin"], D, sb_v, ps, consts)
            kb.barrier()
        elif mode == "launch2":
            ph_transpose_td(kb, io["x"], io["resid"], sb_tp, ps, NTT, ident)
            kb.barrier()
            ph_attn(kb, 0, io["qx"], io["kvfull"], io["abuf"], sb_at, ps, consts)
            kb.barrier()
            load_bias([(bvec[:, 0:FC], io["bo_pm"])])
            ph_wo(kb, io["abuf"], io["wo"], bvec[:, 0:FC], io["resid"], sb_wo, ps)
            kb.barrier()
            if "resid_mid" in dumps:
                io["resid_mid"] = nc.dram_tensor("resid_mid_d", [D, TOK], F32,
                                                 kind="ExternalOutput").ap()

                DM = DS(kb, ("g0",))

                def f_dm(e):
                    DM.issue(e, out=io["resid_mid"], in_=io["resid"])
                kb.sec("gpsimd", f_dm)
                kb.barrier()
            ph_ln_v2(kb, io["resid"], xhat, sb_ln, ps, consts)
            kb.barrier()
            load_bias([(bvec[:, 0:64], io["sg_pm"]), (bvec[:, 64:128], io["su_pm"]),
                       (bvec[:, 128:144], io["sd_pm"])])
            sb_fa = (AI32[:, 0:4096], AB1[:, 0:32768],
                     AB2[:, 0:1024], AB2[:, 1024:2048], AB2[:, 2048:6144])
            ph_ffn0_a(kb, xhat, io["gate_pm"], io["up_pm"], io["h_scr"],
                      bvec[:, 0:64], bvec[:, 64:128], sb_fa, ps)
            kb.barrier()
            sb_fb = (AB1[:, 0:32768], xhat[:, 0:32768], AB2[:, 0:8192],
                     AF1[:, 0:2048], AF1[:, 2048:4096], AF2[:, 1024:1536])
            consts["hsr"] = AF2[0:1, 0:512]
            ph_ffn0_b(kb, io["h_scr"], io["down_pm"], bvec[:, 128:144], io["resid"], sb_fb, ps, consts)
            kb.barrier()
            ph_ln_v2(kb, io["resid"], xhat, sb_ln, ps, consts)
            kb.barrier()
            load_bias([(bvec[:, 0:FC], io["bq1_pm"]), (bvec[:, FC:FC + 1], io["bk1_pm"]),
                       (brow2[0:1, 0:DH], io["bv1_row"])])
            ph_proj_fm(kb, xhat, io["wq1_pm"], bvec[:, 0:FC], FC, io["q1buf"], 0, sb_proj, ps)
            kb.barrier()
            ph_proj_fm(kb, xhat, io["wk1_pm"], bvec[:, FC:FC + 1], 1, io["cin1"], 0, sb_proj, ps)
            kb.barrier()
            sb_v = (AB1[:, 0:16384], AB1[:, 16384:24576], AF1[:, 0:512])
            ph_proj_v(kb, xhat, io["wv1_pm"], brow2[0:1, 0:DH], 1, 128, io["cin1"], DH, sb_v, ps, consts)
            kb.barrier()
        elif mode == "launch3":
            CP = DS(kb, ("g0", "g1"))

            def f_cp(e):
                for fc in range(FC):
                    CP.issue(e, out=io["resid"][fc * 128:(fc + 1) * 128, :],
                             in_=io["resid2"][fc * 128:(fc + 1) * 128, :])
            kb.sec("gpsimd", f_cp)
            kb.barrier()
            ph_attn(kb, 1, io["q1buf"], io["kvfull1"], io["abuf"], sb_at, ps, consts)
            kb.barrier()
            load_bias([(bvec[:, 0:FC], io["bo1_pm"])])
            ph_wo(kb, io["abuf"], io["wo1"], bvec[:, 0:FC], io["resid"], sb_wo, ps)
            kb.barrier()
            ph_ln_v2(kb, io["resid"], xhat, sb_ln, ps, consts)
            kb.barrier()
            sb_f1 = (AB1[:, 0:16384], AB1[:, 16384:18432], AB1[:, 18432:20480],
                     AB1[:, 24576:24704], AB2[0:8, 0:2048], AB1[:, 20480:22528],
                     AB1[:, 22528:24576], AF1[:, 0:2048], AF1[:, 2048:4096],
                     (AF2[:, 0:8], AF2[:, 16:17], AF2[:, 17:18], AF2[:, 18:19],
                      AF2[:, 19:20], AF2[:, 20:21], AF2[:, 21:22], AF2[:, 24:32],
                      AF2[:, 32:40], AF2[:, 40:48], AF2[:, 48:56], AF2[:, 56:64],
                      AF2[:, 64:192]))
            ph_ffn1(kb, xhat, io["wt1_pm"], io["g1_pm"], io["at_pm"], io["bs_pm"],
                    rep_sb, io["resid"], sb_f1, ps, consts)
            kb.barrier()
            ph_transpose_td(kb, io["resid"], io["y"], sb_tp, ps, FC, ident)
            kb.barrier()

        if mode == "fused":
            RG = [[0, 1], [2, 3], [4, 5], [6, 7]]

            def ph_ag(cin_ap, cout_ap, which):
                def f_ag(e):
                    if cc_mode == "cc" or cc_mode == which or cc_mode.startswith("prefix"):
                        e.collective_compute("AllGather", OP.bypass, replica_groups=RG,
                                             ins=[cin_ap.opt()], outs=[cout_ap.opt()]
                                             ).then_inc(kb.sems["cc"], 1)
                        kb.inc("cc", 1)
                    else:
                        nrow = cin_ap.shape[0]
                        for rr in range(2):
                            e.dma_start(out=cout_ap[rr * nrow:(rr + 1) * nrow, :],
                                        in_=cin_ap[:, :]).then_inc(kb.sems["cc"], 16)
                            kb.inc("cc", 16)
                kb.sec("gpsimd", f_ag)
                kb.barrier()

            if cc_mode != "prefix0":
                ph_transpose_td(kb, io["x"], io["resid"], sb_tp, ps, NTT, ident)
                kb.barrier()
                ph_ln_v2(kb, io["resid"], xhat, sb_ln, ps, consts)
                kb.barrier()
                load_bias([(brow2[0:1, 0:D], io["bv_row"])])
                ph_proj_fm(kb, xhat, io["wq_pm"], bvec[:, 0:FC], FC, io["cin"], 2 * D, sb_proj, ps)
                kb.barrier()
                ph_proj_fm(kb, xhat, io["wk_pm"], bvec[:, 16:32], FC, io["cin"], 0, sb_proj, ps)
                kb.barrier()
                sb_v = (AB1[:, 0:16384], AB1[:, 16384:24576], AF1[:, 0:512])
                ph_proj_v(kb, xhat, io["wv_pm"], brow2[0:1, 0:D], 4, 512, io["cin"], D, sb_v, ps, consts)
                kb.barrier()
            ph_ag(io["cin"], io["cout"], "ag0")
            if cc_mode.startswith("prefix"):
                ph_transpose_td(kb, io["resid"], io["y"], sb_tp, ps, FC, ident)
                kb.barrier()
                fused_rest = False
            else:
                fused_rest = True
            if fused_rest and True:
                ph_attn(kb, 0, io["cout"], io["cout"], io["abuf"], sb_atf, ps, consts, qmode="fused")
                kb.barrier()
            if fused_rest:
                ph_wo(kb, io["abuf"], io["wo"], bvec[:, 32:48], io["resid"], sb_wo, ps)
                kb.barrier()
                ph_ln_v2(kb, io["resid"], xhat, sb_ln, ps, consts)
                kb.barrier()
                sb_fa = (AI32[:, 0:4096], AB1[:, 0:32768],
                         AB2[:, 0:1024], AB2[:, 1024:2048], AB2[:, 2048:6144])
                ph_ffn0_a(kb, xhat, io["gate_pm"], io["up_pm"], io["h_scr"],
                          bvec[:, 48:112], bvec[:, 112:176], sb_fa, ps)
                kb.barrier()
                sb_fb = (AB1[:, 0:32768], xhat[:, 0:32768], AB2[:, 0:8192],
                         AF1[:, 0:2048], AF1[:, 2048:4096], AF2[:, 1024:1536])
                consts["hsr"] = AF2[0:1, 0:512]
                ph_ffn0_b(kb, io["h_scr"], io["down_pm"], bvec[:, 176:192], io["resid"], sb_fb, ps, consts)
                kb.barrier()
                ph_ln_v2(kb, io["resid"], xhat, sb_ln, ps, consts)
                kb.barrier()
                load_bias([(brow2[0:1, 0:DH], io["bv1_row"])])
                ph_proj_fm(kb, xhat, io["wq1_pm"], bvec[:, 192:208], FC, io["qbuf"], 0, sb_proj, ps)
                kb.barrier()
                ph_proj_fm(kb, xhat, io["wk1_pm"], bvec[:, 208:209], 1, io["cin1"], 0, sb_proj, ps)
                kb.barrier()
                sb_v = (AB1[:, 0:16384], AB1[:, 16384:24576], AF1[:, 0:512])
                ph_proj_v(kb, xhat, io["wv1_pm"], brow2[0:1, 0:DH], 1, 128, io["cin1"], DH, sb_v, ps, consts)
                kb.barrier()
                ph_ag(io["cin1"], io["cout1"], "ag1")
                ph_attn(kb, 1, io["qbuf"], io["cout1"], io["abuf"], sb_at, ps, consts)
                kb.barrier()
                ph_wo(kb, io["abuf"], io["wo1"], bvec[:, 209:225], io["resid"], sb_wo, ps)
                kb.barrier()
                ph_ln_v2(kb, io["resid"], xhat, sb_ln, ps, consts)
                kb.barrier()
                sb_f1 = (AB1[:, 0:16384], AB1[:, 16384:18432], AB1[:, 18432:20480],
                         AB1[:, 24576:24704], AB2[0:8, 0:2048], AB1[:, 20480:22528],
                         AB1[:, 22528:24576], AF1[:, 0:2048], AF1[:, 2048:4096],
                         (AF2[:, 0:8], AF2[:, 16:17], AF2[:, 17:18], AF2[:, 18:19],
                          AF2[:, 19:20], AF2[:, 20:21], AF2[:, 21:22], AF2[:, 24:32],
                          AF2[:, 32:40], AF2[:, 40:48], AF2[:, 48:56], AF2[:, 56:64],
                          AF2[:, 64:192]))
                ph_ffn1(kb, xhat, io["wt1_pm"], io["g1_pm"], io["at_pm"], io["bs_pm"],
                        rep_sb, io["resid"], sb_f1, ps, consts)
                kb.barrier()
                ph_transpose_td(kb, io["resid"], io["y"], sb_tp, ps, FC, ident)
                kb.barrier()

        for name in dumps:
            if name == 'resid_mid':
                continue
            src = io[name]
            dst = dout(name + "_d", list(src.tensor.shape), src.tensor.dtype)

            DD = DS(kb, ("g0",))

            def f_dump(e, src=src, dst=dst, DD=DD):
                DD.issue(e, out=dst, in_=src)
            kb.sec("gpsimd", f_dump)
            kb.barrier()

        def f_final(e):
            for nm in DMASEMS:
                if kb.cnt[nm] > 0:
                    e.wait_ge(kb.sems[nm], kb.cnt[nm])
        kb.sec("gpsimd", f_final)

    return nc


def pm4(w, n_oc, ocw):
    return np.ascontiguousarray(
        w.reshape(FC, 128, n_oc, ocw).transpose(2, 1, 0, 3).reshape(n_oc * 128, FC * ocw))


def make_consts(core):
    hlf = core % 2
    qpos = (np.arange(TOK, dtype=np.float64) + hlf * TOK) * 1e6
    kpos = (np.arange(SEQ, dtype=np.float64)) * 1e6
    rep = np.zeros((8, 128), np.float32)
    for ee in range(8):
        rep[ee, ee * 16:(ee + 1) * 16] = 16.0
    return {
        "c_ident": np.eye(128, dtype=np.float32),
        "c_ones_bf": np.ones((128, 1), dtype=BF16NP),
        "c_ones_f32": np.ones((1, 128), dtype=np.float32),
        "c_isq": np.full((128, 1), ISQ, np.float32),
        "c_midbias": np.full((128, 1), 0.0 if hlf == 1 else NEG, np.float32),
        "c_kposB": kpos.reshape(NKT, 128).T.astype(np.float32).copy(),
        "c_qposB": qpos.reshape(1, TOK).astype(np.float32),
        "c_rep": rep.astype(BF16NP),
        "c_qw0": np.full((128, 1), 1.0 if hlf == 0 else 0.0, np.float32),
        "c_qw1": np.full((128, 1), 1.0 if hlf == 1 else 0.0, np.float32),
    }


def host_weights(ins):
    w = {}
    w["wq_pm"] = pm4(ins["L0_attn_Wq"], 16, 128)
    w["wk_pm"] = pm4(ins["L0_attn_Wk"], 16, 128)
    w["wv_pm"] = pm4(ins["L0_attn_Wv"], 4, 512)
    w["bq_pm"] = np.ascontiguousarray(ins["L0_attn_bq"].reshape(16, 128).T)
    w["bk_pm"] = np.ascontiguousarray(ins["L0_attn_bk"].reshape(16, 128).T)
    w["bv_row"] = ins["L0_attn_bv"].reshape(1, D)
    w["wo"] = ins["L0_attn_Wo"]
    w["bo_pm"] = np.ascontiguousarray(ins["L0_attn_bo"].reshape(16, 128).T)
    w["gate_pm"] = np.ascontiguousarray(
        ins["L0_ffn_gate_w"].reshape(16, 512, 16, 128).transpose(0, 3, 2, 1).reshape(2048, 8192))
    w["up_pm"] = np.ascontiguousarray(
        ins["L0_ffn_up_w"].reshape(16, 512, 16, 128).transpose(0, 3, 2, 1).reshape(2048, 8192))
    w["down_pm"] = np.ascontiguousarray(
        ins["L0_ffn_down_w"].reshape(16, 128, 64, 128).transpose(2, 3, 0, 1).reshape(8192, 2048))
    w["sg_pm"] = np.ascontiguousarray(ins["L0_ffn_gate_scale"].reshape(64, 128).T)
    w["su_pm"] = np.ascontiguousarray(ins["L0_ffn_up_scale"].reshape(64, 128).T)
    w["sd_pm"] = np.ascontiguousarray(ins["L0_ffn_down_scale"].reshape(16, 128).T)
    w["wq1_pm"] = pm4(ins["L1_attn_Wq"], 16, 128)
    w["wk1_pm"] = pm4(ins["L1_attn_Wk"], 1, 128)
    w["wv1_pm"] = pm4(ins["L1_attn_Wv"], 1, 128)
    w["bq1_pm"] = np.ascontiguousarray(ins["L1_attn_bq"].reshape(16, 128).T)
    w["bk1_pm"] = np.ascontiguousarray(ins["L1_attn_bk"].reshape(1, 128).T)
    w["bv1_row"] = ins["L1_attn_bv"].reshape(1, DH)
    w["wo1"] = ins["L1_attn_Wo"]
    w["bo1_pm"] = np.ascontiguousarray(ins["L1_attn_bo"].reshape(16, 128).T)
    w["wt1_pm"] = np.ascontiguousarray(
        ins["L1_ffn_weight"].reshape(4, 512, 16, 128).transpose(0, 3, 2, 1).reshape(512, 8192))
    w["g1_pm"] = np.ascontiguousarray(
        ins["L1_ffn_gate_w"].reshape(8, 16, 128).transpose(2, 1, 0).reshape(128, 128))
    w["at_pm"] = np.ascontiguousarray(
        ins["L1_ffn_A_w"].reshape(128, 16, 128).transpose(2, 1, 0).reshape(128, 2048))
    w["bs_pm"] = np.ascontiguousarray(
        ins["L1_ffn_B_w"].transpose(0, 2, 1).reshape(128, 2048))
    return {k: (v.astype(np.float32) if v.dtype == np.float64 else v) for k, v in w.items()}


_BUILD_CACHE = {}


def _get_prog(mode):
    if mode not in _BUILD_CACHE:
        _BUILD_CACHE[mode] = build_program(mode)
    return _BUILD_CACHE[mode]


def kernel(**inputs):
    x = np.ascontiguousarray(inputs["L0_attn_x"]).astype(np.float32, copy=False)
    w = host_weights(inputs)
    nc = _get_prog("fused")
    wkeys = ("wq_pm", "wk_pm", "wv_pm", "bq_pm", "bk_pm", "bv_row",
             "wo", "bo_pm", "gate_pm", "up_pm", "down_pm", "sg_pm", "su_pm", "sd_pm",
             "wq1_pm", "wk1_pm", "wv1_pm", "bq1_pm", "bk1_pm", "bv1_row",
             "wo1", "bo1_pm", "wt1_pm", "g1_pm", "at_pm", "bs_pm")
    maps = []
    for c in range(8):
        b, hlf = c // 2, c % 2
        m = {"x": np.ascontiguousarray(x[b, hlf * TOK:(hlf + 1) * TOK, :])}
        m.update({k: w[k] for k in wkeys})
        m.update(make_consts(c))
        maps.append(m)
    res = run_bass_kernel_spmd(nc, maps, core_ids=list(range(8)))
    y = np.zeros((4, SEQ, D), np.float32)
    for c in range(8):
        b, hlf = c // 2, c % 2
        y[b, hlf * TOK:(hlf + 1) * TOK, :] = res.results[c]["y"]
    return y


# version 07
# speedup vs baseline: 1.1932x; 1.0141x over previous
import numpy as np
import ml_dtypes
import concourse.bass as bass
import concourse.mybir as mybir
from concourse.bass_utils import run_bass_kernel_spmd

F32 = mybir.dt.float32
BF16 = mybir.dt.bfloat16
I32 = mybir.dt.int32
AF = mybir.ActivationFunctionType
OP = mybir.AluOpType
AX = mybir.AxisListType.X
BF16NP = ml_dtypes.bfloat16

D = 2048
FC = 16
TOK = 2048
SEQ = 4096
H = 16
DH = 128
IFF = 8192
TC = 512
NTC = TOK // TC
NTT = TOK // 128
NKT = SEQ // 128
E = 8
R = 16
EPS = 1e-5
ISQ = float(1.0 / np.sqrt(128.0))
NEG = -1.0e9

ENGS = ("sync", "tensor", "vector", "scalar", "gpsimd")
LSEMS = ("l0", "l1", "l2", "l3", "l4", "l5", "l6", "l7")
GSEMS = ("g0", "g1", "g2", "g3", "g4", "g5")
ALLSEMS = LSEMS + GSEMS + ("pe", "dve", "act", "cc")
DMASEMS = LSEMS + GSEMS + ("cc",)


class KB:

    def __init__(self, nc, blk, sems):
        self.nc = nc
        self.blk = blk
        self.sems = sems
        self.cnt = {k: 0 for k in sems}
        self.snap = {k: 0 for k in sems}
        self.waited = {e: {k: 0 for k in sems} for e in ENGS}

    def inc(self, name, k=1):
        self.cnt[name] += k
        return self.cnt[name]

    def barrier(self):
        self.snap = dict(self.cnt)

    def sec(self, eng, f):
        kb = self

        def wrapped(e):
            for k, v in kb.snap.items():
                if v > 0 and kb.waited[eng][k] < v:
                    e.wait_ge(kb.sems[k], v)
                    kb.waited[eng][k] = v
            f(e)

        getattr(self.blk, eng)(wrapped)


class DS:

    def __init__(self, kb, names):
        self.kb = kb
        self.names = tuple(names)
        self.k = 0
        self.base = {nm: kb.cnt[nm] for nm in self.names}

    def token(self, k):
        nm = self.names[k % len(self.names)]
        return (nm, self.base[nm] + 16 * (k // len(self.names) + 1))

    def issue(self, e, out=None, in_=None, **kw):
        nm, v = self.token(self.k)
        self.k += 1
        e.dma_start(out=out, in_=in_, **kw).then_inc(self.kb.sems[nm], 16)
        got = self.kb.inc(nm, 16)
        assert got == v, (nm, got, v)
        return (nm, v)

    def w(self, e, k):
        nm, v = self.token(k)
        e.wait_ge(self.kb.sems[nm], v)


def ph_transpose_td(kb, src, dst, sb, ps, nt_src, ident):
    nf = src.shape[1] // 128
    xin, tout = sb
    W = nf * 128
    LD = DS(kb, ("l0", "l1"))
    ST = DS(kb, ("g0", "g1"))
    pe0 = kb.cnt["pe"]
    dv0 = kb.cnt["dve"]

    def f_sync(e):
        for tt in range(nt_src):
            if tt >= 2:
                e.wait_ge(kb.sems["pe"], pe0 + nf * (tt - 1))
            LD.issue(e, out=xin[:, (tt % 2) * W:(tt % 2) * W + W],
                     in_=src[tt * 128:(tt + 1) * 128, :])
    kb.sec("sync", f_sync)

    def f_tensor(e):
        for tt in range(nt_src):
            LD.w(e, tt)
            for f in range(nf):
                k = tt * nf + f
                if k >= 2:
                    e.wait_ge(kb.sems["dve"], dv0 + (k - 1))
                e.transpose(out=ps[k % 2][:, 0:128],
                            in_=xin[:, (tt % 2) * W + f * 128:(tt % 2) * W + f * 128 + 128],
                            identity=ident[:]).then_inc(kb.sems["pe"], 1)
                kb.inc("pe")
    kb.sec("tensor", f_tensor)

    def f_vector(e):
        for tt in range(nt_src):
            if tt >= 2:
                ST.w(e, tt - 2)
            for f in range(nf):
                k = tt * nf + f
                e.wait_ge(kb.sems["pe"], pe0 + k + 1)
                e.tensor_copy(out=tout[:, (tt % 2) * W + f * 128:(tt % 2) * W + f * 128 + 128],
                              in_=ps[k % 2][:, 0:128]).then_inc(kb.sems["dve"], 1)
                kb.inc("dve")
    kb.sec("vector", f_vector)

    def f_gp(e):
        for tt in range(nt_src):
            e.wait_ge(kb.sems["dve"], dv0 + nf * (tt + 1))
            ST.issue(e, out=dst.rearrange("(f p) t -> p f t", p=128)[:, :, tt * 128:(tt + 1) * 128],
                     in_=tout[:, (tt % 2) * W:(tt % 2) * W + W].rearrange("p (f t) -> p f t", f=nf))
    kb.sec("gpsimd", f_gp)


def ph_ln_v2(kb, resid, xhat, sb, ps, consts):
    rfa, rb16, sq16, rowA, rowB, rowT, abc = sb
    ones_bf, ones_f32 = consts["ones_bf"], consts["ones_f32"]
    LD = DS(kb, ("l0", "l1"))
    ac0 = kb.cnt["act"]
    pe0 = kb.cnt["pe"]
    dv0 = kb.cnt["dve"]

    def f_sync(e):
        for tc in range(NTC):
            for fc in range(FC):
                k = tc * FC + fc
                if k >= 2:
                    e.wait_ge(kb.sems["act"], ac0 + 2 * (k - 1))
                LD.issue(e, out=rfa[:, (k % 2) * TC:(k % 2) * TC + TC],
                         in_=resid[fc * 128:(fc + 1) * 128, tc * TC:tc * TC + TC])
    kb.sec("sync", f_sync)

    def f_scalar(e):
        for tc in range(NTC):
            for fc in range(FC):
                k = tc * FC + fc
                LD.w(e, k)
                if k >= 2:
                    e.wait_ge(kb.sems["pe"], pe0 + 2 * (k - 1))
                e.activation(out=rb16[:, (k % 2) * TC:(k % 2) * TC + TC],
                             in_=rfa[:, (k % 2) * TC:(k % 2) * TC + TC],
                             func=AF.Copy).then_inc(kb.sems["act"], 1)
                e.activation(out=sq16[:, (k % 2) * TC:(k % 2) * TC + TC],
                             in_=rfa[:, (k % 2) * TC:(k % 2) * TC + TC],
                             func=AF.Square).then_inc(kb.sems["act"], 1)
                kb.inc("act", 2)
    kb.sec("scalar", f_scalar)

    def f_tensor(e):
        for tc in range(NTC):
            if tc >= 1:
                e.wait_ge(kb.sems["dve"], dv0 + 2 * tc)
            for fc in range(FC):
                k = tc * FC + fc
                e.wait_ge(kb.sems["act"], ac0 + 2 * (k + 1))
                st = (fc == 0)
                sp = (fc == FC - 1)
                e.matmul(out=ps[5][0:1, 0:TC], lhsT=ones_bf[:, 0:1],
                         rhs=rb16[:, (k % 2) * TC:(k % 2) * TC + TC],
                         start=st, stop=sp).then_inc(kb.sems["pe"], 1)
                kb.inc("pe")
                e.matmul(out=ps[6][0:1, 0:TC], lhsT=ones_bf[:, 0:1],
                         rhs=sq16[:, (k % 2) * TC:(k % 2) * TC + TC],
                         start=st, stop=sp).then_inc(kb.sems["pe"], 1)
                kb.inc("pe")
    kb.sec("tensor", f_tensor)

    def f_vector(e):
        for tc in range(NTC):
            e.wait_ge(kb.sems["pe"], pe0 + 2 * FC * (tc + 1))
            e.tensor_copy(out=rowA[0:1, tc * TC:tc * TC + TC],
                          in_=ps[5][0:1, 0:TC]).then_inc(kb.sems["dve"], 1)
            e.tensor_copy(out=rowB[0:1, tc * TC:tc * TC + TC],
                          in_=ps[6][0:1, 0:TC]).then_inc(kb.sems["dve"], 1)
            kb.inc("dve", 2)
    kb.sec("vector", f_vector)
    dv1 = kb.cnt["dve"]

    def f_vector2(e):
        def sw(inst):
            inst.then_inc(kb.sems["dve"], 1)
            e.wait_ge(kb.sems["dve"], kb.inc("dve"))
        e.wait_ge(kb.sems["dve"], dv1)
        sw(e.tensor_scalar(out=rowA[0:1, :], in0=rowA[0:1, :], scalar1=1.0 / D,
                           scalar2=None, op0=OP.mult))
        sw(e.tensor_scalar(out=rowB[0:1, :], in0=rowB[0:1, :], scalar1=1.0 / D,
                           scalar2=None, op0=OP.mult))
        sw(e.tensor_tensor(out=rowT[0:1, :], in0=rowA[0:1, :], in1=rowA[0:1, :],
                           op=OP.mult))
        sw(e.tensor_tensor(out=rowB[0:1, :], in0=rowB[0:1, :], in1=rowT[0:1, :],
                           op=OP.subtract))
        e.tensor_scalar(out=rowB[0:1, :], in0=rowB[0:1, :], scalar1=EPS,
                        scalar2=None, op0=OP.add).then_inc(kb.sems["dve"], 1)
        kb.inc("dve")
    kb.sec("vector", f_vector2)
    dv2 = kb.cnt["dve"]

    def f_scalar2(e):
        e.wait_ge(kb.sems["dve"], dv2)
        e.activation(out=rowT[0:1, :], in_=rowB[0:1, :],
                     func=AF.Sqrt).then_inc(kb.sems["act"], 1)
        kb.inc("act")
    kb.sec("scalar", f_scalar2)
    ac2 = kb.cnt["act"]

    def f_vector3(e):
        def sw(inst):
            inst.then_inc(kb.sems["dve"], 1)
            e.wait_ge(kb.sems["dve"], kb.inc("dve"))
        e.wait_ge(kb.sems["act"], ac2)
        sw(e.reciprocal(out=rowB[0:1, :], in_=rowT[0:1, :]))
        sw(e.tensor_tensor(out=rowA[0:1, :], in0=rowA[0:1, :], in1=rowB[0:1, :],
                           op=OP.mult))
        e.tensor_scalar(out=rowA[0:1, :], in0=rowA[0:1, :], scalar1=-1.0, scalar2=None,
                        op0=OP.mult).then_inc(kb.sems["dve"], 1)
        kb.inc("dve")
    kb.sec("vector", f_vector3)
    dv3 = kb.cnt["dve"]

    pe1 = kb.cnt["pe"]
    sched = []
    carry = None
    for tc in range(NTC):
        sched.append(("copyA", tc))
        if carry is not None:
            sched.append(("add", carry))
            carry = None
        sched.append(("copyB", tc))
        prev = None
        for fc in range(FC):
            k = tc * FC + fc
            sched.append(("mult", k))
            if prev is not None:
                sched.append(("add", prev))
            prev = k
        carry = prev
    sched.append(("add", carry))
    add_tick = {}
    copyB_tick = {}
    t = dv3
    for op, arg in sched:
        t += 1
        if op == "add":
            add_tick[arg] = t
        elif op == "copyB":
            copyB_tick[arg] = t

    def f_tensor2(e):
        e.wait_ge(kb.sems["dve"], dv3)
        for tc in range(NTC):
            if tc >= 1:
                e.wait_ge(kb.sems["dve"], copyB_tick[tc - 1])
            e.matmul(out=ps[5][:, 0:TC], lhsT=ones_f32[0:1, :],
                     rhs=rowB[0:1, tc * TC:tc * TC + TC], start=True, stop=True
                     ).then_inc(kb.sems["pe"], 1)
            kb.inc("pe")
            e.matmul(out=ps[6][:, 0:TC], lhsT=ones_f32[0:1, :],
                     rhs=rowA[0:1, tc * TC:tc * TC + TC], start=True, stop=True
                     ).then_inc(kb.sems["pe"], 1)
            kb.inc("pe")
    kb.sec("tensor", f_tensor2)

    def f_sync2(e):
        for tc in range(NTC):
            for fc in range(FC):
                k = tc * FC + fc
                if k >= 2:
                    e.wait_ge(kb.sems["dve"], add_tick[k - 2])
                elif k == 0:
                    e.wait_ge(kb.sems["dve"], dv3)
                LD.issue(e, out=rfa[:, (k % 2) * TC:(k % 2) * TC + TC],
                         in_=resid[fc * 128:(fc + 1) * 128, tc * TC:tc * TC + TC])
    kb.sec("sync", f_sync2)

    def f_vector4(e):
        for op, arg in sched:
            if op == "copyA":
                e.wait_ge(kb.sems["pe"], pe1 + 2 * (arg + 1))
                e.tensor_copy(out=abc[:, 0:TC], in_=ps[5][:, 0:TC]).then_inc(kb.sems["dve"], 1)
            elif op == "copyB":
                e.tensor_copy(out=abc[:, TC:2 * TC], in_=ps[6][:, 0:TC]).then_inc(kb.sems["dve"], 1)
            elif op == "mult":
                k = arg
                LD.w(e, 64 + k)
                e.tensor_tensor(out=rfa[:, (k % 2) * TC:(k % 2) * TC + TC],
                                in0=rfa[:, (k % 2) * TC:(k % 2) * TC + TC],
                                in1=abc[:, 0:TC], op=OP.mult).then_inc(kb.sems["dve"], 1)
            else:
                k = arg
                tc, fc = k // FC, k % FC
                e.tensor_tensor(out=xhat[:, fc * TOK + tc * TC:fc * TOK + tc * TC + TC],
                                in0=rfa[:, (k % 2) * TC:(k % 2) * TC + TC],
                                in1=abc[:, TC:2 * TC], op=OP.add).then_inc(kb.sems["dve"], 1)
            kb.inc("dve")
    kb.sec("vector", f_vector4)


def ph_proj_fm(kb, xhat, w_pm, bias_pm, n_oc, dest, dest_row0, sb, ps):
    wbf, stage = sb
    WL = DS(kb, ("g0", "g1"))
    SO = DS(kb, ("g2", "g3"))
    pe0 = kb.cnt["pe"]
    dv0 = kb.cnt["dve"]

    def mm_tick(oc, tc):
        return pe0 + oc * NTC + tc + 1
    kb.cnt["pe"] = pe0 + n_oc * NTC

    def ev_tick(oc, tc):
        return dv0 + oc * NTC + tc + 1
    kb.cnt["dve"] = dv0 + n_oc * NTC

    if n_oc == 1:
        ops = [("ld", 0), ("st", 0)]
    else:
        ops = [("ld", 0), ("ld", 1)]
        for oc in range(2, n_oc):
            ops.append(("st", oc - 2))
            ops.append(("ld", oc))
        ops += [("st", n_oc - 2), ("st", n_oc - 1)]

    def f_gp(e):
        for kind, oc in ops:
            if kind == "ld":
                if oc >= 2:
                    e.wait_ge(kb.sems["pe"], mm_tick(oc - 2, NTC - 1))
                WL.issue(e, out=wbf[:, (oc % 2) * FC * 128:(oc % 2) * FC * 128 + FC * 128],
                         in_=w_pm[oc * 128:(oc + 1) * 128, :])
            else:
                e.wait_ge(kb.sems["dve"], ev_tick(oc, NTC - 1))
                SO.issue(e, out=dest[dest_row0 + oc * 128:dest_row0 + (oc + 1) * 128, :],
                         in_=stage[:, (oc % 2) * TOK:(oc % 2) * TOK + TOK])
    kb.sec("gpsimd", f_gp)

    def f_tensor(e):
        for oc in range(n_oc):
            WL.w(e, oc)
            for tc in range(NTC):
                gidx = oc * NTC + tc
                if gidx >= 2:
                    e.wait_ge(kb.sems["dve"], ev_tick((gidx - 2) // NTC, (gidx - 2) % NTC))
                for fc in range(FC):
                    ins = e.matmul(out=ps[gidx % 2][:, 0:TC],
                                   lhsT=wbf[:, (oc % 2) * FC * 128 + fc * 128:(oc % 2) * FC * 128 + fc * 128 + 128],
                                   rhs=xhat[:, fc * TOK + tc * TC:fc * TOK + tc * TC + TC],
                                   start=(fc == 0), stop=(fc == FC - 1))
                ins.then_inc(kb.sems["pe"], 1)
    kb.sec("tensor", f_tensor)

    def f_vector(e):
        for oc in range(n_oc):
            for tc in range(NTC):
                gidx = oc * NTC + tc
                e.wait_ge(kb.sems["pe"], mm_tick(oc, tc))
                if oc >= 2 and tc == 0:
                    SO.w(e, oc - 2)
                if bias_pm is not None:
                    e.tensor_scalar(out=stage[:, (oc % 2) * TOK + tc * TC:(oc % 2) * TOK + tc * TC + TC],
                                    in0=ps[gidx % 2][:, 0:TC],
                                    scalar1=bias_pm[:, oc:oc + 1], scalar2=None,
                                    op0=OP.add).then_inc(kb.sems["dve"], 1)
                else:
                    e.tensor_copy(out=stage[:, (oc % 2) * TOK + tc * TC:(oc % 2) * TOK + tc * TC + TC],
                                  in_=ps[gidx % 2][:, 0:TC]).then_inc(kb.sems["dve"], 1)
    kb.sec("vector", f_vector)


def ph_proj_v(kb, xhat, wv_pm, bv_row, n_vg, VG, dest, dest_row0, sb, ps, consts):
    wbf, vgst, bvb = sb
    ones_f32 = consts["ones_f32"]
    HPG = VG // 128
    WL = DS(kb, ("g0", "g1"))
    SO = DS(kb, ("g2", "g3"))
    pe0 = kb.cnt["pe"]
    dv0 = kb.cnt["dve"]

    def bc_tick(vg):
        return pe0 + vg * (NTT + 1) + 1

    def mm_tick(vg, tt):
        return pe0 + vg * (NTT + 1) + 1 + tt + 1
    kb.cnt["pe"] = pe0 + n_vg * (NTT + 1)

    def bvb_tick(vg):
        return dv0 + vg * (NTT + 1) + 1

    def ev_tick(vg, tt):
        return dv0 + vg * (NTT + 1) + 1 + tt + 1
    kb.cnt["dve"] = dv0 + n_vg * (NTT + 1)

    ops = []
    for vg in range(n_vg):
        ops.append(("ld", vg))
        if vg >= 2:
            ops.append(("st", vg - 2))
    if n_vg >= 2:
        ops.append(("st", n_vg - 2))
    ops.append(("st", n_vg - 1))

    def f_gp(e):
        for kind, vg in ops:
            if kind == "ld":
                if vg >= 2:
                    e.wait_ge(kb.sems["pe"], mm_tick(vg - 2, NTT - 1))
                WL.issue(e, out=wbf[:, (vg % 2) * FC * VG:(vg % 2) * FC * VG + FC * VG],
                         in_=wv_pm[vg * 128:(vg + 1) * 128, :])
            else:
                e.wait_ge(kb.sems["dve"], ev_tick(vg, NTT - 1))
                for hh in range(HPG):
                    h = vg * HPG + hh
                    SO.issue(e, out=dest[dest_row0 + h * 128:dest_row0 + (h + 1) * 128, :],
                             in_=vgst[:, hh * TOK:(hh + 1) * TOK])
    kb.sec("gpsimd", f_gp)

    def f_tensor(e):
        for vg in range(n_vg):
            if vg >= 1:
                e.wait_ge(kb.sems["dve"], bvb_tick(vg - 1))
            e.matmul(out=ps[4][:, 0:VG], lhsT=ones_f32[0:1, :],
                     rhs=bv_row[0:1, vg * VG:(vg + 1) * VG],
                     start=True, stop=True).then_inc(kb.sems["pe"], 1)
            WL.w(e, vg)
            for tt in range(NTT):
                gidx = vg * NTT + tt
                if gidx >= 2:
                    e.wait_ge(kb.sems["dve"], ev_tick((gidx - 2) // NTT, (gidx - 2) % NTT))
                for fc in range(FC):
                    ins = e.matmul(out=ps[gidx % 2][:, 0:VG],
                                   lhsT=xhat[:, fc * TOK + tt * 128:fc * TOK + tt * 128 + 128],
                                   rhs=wbf[:, (vg % 2) * FC * VG + fc * VG:(vg % 2) * FC * VG + fc * VG + VG],
                                   start=(fc == 0), stop=(fc == FC - 1))
                ins.then_inc(kb.sems["pe"], 1)
    kb.sec("tensor", f_tensor)

    def f_vector(e):
        for vg in range(n_vg):
            e.wait_ge(kb.sems["pe"], bc_tick(vg))
            e.tensor_copy(out=bvb[:, 0:VG], in_=ps[4][:, 0:VG]).then_inc(kb.sems["dve"], 1)
            e.wait_ge(kb.sems["dve"], bvb_tick(vg))
            if vg >= 1:
                SO.w(e, vg * HPG - 1)
            for tt in range(NTT):
                gidx = vg * NTT + tt
                e.wait_ge(kb.sems["pe"], mm_tick(vg, tt))
                e.tensor_tensor(out=vgst[:].rearrange("p (hh t) -> p hh t", hh=HPG)[:, :, tt * 128:tt * 128 + 128],
                                in0=ps[gidx % 2][:, 0:VG].rearrange("p (hh dh) -> p hh dh", hh=HPG),
                                in1=bvb[:, 0:VG].rearrange("p (hh dh) -> p hh dh", hh=HPG),
                                op=OP.add).then_inc(kb.sems["dve"], 1)
    kb.sec("vector", f_vector)
def ph_attn(kb, layer, qbuf, kvfull, abuf, sb, ps, consts, qmode=None):
    (ksb, vsb, qsb, ptsb, msksb, tmpsb, asb, qbrow, recb, srw, qrhs, qtmp, accsb) = sb
    ones_f32 = consts["ones_f32"]
    ones_bf = consts["ones_bf"]
    onescol = consts["onescol"]

    def accs(qc, par):
        return accsb[:, (qc % 2) * 1024 + par * 512:(qc % 2) * 1024 + par * 512 + 512]
    isq_vec = consts["isq_vec"]
    midbias = consts["midbias"]
    kposB = consts["kposB"]
    qposB = consts["qposB"]
    mqa = (layer == 1)
    SEQH = SEQ // 2

    if layer == 0:
        nkt = [20 + 4 * qc for qc in range(NTC)]
        masked = [list(range(4 * qc, 4 * qc + 4)) + list(range(16 + 4 * qc, 20 + 4 * qc))
                  for qc in range(NTC)]
        midt = [set(range(4 * qc + 4, 16 + 4 * qc)) for qc in range(NTC)]
        kv_row0 = [0, D]
        rankrows = 2 * D
    else:
        nkt = [NKT] * NTC
        masked = [[] for _ in range(NTC)]
        midt = [set() for _ in range(NTC)]
        kv_row0 = [0, DH]
        rankrows = 2 * DH

    if layer == 0:
        pe0 = kb.cnt["pe"]
        dv0 = kb.cnt["dve"]
        QP = DS(kb, ("l4",))

        def f_syncq(e):
            QP.issue(e, out=qposB, in_=consts["qposB_dram"])
        kb.sec("sync", f_syncq)

        def cp_tick(qc):
            return dv0 + qc * 9 + 1

        def f_tensor_m(e):
            QP.w(e, 0)
            for qc in range(NTC):
                if qc >= 1:
                    e.wait_ge(kb.sems["dve"], cp_tick(qc - 1))
                e.matmul(out=ps[4][:, 0:TC], lhsT=ones_f32[0:1, :],
                         rhs=qposB[0:1, qc * TC:(qc + 1) * TC],
                         start=True, stop=True).then_inc(kb.sems["pe"], 1)
                kb.inc("pe")
        kb.sec("tensor", f_tensor_m)

        def f_vector_m(e):
            for qc in range(NTC):
                e.wait_ge(kb.sems["pe"], pe0 + qc + 1)
                e.tensor_copy(out=qbrow[:, 0:TC], in_=ps[4][:, 0:TC]).then_inc(kb.sems["dve"], 1)
                e.wait_ge(kb.sems["dve"], kb.inc("dve"))
                for j, t in enumerate(masked[qc]):
                    e.tensor_scalar(out=msksb[:, (qc * 8 + j) * TC:(qc * 8 + j) * TC + TC],
                                    in0=qbrow[:, 0:TC], scalar1=kposB[:, t:t + 1],
                                    scalar2=0.0, op0=OP.subtract, op1=OP.min
                                    ).then_inc(kb.sems["dve"], 1)
                    kb.inc("dve")
        kb.sec("vector", f_vector_m)
        kb.barrier()

    nh = H
    scram = (layer == 0)
    if qmode is None:
        qmode = "host" if scram else "plain"
    fusedq = (qmode == "fused")
    if fusedq:
        rankrows = 3 * D
    KV = DS(kb, ("l0", "l1", "l2", "l3"))
    QL = DS(kb, ("l4", "l5", "l6", "l7") if fusedq else ("l4", "l5"))
    AS = DS(kb, ("g0", "g1"))
    QW = H * TOK
    if fusedq:
        QW = 16 * 256

    pe_t = kb.cnt["pe"]
    act_t = kb.cnt["act"]
    dve_t = kb.cnt["dve"]
    S_tick = {}
    PV_tick = {}
    RSUM_tick = {}
    BC_tick = {}
    EXP_tick = {}
    MADD_tick = {}
    ACC_tick = {}
    EV_tick = {}
    GA_tick = {}
    for h in range(nh):
        for qc in range(NTC):
            n = nkt[qc]
            mset = set(masked[qc])
            if scram:
                dve_t += 3 if fusedq else 1
                GA_tick[(h, qc)] = dve_t
            for kt in range(n):
                pe_t += 1
                S_tick[(h, qc, kt)] = pe_t
                if kt >= 1:
                    pe_t += 1
                    PV_tick[(h, qc, kt - 1)] = pe_t
                if kt in mset:
                    dve_t += 1
                    MADD_tick[(h, qc, kt)] = dve_t
                if kt >= 1:
                    dve_t += 1
                    ACC_tick[(h, qc, kt - 1)] = dve_t
                act_t += 1
                EXP_tick[(h, qc, kt)] = act_t
            pe_t += 1
            PV_tick[(h, qc, n - 1)] = pe_t
            dve_t += 1
            ACC_tick[(h, qc, n - 1)] = dve_t
            pe_t += 2
            RSUM_tick[(h, qc)] = pe_t
            dve_t += 1
            pe_t += 1
            BC_tick[(h, qc)] = pe_t
            dve_t += 1
            dve_t += 1
            EV_tick[(h, qc)] = dve_t
    kb.cnt["pe"] = pe_t
    kb.cnt["act"] = act_t
    kb.cnt["dve"] = dve_t

    def prev_hqc(h, qc):
        return (h, qc - 1) if qc > 0 else (h - 1, NTC - 1)

    def prev2_hqc(h, qc):
        if qc >= 2:
            return (h, qc - 2)
        if h >= 1:
            return (h - 1, NTC - 2 + qc)
        return None

    def bank_free_tick(h, qc, kt):
        if kt in set(masked[qc]):
            return ("dve", MADD_tick[(h, qc, kt)])
        return ("act", EXP_tick[(h, qc, kt)])

    def f_sync(e):
        for h in range(nh):
            if (not mqa) or h == 0:
                if (not mqa) and h >= 2:
                    e.wait_ge(kb.sems["pe"], PV_tick[(h - 2, NTC - 1, nkt[NTC - 1] - 1)])
                slot = 0 if mqa else (h % 2)
                krow = kv_row0[0] + (0 if mqa else h * 128)
                vrow = kv_row0[1] + (0 if mqa else h * 128)
                KV.issue(e, out=ksb[:, slot * SEQ:(slot + 1) * SEQ].rearrange("p (r t) -> p r t", r=2),
                         in_=kvfull.rearrange("(r z) t -> z r t", r=2)[krow:krow + 128, :, :])
                KV.issue(e, out=vsb[:, slot * SEQ:(slot + 1) * SEQ].rearrange("p (r t) -> p r t", r=2),
                         in_=kvfull.rearrange("(r z) t -> z r t", r=2)[vrow:vrow + 128, :, :])
            if h >= 2:
                if scram:
                    e.wait_ge(kb.sems["dve"], GA_tick[(h - 2, NTC - 1)])
                else:
                    e.wait_ge(kb.sems["pe"], S_tick[(h - 2, NTC - 1, nkt[NTC - 1] - 1)])
            if fusedq:
                r = h // 8
                qrow = r * 3 * D + 2 * D
                qcol = 256 * (h % 8)
                for fh in range(2):
                    QL.issue(e, out=qsb[:, (h % 2) * QW + fh * 8 * 256:(h % 2) * QW + (fh + 1) * 8 * 256
                                        ].rearrange("p (fc g) -> p fc g", fc=8),
                             in_=qbuf[qrow + fh * 8 * 128:qrow + (fh + 1) * 8 * 128,
                                      qcol:qcol + 256].rearrange("(fc p) g -> p fc g", p=128))
            elif scram:
                QL.issue(e, out=qsb[:, (h % 2) * TOK:(h % 2) * TOK + TOK],
                         in_=qbuf[:, h * TOK:(h + 1) * TOK])
            else:
                QL.issue(e, out=qsb[:, (h % 2) * TOK:(h % 2) * TOK + TOK],
                         in_=qbuf[h * 128:(h + 1) * 128, :])
    kb.sec("sync", f_sync)

    def f_tensor(e):
        for h in range(nh):
            kslot = 0 if mqa else (h % 2)
            if mqa:
                KV.w(e, 0)
                KV.w(e, 1)
            else:
                KV.w(e, 2 * h)
                KV.w(e, 2 * h + 1)
            QL.w(e, h)
            for qc in range(NTC):
                n = nkt[qc]
                for kt in range(n):
                    if kt >= 2:
                        sem, v = bank_free_tick(h, qc, kt - 2)
                        e.wait_ge(kb.sems[sem], v)
                    elif qc > 0 or h > 0:
                        ph_, pqc = prev_hqc(h, qc)
                        pn = nkt[pqc]
                        sem, v = bank_free_tick(ph_, pqc, pn - 2 + kt)
                        e.wait_ge(kb.sems[sem], v)
                    if kt == 0 and scram:
                        e.wait_ge(kb.sems["dve"], GA_tick[(h, qc)])
                    if scram:
                        s_rhs = qrhs[:, (qc % 2) * TC:(qc % 2) * TC + TC]
                    else:
                        s_rhs = qsb[:, (h % 2) * TOK + qc * TC:(h % 2) * TOK + qc * TC + TC]
                    e.matmul(out=ps[kt % 2][:, 0:TC],
                             lhsT=ksb[:, kslot * SEQ + kt * 128:kslot * SEQ + kt * 128 + 128],
                             rhs=s_rhs,
                             start=True, stop=True).then_inc(kb.sems["pe"], 1)
                    if kt >= 1:
                        kp = kt - 1
                        e.wait_ge(kb.sems["act"], EXP_tick[(h, qc, kp)])
                        if kp == 0 and (qc > 0 or h > 0):
                            ph_, pqc = prev_hqc(h, qc)
                            e.wait_ge(kb.sems["dve"], EV_tick[(ph_, pqc)])
                        e.matmul(out=ps[2][:, 0:TC],
                                 lhsT=vsb[:, kslot * SEQ + kp * 128:kslot * SEQ + kp * 128 + 128],
                                 rhs=ptsb[:, (kp % 2) * TC:(kp % 2) * TC + TC],
                                 start=(kp == 0), stop=False).then_inc(kb.sems["pe"], 1)
                kp = n - 1
                e.wait_ge(kb.sems["act"], EXP_tick[(h, qc, kp)])
                e.matmul(out=ps[2][:, 0:TC],
                         lhsT=vsb[:, kslot * SEQ + kp * 128:kslot * SEQ + kp * 128 + 128],
                         rhs=ptsb[:, (kp % 2) * TC:(kp % 2) * TC + TC],
                         start=False, stop=True).then_inc(kb.sems["pe"], 1)
                e.wait_ge(kb.sems["dve"], ACC_tick[(h, qc, n - 1)])
                if qc > 0 or h > 0:
                    ph_, pqc = prev_hqc(h, qc)
                    e.wait_ge(kb.sems["dve"], EV_tick[(ph_, pqc)] - 2)
                e.matmul(out=ps[3][0:1, 0:TC], lhsT=onescol[:],
                         rhs=accs(qc, 0), start=True, stop=False).then_inc(kb.sems["pe"], 1)
                e.matmul(out=ps[3][0:1, 0:TC], lhsT=onescol[:],
                         rhs=accs(qc, 1), start=False, stop=True).then_inc(kb.sems["pe"], 1)
                e.wait_ge(kb.sems["dve"], EV_tick[(h, qc)] - 2)
                e.matmul(out=ps[4][:, 0:TC], lhsT=ones_f32[0:1, :],
                         rhs=srw[0:1, 0:TC], start=True, stop=True).then_inc(kb.sems["pe"], 1)
    kb.sec("tensor", f_tensor)

    def f_scalar(e):
        for h in range(nh):
            for qc in range(NTC):
                n = nkt[qc]
                mset = set(masked[qc])
                for kt in range(n):
                    if kt >= 2:
                        e.wait_ge(kb.sems["pe"], PV_tick[(h, qc, kt - 2)])
                        e.wait_ge(kb.sems["dve"], ACC_tick[(h, qc, kt - 2)])
                    elif qc > 0 or h > 0:
                        ph_, pqc = prev_hqc(h, qc)
                        pn = nkt[pqc]
                        e.wait_ge(kb.sems["pe"], PV_tick[(ph_, pqc, pn - 2 + kt)])
                        e.wait_ge(kb.sems["dve"], ACC_tick[(ph_, pqc, pn - 2 + kt)])
                    if kt in mset:
                        e.wait_ge(kb.sems["dve"], MADD_tick[(h, qc, kt)])
                        e.activation(out=ptsb[:, (kt % 2) * TC:(kt % 2) * TC + TC],
                                     in_=tmpsb[:, (kt % 2) * TC:(kt % 2) * TC + TC],
                                     func=AF.Exp,
                                     scale=isq_vec[:, 0:1]).then_inc(kb.sems["act"], 1)
                    elif kt in midt[qc]:
                        e.wait_ge(kb.sems["pe"], S_tick[(h, qc, kt)])
                        e.activation(out=ptsb[:, (kt % 2) * TC:(kt % 2) * TC + TC],
                                     in_=ps[kt % 2][:, 0:TC], func=AF.Exp,
                                     scale=isq_vec[:, 0:1],
                                     bias=midbias[:, 0:1]).then_inc(kb.sems["act"], 1)
                    else:
                        e.wait_ge(kb.sems["pe"], S_tick[(h, qc, kt)])
                        e.activation(out=ptsb[:, (kt % 2) * TC:(kt % 2) * TC + TC],
                                     in_=ps[kt % 2][:, 0:TC], func=AF.Exp,
                                     scale=isq_vec[:, 0:1]).then_inc(kb.sems["act"], 1)
    kb.sec("scalar", f_scalar)

    def f_vector(e):
        for h in range(nh):
            for qc in range(NTC):
                n = nkt[qc]
                if scram:
                    if fusedq:
                        QL.w(e, 2 * h)
                        QL.w(e, 2 * h + 1)
                    else:
                        QL.w(e, h)
                    if qc >= 2:
                        e.wait_ge(kb.sems["pe"], S_tick[(h, qc - 2, nkt[qc - 2] - 1)])
                    elif h >= 1:
                        e.wait_ge(kb.sems["pe"], S_tick[(h - 1, NTC - 2 + qc, nkt[NTC - 2 + qc] - 1)])
                    if fusedq:
                        qv = qsb[:, (h % 2) * QW:(h % 2) * QW + QW].rearrange("p (m g) -> p g m", m=16)
                        e.tensor_scalar(
                            out=qtmp[:, 0:TC].rearrange("p (g m) -> p g m", m=16),
                            in0=qv[:, 128 + qc * 32:128 + (qc + 1) * 32, :],
                            scalar1=consts["qw1"][:, 0:1], scalar2=None,
                            op0=OP.mult).then_inc(kb.sems["dve"], 1)
                        e.tensor_scalar(
                            out=qrhs[:, (qc % 2) * TC:(qc % 2) * TC + TC].rearrange("p (g m) -> p g m", m=16),
                            in0=qv[:, qc * 32:(qc + 1) * 32, :],
                            scalar1=consts["qw0"][:, 0:1], scalar2=None,
                            op0=OP.mult).then_inc(kb.sems["dve"], 1)
                        e.wait_ge(kb.sems["dve"], GA_tick[(h, qc)] - 1)
                        e.tensor_tensor(
                            out=qrhs[:, (qc % 2) * TC:(qc % 2) * TC + TC],
                            in0=qrhs[:, (qc % 2) * TC:(qc % 2) * TC + TC],
                            in1=qtmp[:, 0:TC],
                            op=OP.add).then_inc(kb.sems["dve"], 1)
                    else:
                        e.tensor_copy(
                            out=qrhs[:, (qc % 2) * TC:(qc % 2) * TC + TC].rearrange("p (g m) -> p g m", m=16),
                            in_=qsb[:, (h % 2) * TOK:(h % 2) * TOK + TOK].rearrange(
                                "p (m g) -> p g m", m=16)[:, qc * 32:(qc + 1) * 32, :]
                        ).then_inc(kb.sems["dve"], 1)
                def acc_op(kt):
                    e.wait_ge(kb.sems["act"], EXP_tick[(h, qc, kt)])
                    dst = accs(qc, kt % 2)
                    if kt <= 1:
                        p2 = prev2_hqc(h, qc)
                        if p2 is not None:
                            e.wait_ge(kb.sems["pe"], RSUM_tick[p2])
                        e.tensor_copy(out=dst,
                                      in_=ptsb[:, (kt % 2) * TC:(kt % 2) * TC + TC]
                                      ).then_inc(kb.sems["dve"], 1)
                    else:
                        e.tensor_tensor(out=dst, in0=dst,
                                        in1=ptsb[:, (kt % 2) * TC:(kt % 2) * TC + TC],
                                        op=OP.add).then_inc(kb.sems["dve"], 1)
                for kt in range(n):
                    if kt in set(masked[qc]):
                        e.wait_ge(kb.sems["pe"], S_tick[(h, qc, kt)])
                        if kt >= 2:
                            e.wait_ge(kb.sems["act"], EXP_tick[(h, qc, kt - 2)])
                        elif qc > 0 or h > 0:
                            ph_, pqc = prev_hqc(h, qc)
                            pn = nkt[pqc]
                            e.wait_ge(kb.sems["act"], EXP_tick[(ph_, pqc, pn - 2 + kt)])
                        j = masked[qc].index(kt)
                        e.tensor_tensor(out=tmpsb[:, (kt % 2) * TC:(kt % 2) * TC + TC],
                                        in0=ps[kt % 2][:, 0:TC],
                                        in1=msksb[:, (qc * 8 + j) * TC:(qc * 8 + j) * TC + TC],
                                        op=OP.add).then_inc(kb.sems["dve"], 1)
                    if kt >= 1:
                        acc_op(kt - 1)
                acc_op(n - 1)
                e.wait_ge(kb.sems["pe"], RSUM_tick[(h, qc)])
                e.reciprocal(out=srw[0:1, 0:TC], in_=ps[3][0:1, 0:TC]).then_inc(kb.sems["dve"], 1)
                e.wait_ge(kb.sems["dve"], EV_tick[(h, qc)] - 2)
                e.wait_ge(kb.sems["pe"], BC_tick[(h, qc)])
                e.tensor_copy(out=recb[:, 0:TC], in_=ps[4][:, 0:TC]).then_inc(kb.sems["dve"], 1)
                e.wait_ge(kb.sems["dve"], EV_tick[(h, qc)] - 1)
                if h >= 2 and qc == 0:
                    AS.w(e, h - 2)
                e.tensor_tensor(out=asb[:, (h % 2) * TOK + qc * TC:(h % 2) * TOK + qc * TC + TC],
                                in0=ps[2][:, 0:TC], in1=recb[:, 0:TC],
                                op=OP.mult).then_inc(kb.sems["dve"], 1)
    kb.sec("vector", f_vector)

    def f_gp(e):
        for h in range(nh):
            e.wait_ge(kb.sems["dve"], EV_tick[(h, NTC - 1)])
            AS.issue(e, out=abuf[h * 128:(h + 1) * 128, :],
                     in_=asb[:, (h % 2) * TOK:(h % 2) * TOK + TOK])
    kb.sec("gpsimd", f_gp)


def ph_wo(kb, abuf, wo_dram, bo_pm, resid, sb, ps):
    wosb, atsb, rsb, rosb = sb
    for half in range(2):
        pe0 = kb.cnt["pe"]
        dv0 = kb.cnt["dve"]
        nidx = NTC * 8
        AT = DS(kb, ("l0", "l1"))
        RS = DS(kb, ("l2", "l3"))
        WOL = DS(kb, ("g0", "g1"))
        WB = DS(kb, ("g2", "g3", "g4", "g5"))

        dsched = [("ar", 0), ("ar", 1)]
        for i in range(2, nidx):
            dsched.append(("ab", i - 2))
            dsched.append(("ar", i))
        dsched += [("ab", nidx - 2), ("ab", nidx - 1)]
        ar_tick = {}
        ab_tick = {}
        t = dv0
        for kind, i in dsched:
            t += 1
            (ar_tick if kind == "ar" else ab_tick)[i] = t
        kb.cnt["dve"] = t

        def mm_tick(i, pe0=pe0):
            return pe0 + i + 1
        kb.cnt["pe"] = pe0 + nidx

        def f_gp(e, half=half, pe0=pe0, ab_tick=ab_tick, WOL=WOL, WB=WB):
            e.wait_ge(kb.sems["pe"], pe0)
            WOL.issue(e, out=wosb[:],
                      in_=wo_dram.rearrange("(oc p) o2 -> p oc o2", p=128)[:, :, half * 1024:(half + 1) * 1024])
            for i in range(nidx):
                e.wait_ge(kb.sems["dve"], ab_tick[i])
                tc, j = i // 8, i % 8
                o2 = half * 8 + j
                WB.issue(e, out=resid[o2 * 128:(o2 + 1) * 128, tc * TC:(tc + 1) * TC],
                         in_=rosb[:, (i % 4) * TC:(i % 4) * TC + TC])
        kb.sec("gpsimd", f_gp)

        def f_sync(e, half=half, pe0=pe0, dv0=dv0, ar_tick=ar_tick, AT=AT, RS=RS, mm_tick=mm_tick):
            for tc in range(NTC):
                if tc >= 2:
                    e.wait_ge(kb.sems["pe"], mm_tick((tc - 1) * 8 - 1))
                else:
                    e.wait_ge(kb.sems["pe"], pe0)
                AT.issue(e, out=atsb[:, (tc % 2) * FC * TC:(tc % 2) * FC * TC + FC * TC],
                         in_=abuf.rearrange("(oc p) t -> p oc t", p=128)[:, :, tc * TC:(tc + 1) * TC])
                for j in range(8):
                    i = tc * 8 + j
                    if i >= 2:
                        e.wait_ge(kb.sems["dve"], ar_tick[i - 2])
                    elif half == 1:
                        e.wait_ge(kb.sems["dve"], dv0)
                    RS.issue(e, out=rsb[:, (i % 4) * TC:(i % 4) * TC + TC],
                             in_=resid[(half * 8 + j) * 128:(half * 8 + j + 1) * 128,
                                       tc * TC:(tc + 1) * TC])
        kb.sec("sync", f_sync)

        def f_tensor(e, half=half, AT=AT, WOL=WOL, ar_tick=ar_tick, mm_tick=mm_tick):
            WOL.w(e, 0)
            for tc in range(NTC):
                AT.w(e, tc)
                for j in range(8):
                    i = tc * 8 + j
                    if i >= 2:
                        e.wait_ge(kb.sems["dve"], ar_tick[i - 2])
                    for oc in range(FC):
                        ins = e.matmul(out=ps[i % 2][:, 0:TC],
                                       lhsT=wosb[:, oc * 1024 + j * 128:oc * 1024 + j * 128 + 128],
                                       rhs=atsb[:, (tc % 2) * FC * TC + oc * TC:(tc % 2) * FC * TC + oc * TC + TC],
                                       start=(oc == 0), stop=(oc == FC - 1))
                    ins.then_inc(kb.sems["pe"], 1)
        kb.sec("tensor", f_tensor)

        def f_vector(e, half=half, dsched=dsched, RS=RS, WB=WB, mm_tick=mm_tick):
            for kind, i in dsched:
                tc, j = i // 8, i % 8
                if kind == "ar":
                    e.wait_ge(kb.sems["pe"], mm_tick(i))
                    RS.w(e, i)
                    if i >= 4:
                        WB.w(e, i - 4)
                    e.tensor_tensor(out=rosb[:, (i % 4) * TC:(i % 4) * TC + TC],
                                    in0=ps[i % 2][:, 0:TC],
                                    in1=rsb[:, (i % 4) * TC:(i % 4) * TC + TC],
                                    op=OP.add).then_inc(kb.sems["dve"], 1)
                else:
                    o2 = half * 8 + j
                    e.tensor_scalar(out=rosb[:, (i % 4) * TC:(i % 4) * TC + TC],
                                    in0=rosb[:, (i % 4) * TC:(i % 4) * TC + TC],
                                    scalar1=bo_pm[:, o2:o2 + 1], scalar2=None,
                                    op0=OP.add).then_inc(kb.sems["dve"], 1)
        kb.sec("vector", f_vector)
def ph_down_prep(kb, down_pm, wd_scr, ai32, ab, ps):
    NI = IFF // 128
    DL = DS(kb, ("l0", "l1"))
    PS_ = DS(kb, ("g0", "g1"))
    ac0 = kb.cnt["act"]

    def f_sync(e):
        for ic in range(NI):
            if ic >= 2:
                e.wait_ge(kb.sems["act"], ac0 + ic - 1)
            DL.issue(e, out=ai32[:, (ic % 2) * D:(ic % 2) * D + D],
                     in_=down_pm[ic * 128:(ic + 1) * 128, :])
    kb.sec("sync", f_sync)

    def f_scalar(e):
        for ic in range(NI):
            DL.w(e, ic)
            if ic >= 2:
                PS_.w(e, ic - 2)
            e.activation(out=ab[:, (ic % 2) * D:(ic % 2) * D + D],
                         in_=ai32[:, (ic % 2) * D:(ic % 2) * D + D],
                         func=AF.Copy, bias=-128.0).then_inc(kb.sems["act"], 1)
            kb.inc("act")
    kb.sec("scalar", f_scalar)

    def f_gp(e):
        for ic in range(NI):
            e.wait_ge(kb.sems["act"], ac0 + ic + 1)
            PS_.issue(e, out=wd_scr[ic * 128:(ic + 1) * 128, :],
                      in_=ab[:, (ic % 2) * D:(ic % 2) * D + D])
    kb.sec("gpsimd", f_gp)


def ph_ffn0_a(kb, xhat, gate_pm, up_pm, h_scr, sgv, suv, sb, ps):
    ai32, wbuf, silu_sb, u_sb, h_stage = sb
    NIB = 16
    GL = DS(kb, ("l0", "l1"))
    HS = DS(kb, ("g0", "g1"))
    ac0 = kb.cnt["act"]
    pe0 = kb.cnt["pe"]
    dv0 = kb.cnt["dve"]

    def wgT(ib):
        return wbuf[:, (ib % 2) * 8192:(ib % 2) * 8192 + 8192]

    def wuT(ib):
        return wbuf[:, 16384 + (ib % 2) * 8192:16384 + (ib % 2) * 8192 + 8192]

    sched = [("c", 0, j) for j in range(8)]
    for ib in range(NIB):
        for k in range(16):
            sched.append(("s", ib, k))
            if ib + 1 < NIB and k % 2 == 0:
                sched.append(("c", ib + 1, k // 2))
    cast_tick = {}
    silu_tick = {}
    t = ac0
    for kind, ib, k in sched:
        t += 1
        (cast_tick if kind == "c" else silu_tick)[(ib, k)] = t
    kb.cnt["act"] = t

    def g_tick(ib, k):
        return pe0 + ib * 32 + 2 * k + 1

    def u_tick(ib, k):
        return pe0 + ib * 32 + 2 * k + 2

    def dsched_ib():
        s = [("ue", 0), ("ue", 1)]
        for k in range(2, 16):
            s.append(("h", k - 2))
            s.append(("ue", k))
        s += [("h", 14), ("h", 15)]
        return s
    ue_tick = {}
    h_tick = {}
    t = dv0
    for ib in range(NIB):
        for kind, k in dsched_ib():
            t += 1
            (ue_tick if kind == "ue" else h_tick)[(ib, k)] = t
    kb.cnt["dve"] = t
    kb.cnt["pe"] = pe0 + NIB * 32

    def f_sync(e):
        for ib in range(NIB):
            for j in range(8):
                if j >= 2:
                    e.wait_ge(kb.sems["act"], cast_tick[(ib, j - 2)])
                elif ib >= 1:
                    e.wait_ge(kb.sems["act"], cast_tick[(ib - 1, j + 6)])
                src = gate_pm if j < 4 else up_pm
                q = j % 4
                GL.issue(e, out=ai32[:, (j % 2) * 2048:(j % 2) * 2048 + 2048],
                         in_=src[ib * 128:(ib + 1) * 128, q * 2048:(q + 1) * 2048])
    kb.sec("sync", f_sync)

    def f_scalar(e):
        for kind, ib, k in sched:
            if kind == "c":
                j = k
                GL.w(e, ib * 8 + j)
                if ib >= 2 and j in (0, 4):
                    e.wait_ge(kb.sems["pe"], pe0 + (ib - 1) * 32)
                dst = wgT(ib) if j < 4 else wuT(ib)
                q = j % 4
                e.activation(out=dst[:, q * 2048:(q + 1) * 2048],
                             in_=ai32[:, (j % 2) * 2048:(j % 2) * 2048 + 2048],
                             func=AF.Copy, bias=-128.0).then_inc(kb.sems["act"], 1)
            else:
                icb, tc = k // 4, k % 4
                e.wait_ge(kb.sems["pe"], g_tick(ib, k))
                gi = ib * 16 + k
                if gi >= 2:
                    pib, pk = (gi - 2) // 16, (gi - 2) % 16
                    e.wait_ge(kb.sems["dve"], h_tick[(pib, pk)])
                ic = ib * 4 + icb
                e.activation(out=silu_sb[:, (k % 2) * TC:(k % 2) * TC + TC],
                             in_=ps[k % 2][:, 0:TC], func=AF.Silu,
                             scale=sgv[:, ic:ic + 1]).then_inc(kb.sems["act"], 1)
    kb.sec("scalar", f_scalar)

    def f_tensor(e):
        for ib in range(NIB):
            for j in range(8):
                e.wait_ge(kb.sems["act"], cast_tick[(ib, j)])
            for k in range(16):
                icb, tc = k // 4, k % 4
                gi = ib * 16 + k
                if gi >= 2:
                    pib, pk = (gi - 2) // 16, (gi - 2) % 16
                    e.wait_ge(kb.sems["act"], silu_tick[(pib, pk)])
                    e.wait_ge(kb.sems["dve"], ue_tick[(pib, pk)])
                for fc in range(FC):
                    ins = e.matmul(out=ps[k % 2][:, 0:TC],
                                   lhsT=wgT(ib)[:, fc * 512 + icb * 128:fc * 512 + icb * 128 + 128],
                                   rhs=xhat[:, fc * TOK + tc * TC:fc * TOK + tc * TC + TC],
                                   start=(fc == 0), stop=(fc == FC - 1))
                ins.then_inc(kb.sems["pe"], 1)
                for fc in range(FC):
                    ins = e.matmul(out=ps[2 + k % 2][:, 0:TC],
                                   lhsT=wuT(ib)[:, fc * 512 + icb * 128:fc * 512 + icb * 128 + 128],
                                   rhs=xhat[:, fc * TOK + tc * TC:fc * TOK + tc * TC + TC],
                                   start=(fc == 0), stop=(fc == FC - 1))
                ins.then_inc(kb.sems["pe"], 1)
    kb.sec("tensor", f_tensor)

    def f_vector(e):
        for ib in range(NIB):
            for kind, k in dsched_ib():
                icb, tc = k // 4, k % 4
                ic = ib * 4 + icb
                if kind == "ue":
                    e.wait_ge(kb.sems["pe"], u_tick(ib, k))
                    if ib > 0 or k >= 2:
                        pib, pk = (ib, k - 2) if k >= 2 else (ib - 1, 14 + k)
                        e.wait_ge(kb.sems["dve"], h_tick[(pib, pk)])
                    e.tensor_scalar(out=u_sb[:, (k % 2) * TC:(k % 2) * TC + TC],
                                    in0=ps[2 + k % 2][:, 0:TC],
                                    scalar1=suv[:, ic:ic + 1], scalar2=None,
                                    op0=OP.mult).then_inc(kb.sems["dve"], 1)
                else:
                    e.wait_ge(kb.sems["act"], silu_tick[(ib, k)])
                    if tc == 0 and ic >= 2:
                        HS.w(e, ic - 2)
                    e.tensor_tensor(out=h_stage[:, (icb % 2) * TOK + tc * TC:(icb % 2) * TOK + tc * TC + TC],
                                    in0=silu_sb[:, (k % 2) * TC:(k % 2) * TC + TC],
                                    in1=u_sb[:, (k % 2) * TC:(k % 2) * TC + TC],
                                    op=OP.mult).then_inc(kb.sems["dve"], 1)
    kb.sec("vector", f_vector)

    def f_gp(e):
        for ib in range(NIB):
            for icb in range(4):
                ic = ib * 4 + icb
                e.wait_ge(kb.sems["dve"], h_tick[(ib, icb * 4 + 3)])
                HS.issue(e, out=h_scr[:, ic * TOK:(ic + 1) * TOK],
                         in_=h_stage[:, (icb % 2) * TOK:(icb % 2) * TOK + TOK])
    kb.sec("gpsimd", f_gp)


def ph_ffn0_b(kb, h_scr, down_pm, sdv, resid, sb, ps, consts):
    htA, htB, wdt, rsb, rosb, hsb = sb
    ones_bf, ones_f32 = consts["ones_bf"], consts["ones_f32"]
    hsr = consts["hsr"]
    NI = IFF // 128
    NG = 8
    GSZ = NI // NG
    hts = (htA, htB)
    HT = DS(kb, ("l0", "l1"))
    RS = DS(kb, ("l2", "l3"))
    WDT = DS(kb, ("g0", "g1"))
    WB = DS(kb, ("g2", "g3", "g4", "g5"))
    pe0 = kb.cnt["pe"]
    dv0 = kb.cnt["dve"]

    def hs_tick(tc, ic):
        return pe0 + tc * (NI + 1 + 4 * NG) + ic + 1

    def bc_tick(tc):
        return pe0 + tc * (NI + 1 + 4 * NG) + NI + 1

    def mm_tick(tc, og, icg):
        return pe0 + tc * (NI + 1 + 4 * NG) + NI + 1 + og * NG + icg + 1
    kb.cnt["pe"] = pe0 + NTC * (NI + 1 + 4 * NG)

    seq12 = [("a", 0), ("a", 1), ("m", 0), ("a", 2), ("m", 1), ("a", 3),
             ("m", 2), ("r", 0), ("m", 3), ("r", 1), ("r", 2), ("r", 3)]
    a_tick = {}
    m_tick = {}
    r_tick = {}
    hrow_tick = {}
    t = dv0
    for tc in range(NTC):
        t += 3
        hrow_tick[tc] = t
        for og in range(4):
            for kind, j in seq12:
                t += 1
                {"a": a_tick, "m": m_tick, "r": r_tick}[kind][(tc, og, j)] = t
    kb.cnt["dve"] = t

    def f_sync(e):
        for tc in range(NTC):
            if tc >= 2:
                e.wait_ge(kb.sems["pe"], mm_tick(tc - 2, 3, NG - 1))
            HT.issue(e, out=hts[tc % 2].rearrange("p (ic t) -> p ic t", ic=NI),
                     in_=h_scr.rearrange("p (ic t) -> p ic t", ic=NI)[:, :, tc * TC:(tc + 1) * TC])
            for og in range(4):
                for j in range(4):
                    i = (tc * 4 + og) * 4 + j
                    if i >= 2:
                        pi = i - 2
                        ptc, pog, pj = pi // 16, (pi // 4) % 4, pi % 4
                        e.wait_ge(kb.sems["dve"], r_tick[(ptc, pog, pj)])
                    oc = og * 4 + j
                    RS.issue(e, out=rsb[:, (i % 4) * TC:(i % 4) * TC + TC],
                             in_=resid[oc * 128:(oc + 1) * 128, tc * TC:(tc + 1) * TC])
    kb.sec("sync", f_sync)

    def f_gp(e):
        for tc in range(NTC):
            for og in range(4):
                for icg in range(NG):
                    gi = (tc * 4 + og) * NG + icg
                    if gi >= 2:
                        pgi = gi - 2
                        ptc, pog, picg = pgi // (4 * NG), (pgi // NG) % 4, pgi % NG
                        e.wait_ge(kb.sems["pe"], mm_tick(ptc, pog, picg))
                    WDT.issue(e,
                              out=wdt[:, (icg % 2) * 4096:(icg % 2) * 4096 + 4096]
                              .rearrange("p (ic oc) -> p ic oc", ic=GSZ),
                              in_=down_pm.rearrange("(ic p) oc -> p ic oc", p=128)
                              [:, icg * GSZ:(icg + 1) * GSZ, og * 512:(og + 1) * 512])
    kb.sec("gpsimd", f_gp)

    def f_scalar(e):
        for tc in range(NTC):
            for og in range(4):
                for j in range(4):
                    e.wait_ge(kb.sems["dve"], r_tick[(tc, og, j)])
                    oc = og * 4 + j
                    WB.issue(e, out=resid[oc * 128:(oc + 1) * 128, tc * TC:(tc + 1) * TC],
                             in_=rosb[:, (j % 4) * TC:(j % 4) * TC + TC])
    kb.sec("scalar", f_scalar)

    def f_tensor(e):
        for tc in range(NTC):
            ht = hts[tc % 2]
            HT.w(e, tc)
            if tc >= 1:
                e.wait_ge(kb.sems["dve"], hrow_tick[tc - 1] - 2)
            for ic in range(NI):
                e.matmul(out=ps[5][0:1, 0:TC], lhsT=ones_bf[:, 0:1],
                         rhs=ht[:, ic * 512:(ic + 1) * 512],
                         start=(ic == 0), stop=(ic == NI - 1)).then_inc(kb.sems["pe"], 1)
            e.wait_ge(kb.sems["dve"], hrow_tick[tc] - 1)
            if tc >= 1:
                e.wait_ge(kb.sems["dve"], hrow_tick[tc - 1])
            e.matmul(out=ps[6][:, 0:TC], lhsT=ones_f32[0:1, :],
                     rhs=hsr[0:1, 0:TC], start=True, stop=True).then_inc(kb.sems["pe"], 1)
            for og in range(4):
                for icg in range(NG):
                    gi = (tc * 4 + og) * NG + icg
                    WDT.w(e, gi)
                    if icg == 0 and (tc > 0 or og > 0):
                        ptc, pog = (tc, og - 1) if og > 0 else (tc - 1, 3)
                        e.wait_ge(kb.sems["dve"], a_tick[(ptc, pog, 3)])
                    for i8 in range(GSZ):
                        ic = icg * GSZ + i8
                        for j in range(4):
                            base = (icg % 2) * 4096 + i8 * 512 + j * 128
                            ins = e.matmul(out=ps[j][:, 0:TC],
                                           lhsT=wdt[:, base:base + 128],
                                           rhs=ht[:, ic * 512:(ic + 1) * 512],
                                           start=(ic == 0), stop=(ic == NI - 1))
                    ins.then_inc(kb.sems["pe"], 1)
    kb.sec("tensor", f_tensor)

    def f_vector(e):
        for tc in range(NTC):
            e.wait_ge(kb.sems["pe"], hs_tick(tc, NI - 1))
            e.tensor_copy(out=hsr[0:1, 0:TC], in_=ps[5][0:1, 0:TC]).then_inc(kb.sems["dve"], 1)
            e.wait_ge(kb.sems["dve"], hrow_tick[tc] - 2)
            e.tensor_scalar(out=hsr[0:1, 0:TC], in0=hsr[0:1, 0:TC], scalar1=-128.0,
                            scalar2=None, op0=OP.mult).then_inc(kb.sems["dve"], 1)
            e.wait_ge(kb.sems["pe"], bc_tick(tc))
            e.tensor_copy(out=hsb[:, 0:TC], in_=ps[6][:, 0:TC]).then_inc(kb.sems["dve"], 1)
            e.wait_ge(kb.sems["dve"], hrow_tick[tc])
            for og in range(4):
                for kind, j in seq12:
                    oc = og * 4 + j
                    i = (tc * 4 + og) * 4 + j
                    if kind == "a":
                        e.wait_ge(kb.sems["pe"], mm_tick(tc, og, NG - 1))
                        if i >= 4:
                            WB.w(e, i - 4)
                        e.tensor_tensor(out=rosb[:, (j % 4) * TC:(j % 4) * TC + TC],
                                        in0=ps[j][:, 0:TC],
                                        in1=hsb[:, 0:TC],
                                        op=OP.add).then_inc(kb.sems["dve"], 1)
                    elif kind == "m":
                        e.tensor_scalar(out=rosb[:, (j % 4) * TC:(j % 4) * TC + TC],
                                        in0=rosb[:, (j % 4) * TC:(j % 4) * TC + TC],
                                        scalar1=sdv[:, oc:oc + 1], scalar2=None,
                                        op0=OP.mult).then_inc(kb.sems["dve"], 1)
                    else:
                        RS.w(e, i)
                        e.tensor_tensor(out=rosb[:, (j % 4) * TC:(j % 4) * TC + TC],
                                        in0=rosb[:, (j % 4) * TC:(j % 4) * TC + TC],
                                        in1=rsb[:, (i % 4) * TC:(i % 4) * TC + TC],
                                        op=OP.add).then_inc(kb.sems["dve"], 1)
    kb.sec("vector", f_vector)


def ph_ffn1(kb, xhat, wt1_pm, g1_pm, at_pm, bs_pm, rep_sb, resid, sb, ps, consts):
    (wtb, at_sb, bs_sb, g1_sb, cwT, cwt_sb, tc_sb, rsb, rosb, tk) = sb
    ident = consts["ident"]

    SL = DS(kb, ("g0", "g1"))

    def f_ld(e):
        SL.issue(e, out=g1_sb[:], in_=g1_pm)
        SL.issue(e, out=at_sb[:], in_=at_pm)
        SL.issue(e, out=bs_sb[:], in_=bs_pm)
    kb.sec("gpsimd", f_ld)
    kb.barrier()

    lg, t1, t2, t5, t7, t8, a2, t3, t4, t6, t9, a1, cwall = tk
    pe0 = kb.cnt["pe"]
    dv0 = kb.cnt["dve"]
    ac0 = kb.cnt["act"]
    ev_tick = {}
    dv = dv0
    ac = ac0
    for tt in range(NTT):
        ev_tick[tt] = dv + 1
        dv += 12
        ac += 2
    kb.cnt["dve"] = dv
    kb.cnt["act"] = ac

    def f_tensor_l(e):
        for tt in range(NTT):
            if tt >= 1:
                e.wait_ge(kb.sems["dve"], ev_tick[tt - 1])
            for fc in range(FC):
                ins = e.matmul(out=ps[0][:, 0:8],
                               lhsT=xhat[:, fc * TOK + tt * 128:fc * TOK + tt * 128 + 128],
                               rhs=g1_sb[:, fc * 8:fc * 8 + 8],
                               start=(fc == 0), stop=(fc == FC - 1))
            ins.then_inc(kb.sems["pe"], 1)
            kb.inc("pe")
    kb.sec("tensor", f_tensor_l)

    def f_scalar_l(e):
        for tt in range(NTT):
            e.wait_ge(kb.sems["dve"], dv0 + tt * 12 + 8)
            e.activation(out=a1[:], in_=lg[:], func=AF.Exp,
                         bias=t2[:, 0:1]).then_inc(kb.sems["act"], 1)
            e.activation(out=a2[:], in_=t5[:, 0:1], func=AF.Exp,
                         bias=t2[:, 0:1]).then_inc(kb.sems["act"], 1)
    kb.sec("scalar", f_scalar_l)

    def f_vector_l(e):
        for tt in range(NTT):
            base = dv0 + tt * 12
            e.wait_ge(kb.sems["pe"], pe0 + tt + 1)

            def swx(inst, k, base=base, e=e):
                inst.then_inc(kb.sems["dve"], 1)
                e.wait_ge(kb.sems["dve"], base + k)
            swx(e.tensor_copy(out=lg[:], in_=ps[0][:, 0:8]), 1)
            swx(e.tensor_reduce(out=t1[:, 0:1], in_=lg[:], axis=AX, op=OP.max), 2)
            swx(e.tensor_scalar(out=t2[:, 0:1], in0=t1[:, 0:1], scalar1=-1.0,
                                scalar2=None, op0=OP.mult), 3)
            swx(e.tensor_scalar(out=t3[:], in0=lg[:], scalar1=t1[:, 0:1],
                                scalar2=1.0e9, op0=OP.is_equal, op1=OP.mult), 4)
            swx(e.tensor_tensor(out=t4[:], in0=lg[:], in1=t3[:], op=OP.subtract), 5)
            swx(e.tensor_reduce(out=t5[:, 0:1], in_=t4[:], axis=AX, op=OP.max), 6)
            swx(e.tensor_scalar(out=t6[:], in0=lg[:], scalar1=t5[:, 0:1],
                                scalar2=None, op0=OP.is_ge), 7)
            e.nop().then_inc(kb.sems["dve"], 1)
            e.wait_ge(kb.sems["act"], ac0 + 2 * (tt + 1))
            swx(e.tensor_scalar(out=t7[:, 0:1], in0=a2[:, 0:1], scalar1=1.0,
                                scalar2=None, op0=OP.add), 9)
            swx(e.reciprocal(out=t8[:, 0:1], in_=t7[:, 0:1]), 10)
            swx(e.tensor_tensor(out=t9[:], in0=a1[:], in1=t6[:], op=OP.mult), 11)
            swx(e.tensor_scalar(out=cwall[:, tt * 8:(tt + 1) * 8], in0=t9[:],
                                scalar1=t8[:, 0:1], scalar2=None, op0=OP.mult), 12)
    kb.sec("vector", f_vector_l)
    kb.barrier()

    pe1 = kb.cnt["pe"]
    dv1 = kb.cnt["dve"]

    def f_tensor_c(e):
        for tt in range(NTT):
            if tt >= 2:
                e.wait_ge(kb.sems["dve"], dv1 + tt - 1)
            e.transpose(out=ps[4 + tt % 2][0:8, 0:128], in_=cwall[:, tt * 8:(tt + 1) * 8],
                        identity=ident[:]).then_inc(kb.sems["pe"], 1)
            kb.inc("pe")
    kb.sec("tensor", f_tensor_c)

    def f_vector_c(e):
        for tt in range(NTT):
            e.wait_ge(kb.sems["pe"], pe1 + tt + 1)
            e.tensor_copy(out=cwT[0:8, tt * 128:(tt + 1) * 128],
                          in_=ps[4 + tt % 2][0:8, 0:128]).then_inc(kb.sems["dve"], 1)
            kb.inc("dve")
    kb.sec("vector", f_vector_c)
    kb.barrier()

    pe2 = kb.cnt["pe"]
    dv2 = kb.cnt["dve"]

    def f_tensor_t(e):
        for tc in range(NTC):
            if tc >= 1:
                e.wait_ge(kb.sems["dve"], dv2 + 2 * tc)
            e.matmul(out=ps[4][:, 0:TC], lhsT=rep_sb[0:8, :],
                     rhs=cwT[0:8, tc * TC:(tc + 1) * TC], start=True, stop=True
                     ).then_inc(kb.sems["pe"], 1)
            kb.inc("pe")
            for fc in range(FC):
                ins = e.matmul(out=ps[tc % 2][:, 0:TC],
                               lhsT=at_sb[:, fc * 128:(fc + 1) * 128],
                               rhs=xhat[:, fc * TOK + tc * TC:fc * TOK + tc * TC + TC],
                               start=(fc == 0), stop=(fc == FC - 1))
            ins.then_inc(kb.sems["pe"], 1)
            kb.inc("pe")
    kb.sec("tensor", f_tensor_t)

    def f_vector_t(e):
        for tc in range(NTC):
            e.wait_ge(kb.sems["pe"], pe2 + 2 * (tc + 1))
            e.tensor_copy(out=cwt_sb[:, tc * TC:(tc + 1) * TC],
                          in_=ps[4][:, 0:TC]).then_inc(kb.sems["dve"], 1)
            e.wait_ge(kb.sems["dve"], kb.inc("dve"))
            e.tensor_tensor(out=tc_sb[:, tc * TC:(tc + 1) * TC],
                            in0=ps[tc % 2][:, 0:TC],
                            in1=cwt_sb[:, tc * TC:(tc + 1) * TC],
                            op=OP.mult).then_inc(kb.sems["dve"], 1)
            kb.inc("dve")
    kb.sec("vector", f_vector_t)
    kb.barrier()

    pe3 = kb.cnt["pe"]
    dv3 = kb.cnt["dve"]
    WT = DS(kb, ("g0", "g1"))
    WB = DS(kb, ("g2", "g3", "g4", "g5"))
    RS = DS(kb, ("l2", "l3"))

    def mm_tick(i):
        return pe3 + i + 1
    kb.cnt["pe"] = pe3 + 64

    def ad_tick(i):
        return dv3 + i + 1
    kb.cnt["dve"] = dv3 + 64

    def f_gp(e):
        ops = []
        for og in range(4):
            ops.append(("ld", og))
            if og >= 1:
                for i in range((og - 1) * 16, og * 16):
                    ops.append(("st", i))
        for i in range(48, 64):
            ops.append(("st", i))
        for kind, v in ops:
            if kind == "ld":
                if v >= 2:
                    e.wait_ge(kb.sems["pe"], mm_tick((v - 2) * 16 + 15))
                WT.issue(e, out=wtb[:, (v % 2) * 8192:(v % 2) * 8192 + 8192],
                         in_=wt1_pm[v * 128:(v + 1) * 128, :])
            else:
                e.wait_ge(kb.sems["dve"], ad_tick(v))
                og, rest = v // 16, v % 16
                oc = og * 4 + rest // 4
                tc = rest % 4
                WB.issue(e, out=resid[oc * 128:(oc + 1) * 128, tc * TC:(tc + 1) * TC],
                         in_=rosb[:, (v % 4) * TC:(v % 4) * TC + TC])
    kb.sec("gpsimd", f_gp)

    def f_sync(e):
        for i in range(64):
            if i >= 2:
                e.wait_ge(kb.sems["dve"], ad_tick(i - 2))
            og, rest = i // 16, i % 16
            oc = og * 4 + rest // 4
            tc = rest % 4
            RS.issue(e, out=rsb[:, (i % 4) * TC:(i % 4) * TC + TC],
                     in_=resid[oc * 128:(oc + 1) * 128, tc * TC:(tc + 1) * TC])
    kb.sec("sync", f_sync)

    def f_tensor_d(e):
        for og in range(4):
            WT.w(e, og)
            for ocr in range(4):
                for tc in range(NTC):
                    i = (og * 4 + ocr) * 4 + tc
                    oc = og * 4 + ocr
                    if i >= 2:
                        e.wait_ge(kb.sems["dve"], ad_tick(i - 2))
                    for fc in range(FC):
                        e.matmul(out=ps[i % 2][:, 0:TC],
                                 lhsT=wtb[:, (og % 2) * 8192 + fc * 512 + ocr * 128:(og % 2) * 8192 + fc * 512 + ocr * 128 + 128],
                                 rhs=xhat[:, fc * TOK + tc * TC:fc * TOK + tc * TC + TC],
                                 start=(fc == 0), stop=False)
                    ins = e.matmul(out=ps[i % 2][:, 0:TC],
                                   lhsT=bs_sb[:, oc * 128:(oc + 1) * 128],
                                   rhs=tc_sb[:, tc * TC:(tc + 1) * TC],
                                   start=False, stop=True)
                    ins.then_inc(kb.sems["pe"], 1)
    kb.sec("tensor", f_tensor_d)

    def f_vector_d(e):
        for i in range(64):
            e.wait_ge(kb.sems["pe"], mm_tick(i))
            RS.w(e, i)
            if i >= 4:
                WB.w(e, i - 4)
            e.tensor_tensor(out=rosb[:, (i % 4) * TC:(i % 4) * TC + TC],
                            in0=ps[i % 2][:, 0:TC],
                            in1=rsb[:, (i % 4) * TC:(i % 4) * TC + TC],
                            op=OP.add).then_inc(kb.sems["dve"], 1)
    kb.sec("vector", f_vector_d)
from contextlib import ExitStack


def build_program(mode, dumps=(), cc_mode="cc"):
    nc = bass.Bass("TRN2", target_bir_lowering=False, num_swdge_queues=4)
    io = {}

    def din(name, shape, dt):
        io[name] = nc.dram_tensor(name, shape, dt, kind="ExternalInput").ap()
        return io[name]

    def dout(name, shape, dt):
        io[name] = nc.dram_tensor(name, shape, dt, kind="ExternalOutput").ap()
        return io[name]

    def dint(name, shape, dt):
        io[name] = nc.dram_tensor(name, shape, dt, kind="Internal").ap()
        return io[name]

    din("c_ident", [128, 128], F32)
    din("c_ones_bf", [128, 1], BF16)
    din("c_ones_f32", [1, 128], F32)
    din("c_onescol", [128, 1], F32)
    din("c_isq", [128, 1], F32)
    din("c_midbias", [128, 1], F32)
    din("c_kposB", [128, NKT], F32)
    din("c_qposB", [1, TOK], F32)
    din("c_rep", [8, 128], BF16)
    din("c_qw0", [128, 1], F32)
    din("c_qw1", [128, 1], F32)

    if mode == "launch1":
        din("x", [TOK, D], F32)
        dint("resid", [D, TOK], F32)
        din("wq_pm", [D, D], F32)
        din("wk_pm", [D, D], F32)
        din("wv_pm", [4 * 128, FC * 512], F32)
        din("bq_pm", [128, FC], F32)
        din("bk_pm", [128, FC], F32)
        din("bv_row", [1, D], F32)
        dout("qbuf", [D, TOK], BF16)
        dout("cin", [2 * D, TOK], BF16)
    elif mode == "launch2":
        din("x", [TOK, D], F32)
        din("qx", [128, H * TOK], BF16)
        din("kvfull", [2 * 2 * D, TOK], BF16)
        din("wo", [D, D], F32)
        din("bo_pm", [128, FC], F32)
        din("gate_pm", [D, IFF], I32)
        din("up_pm", [D, IFF], I32)
        din("down_pm", [IFF, D], I32)
        din("sg_pm", [128, 64], F32)
        din("su_pm", [128, 64], F32)
        din("sd_pm", [128, 16], F32)
        din("wq1_pm", [D, D], F32)
        din("wk1_pm", [128, D], F32)
        din("wv1_pm", [128, D], F32)
        din("bq1_pm", [128, FC], F32)
        din("bk1_pm", [128, 1], F32)
        din("bv1_row", [1, DH], F32)
        dout("resid", [D, TOK], F32)
        dout("q1buf", [D, TOK], BF16)
        dout("cin1", [2 * DH, TOK], BF16)
        dint("abuf", [D, TOK], BF16)
        dint("h_scr", [128, (IFF // 128) * TOK], BF16)
    elif mode == "launch3":
        din("resid2", [D, TOK], F32)
        din("q1buf", [D, TOK], BF16)
        din("kvfull1", [2 * 2 * DH, TOK], BF16)
        din("wo1", [D, D], F32)
        din("bo1_pm", [128, FC], F32)
        din("wt1_pm", [512, 8192], F32)
        din("g1_pm", [128, 128], F32)
        din("at_pm", [128, 2048], F32)
        din("bs_pm", [128, 2048], F32)
        dout("y", [TOK, D], F32)
        dint("resid", [D, TOK], F32)
        dint("abuf", [D, TOK], BF16)
    elif mode == "fused":
        din("x", [TOK, D], F32)
        din("wq_pm", [D, D], F32)
        din("wk_pm", [D, D], F32)
        din("wv_pm", [4 * 128, FC * 512], F32)
        din("bq_pm", [128, FC], F32)
        din("bk_pm", [128, FC], F32)
        din("bv_row", [1, D], F32)
        din("wo", [D, D], F32)
        din("bo_pm", [128, FC], F32)
        din("gate_pm", [D, IFF], I32)
        din("up_pm", [D, IFF], I32)
        din("down_pm", [IFF, D], I32)
        din("sg_pm", [128, 64], F32)
        din("su_pm", [128, 64], F32)
        din("sd_pm", [128, 16], F32)
        din("wq1_pm", [D, D], F32)
        din("wk1_pm", [128, D], F32)
        din("wv1_pm", [128, D], F32)
        din("bq1_pm", [128, FC], F32)
        din("bk1_pm", [128, 1], F32)
        din("bv1_row", [1, DH], F32)
        din("wo1", [D, D], F32)
        din("bo1_pm", [128, FC], F32)
        din("wt1_pm", [512, 8192], F32)
        din("g1_pm", [128, 128], F32)
        din("at_pm", [128, 2048], F32)
        din("bs_pm", [128, 2048], F32)
        dout("y", [TOK, D], F32)
        dint("resid", [D, TOK], F32)
        dint("qbuf", [D, TOK], BF16)
        dint("abuf", [D, TOK], BF16)
        dint("h_scr", [128, (IFF // 128) * TOK], BF16)
        io["cin"] = nc.dram_tensor("cin", [3 * D, TOK], BF16).ap()
        io["cout"] = nc.dram_tensor("cout", [2 * 3 * D, TOK], BF16).ap()
        io["cin1"] = nc.dram_tensor("cin1", [2 * DH, TOK], BF16).ap()
        io["cout1"] = nc.dram_tensor("cout1", [2 * 2 * DH, TOK], BF16).ap()
    else:
        raise ValueError(mode)

    with ExitStack() as stack:
        ec = stack.enter_context
        xhat = ec(nc.sbuf_tensor("xhat", [128, FC * TOK], BF16))
        AB1 = ec(nc.sbuf_tensor("AB1", [128, 32768], BF16))
        AB2 = ec(nc.sbuf_tensor("AB2", [128, 8192], BF16))
        AF1 = ec(nc.sbuf_tensor("AF1", [128, 4096], F32))
        AF2 = ec(nc.sbuf_tensor("AF2", [128, 4096], F32))
        PT = ec(nc.sbuf_tensor("PT", [128, 2560], BF16))
        QS2 = ec(nc.sbuf_tensor("QS2", [128, 8192], BF16))
        AI32 = QS2[:].bitcast(I32)
        ident = ec(nc.sbuf_tensor("ident", [128, 128], F32))
        ones_bf = ec(nc.sbuf_tensor("ones_bf", [128, 1], BF16))
        ones_f32 = ec(nc.sbuf_tensor("ones_f32", [1, 128], F32))
        onescol = ec(nc.sbuf_tensor("onescol", [128, 1], F32))
        isq_vec = ec(nc.sbuf_tensor("isq_vec", [128, 1], F32))
        midbias = ec(nc.sbuf_tensor("midbias", [128, 1], F32))
        kposB = ec(nc.sbuf_tensor("kposB", [128, NKT], F32))
        rep_sb = ec(nc.sbuf_tensor("rep_sb", [8, 128], BF16))
        qw0 = ec(nc.sbuf_tensor("qw0", [128, 1], F32))
        qw1 = ec(nc.sbuf_tensor("qw1", [128, 1], F32))
        bvec = ec(nc.sbuf_tensor("bvec", [128, 256], F32))
        brow2 = AF2[0:1, 0:2048]
        ps = [ec(nc.psum_tensor(f"ps{i}", [128, 512], F32)) for i in range(7)]
        sems = {k: ec(nc.semaphore(f"s_{k}")) for k in ALLSEMS}
        blk = ec(nc.Block())

        kb = KB(nc, blk, sems)
        consts = {"ones_bf": ones_bf, "ones_f32": ones_f32, "onescol": onescol,
                  "isq_vec": isq_vec,
                  "midbias": midbias, "kposB": kposB, "qposB": AF2[0:1, 2048:4096],
                  "qposB_dram": io["c_qposB"], "ident": ident,
                  "qw0": qw0, "qw1": qw1}

        CL = DS(kb, ("l0", "l1"))

        def f_consts(e):
            pairs = [(ident[:], io["c_ident"]), (ones_bf[:], io["c_ones_bf"]),
                     (ones_f32[:], io["c_ones_f32"]), (onescol[:], io["c_onescol"]),
                     (isq_vec[:], io["c_isq"]),
                     (midbias[:], io["c_midbias"]), (kposB[:], io["c_kposB"]),
                     (rep_sb[:], io["c_rep"]), (qw0[:], io["c_qw0"]),
                     (qw1[:], io["c_qw1"])]
            bv_map = [("bq_pm", 0, FC), ("bk_pm", 16, FC), ("bo_pm", 32, FC),
                      ("sg_pm", 48, 64), ("su_pm", 112, 64), ("sd_pm", 176, 16),
                      ("bq1_pm", 192, FC), ("bk1_pm", 208, 1), ("bo1_pm", 209, FC)]
            for nm, off, wdt_ in bv_map:
                if nm in io:
                    pairs.append((bvec[:, off:off + wdt_], io[nm]))
            for dst, srcap in pairs:
                CL.issue(e, out=dst, in_=srcap, allow_slow_non_contiguous=True)
        kb.sec("sync", f_consts)
        kb.barrier()

        sb_tp = (AF1[:, 0:4096], AF2[:, 0:4096])
        sb_ln = (AF1[:, 0:1024], AB2[:, 0:1024], AB2[:, 1024:2048],
                 AF2[0:1, 0:2048], AF2[0:1, 2048:4096], AF1[0:1, 2048:4096],
                 AF1[:, 1024:2048])
        sb_proj = (AB1[:, 24576:28672], AB1[:, 28672:32768])
        sb_at = (AB1[:, 0:8192], AB1[:, 8192:16384], AB2[:, 0:4096], PT[:, 0:1024],
                 AB1[:, 16384:32768], AF1[:, 0:1024], AB2[:, 4096:8192],
                 AF1[:, 1536:2048], AF1[:, 1024:1536], AF2[0:1, 0:TOK],
                 PT[:, 1024:2048], PT[:, 2048:2560], AF1[:, 2048:4096])
        sb_atf = (AB1[:, 0:8192], AB1[:, 8192:16384], QS2[:, 0:8192], PT[:, 0:1024],
                  AB1[:, 16384:32768], AF1[:, 0:1024], AB2[:, 4096:8192],
                  AF1[:, 1536:2048], AF1[:, 1024:1536], AF2[0:1, 0:TOK],
                  PT[:, 1024:2048], PT[:, 2048:2560], AF1[:, 2048:4096])
        sb_wo = (AB1[:, 0:16384], AB1[:, 16384:32768], AF1[:, 0:2048], AF1[:, 2048:4096])

        def load_bias(pairs):
            BL = DS(kb, ("l0", "l1"))

            def f_b(e):
                for dst, srcap in pairs:
                    BL.issue(e, out=dst, in_=srcap)
            kb.sec("sync", f_b)
            kb.barrier()

        if mode == "launch1":
            ph_transpose_td(kb, io["x"], io["resid"], sb_tp, ps, NTT, ident)
            kb.barrier()
            ph_ln_v2(kb, io["resid"], xhat, sb_ln, ps, consts)
            kb.barrier()
            load_bias([(bvec[:, 0:FC], io["bq_pm"]), (bvec[:, FC:2 * FC], io["bk_pm"]),
                       (brow2[0:1, 0:D], io["bv_row"])])
            ph_proj_fm(kb, xhat, io["wq_pm"], bvec[:, 0:FC], FC, io["qbuf"], 0, sb_proj, ps)
            kb.barrier()
            ph_proj_fm(kb, xhat, io["wk_pm"], bvec[:, FC:2 * FC], FC, io["cin"], 0, sb_proj, ps)
            kb.barrier()
            sb_v = (AB1[:, 0:16384], AB1[:, 16384:24576], AF1[:, 0:512])
            ph_proj_v(kb, xhat, io["wv_pm"], brow2[0:1, 0:D], 4, 512, io["cin"], D, sb_v, ps, consts)
            kb.barrier()
        elif mode == "launch2":
            ph_transpose_td(kb, io["x"], io["resid"], sb_tp, ps, NTT, ident)
            kb.barrier()
            ph_attn(kb, 0, io["qx"], io["kvfull"], io["abuf"], sb_at, ps, consts)
            kb.barrier()
            load_bias([(bvec[:, 0:FC], io["bo_pm"])])
            ph_wo(kb, io["abuf"], io["wo"], bvec[:, 0:FC], io["resid"], sb_wo, ps)
            kb.barrier()
            if "resid_mid" in dumps:
                io["resid_mid"] = nc.dram_tensor("resid_mid_d", [D, TOK], F32,
                                                 kind="ExternalOutput").ap()

                DM = DS(kb, ("g0",))

                def f_dm(e):
                    DM.issue(e, out=io["resid_mid"], in_=io["resid"])
                kb.sec("gpsimd", f_dm)
                kb.barrier()
            ph_ln_v2(kb, io["resid"], xhat, sb_ln, ps, consts)
            kb.barrier()
            load_bias([(bvec[:, 0:64], io["sg_pm"]), (bvec[:, 64:128], io["su_pm"]),
                       (bvec[:, 128:144], io["sd_pm"])])
            sb_fa = (AI32[:, 0:4096], AB1[:, 0:32768],
                     AB2[:, 0:1024], AB2[:, 1024:2048], AB2[:, 2048:6144])
            ph_ffn0_a(kb, xhat, io["gate_pm"], io["up_pm"], io["h_scr"],
                      bvec[:, 0:64], bvec[:, 64:128], sb_fa, ps)
            kb.barrier()
            sb_fb = (AB1[:, 0:32768], xhat[:, 0:32768], AB2[:, 0:8192],
                     AF1[:, 0:2048], AF1[:, 2048:4096], AF2[:, 1024:1536])
            consts["hsr"] = AF2[0:1, 0:512]
            ph_ffn0_b(kb, io["h_scr"], io["down_pm"], bvec[:, 128:144], io["resid"], sb_fb, ps, consts)
            kb.barrier()
            ph_ln_v2(kb, io["resid"], xhat, sb_ln, ps, consts)
            kb.barrier()
            load_bias([(bvec[:, 0:FC], io["bq1_pm"]), (bvec[:, FC:FC + 1], io["bk1_pm"]),
                       (brow2[0:1, 0:DH], io["bv1_row"])])
            ph_proj_fm(kb, xhat, io["wq1_pm"], bvec[:, 0:FC], FC, io["q1buf"], 0, sb_proj, ps)
            kb.barrier()
            ph_proj_fm(kb, xhat, io["wk1_pm"], bvec[:, FC:FC + 1], 1, io["cin1"], 0, sb_proj, ps)
            kb.barrier()
            sb_v = (AB1[:, 0:16384], AB1[:, 16384:24576], AF1[:, 0:512])
            ph_proj_v(kb, xhat, io["wv1_pm"], brow2[0:1, 0:DH], 1, 128, io["cin1"], DH, sb_v, ps, consts)
            kb.barrier()
        elif mode == "launch3":
            CP = DS(kb, ("g0", "g1"))

            def f_cp(e):
                for fc in range(FC):
                    CP.issue(e, out=io["resid"][fc * 128:(fc + 1) * 128, :],
                             in_=io["resid2"][fc * 128:(fc + 1) * 128, :])
            kb.sec("gpsimd", f_cp)
            kb.barrier()
            ph_attn(kb, 1, io["q1buf"], io["kvfull1"], io["abuf"], sb_at, ps, consts)
            kb.barrier()
            load_bias([(bvec[:, 0:FC], io["bo1_pm"])])
            ph_wo(kb, io["abuf"], io["wo1"], bvec[:, 0:FC], io["resid"], sb_wo, ps)
            kb.barrier()
            ph_ln_v2(kb, io["resid"], xhat, sb_ln, ps, consts)
            kb.barrier()
            sb_f1 = (AB1[:, 0:16384], AB1[:, 16384:18432], AB1[:, 18432:20480],
                     AB1[:, 24576:24704], AB2[0:8, 0:2048], AB1[:, 20480:22528],
                     AB1[:, 22528:24576], AF1[:, 0:2048], AF1[:, 2048:4096],
                     (AF2[:, 0:8], AF2[:, 16:17], AF2[:, 17:18], AF2[:, 18:19],
                      AF2[:, 19:20], AF2[:, 20:21], AF2[:, 21:22], AF2[:, 24:32],
                      AF2[:, 32:40], AF2[:, 40:48], AF2[:, 48:56], AF2[:, 56:64],
                      AF2[:, 64:192]))
            ph_ffn1(kb, xhat, io["wt1_pm"], io["g1_pm"], io["at_pm"], io["bs_pm"],
                    rep_sb, io["resid"], sb_f1, ps, consts)
            kb.barrier()
            ph_transpose_td(kb, io["resid"], io["y"], sb_tp, ps, FC, ident)
            kb.barrier()

        if mode == "fused":
            RG = [[0, 1], [2, 3], [4, 5], [6, 7]]

            def ph_ag(cin_ap, cout_ap, which):
                def f_ag(e):
                    if cc_mode == "cc" or cc_mode == which or cc_mode.startswith("prefix"):
                        e.collective_compute("AllGather", OP.bypass, replica_groups=RG,
                                             ins=[cin_ap.opt()], outs=[cout_ap.opt()]
                                             ).then_inc(kb.sems["cc"], 1)
                        kb.inc("cc", 1)
                    else:
                        nrow = cin_ap.shape[0]
                        for rr in range(2):
                            e.dma_start(out=cout_ap[rr * nrow:(rr + 1) * nrow, :],
                                        in_=cin_ap[:, :]).then_inc(kb.sems["cc"], 16)
                            kb.inc("cc", 16)
                kb.sec("gpsimd", f_ag)
                kb.barrier()

            if cc_mode != "prefix0":
                ph_transpose_td(kb, io["x"], io["resid"], sb_tp, ps, NTT, ident)
                kb.barrier()
                ph_ln_v2(kb, io["resid"], xhat, sb_ln, ps, consts)
                kb.barrier()
                load_bias([(brow2[0:1, 0:D], io["bv_row"])])
                ph_proj_fm(kb, xhat, io["wq_pm"], bvec[:, 0:FC], FC, io["cin"], 2 * D, sb_proj, ps)
                kb.barrier()
                ph_proj_fm(kb, xhat, io["wk_pm"], bvec[:, 16:32], FC, io["cin"], 0, sb_proj, ps)
                kb.barrier()
                sb_v = (AB1[:, 0:16384], AB1[:, 16384:24576], AF1[:, 0:512])
                ph_proj_v(kb, xhat, io["wv_pm"], brow2[0:1, 0:D], 4, 512, io["cin"], D, sb_v, ps, consts)
                kb.barrier()
            ph_ag(io["cin"], io["cout"], "ag0")
            if cc_mode.startswith("prefix"):
                ph_transpose_td(kb, io["resid"], io["y"], sb_tp, ps, FC, ident)
                kb.barrier()
                fused_rest = False
            else:
                fused_rest = True
            if fused_rest and True:
                ph_attn(kb, 0, io["cout"], io["cout"], io["abuf"], sb_atf, ps, consts, qmode="fused")
                kb.barrier()
            if fused_rest:
                ph_wo(kb, io["abuf"], io["wo"], bvec[:, 32:48], io["resid"], sb_wo, ps)
                kb.barrier()
                ph_ln_v2(kb, io["resid"], xhat, sb_ln, ps, consts)
                kb.barrier()
                sb_fa = (AI32[:, 0:4096], AB1[:, 0:32768],
                         AB2[:, 0:1024], AB2[:, 1024:2048], AB2[:, 2048:6144])
                ph_ffn0_a(kb, xhat, io["gate_pm"], io["up_pm"], io["h_scr"],
                          bvec[:, 48:112], bvec[:, 112:176], sb_fa, ps)
                kb.barrier()
                sb_fb = (AB1[:, 0:32768], xhat[:, 0:32768], AB2[:, 0:8192],
                         AF1[:, 0:2048], AF1[:, 2048:4096], AF2[:, 1024:1536])
                consts["hsr"] = AF2[0:1, 0:512]
                ph_ffn0_b(kb, io["h_scr"], io["down_pm"], bvec[:, 176:192], io["resid"], sb_fb, ps, consts)
                kb.barrier()
                ph_ln_v2(kb, io["resid"], xhat, sb_ln, ps, consts)
                kb.barrier()
                load_bias([(brow2[0:1, 0:DH], io["bv1_row"])])
                ph_proj_fm(kb, xhat, io["wq1_pm"], bvec[:, 192:208], FC, io["qbuf"], 0, sb_proj, ps)
                kb.barrier()
                ph_proj_fm(kb, xhat, io["wk1_pm"], bvec[:, 208:209], 1, io["cin1"], 0, sb_proj, ps)
                kb.barrier()
                sb_v = (AB1[:, 0:16384], AB1[:, 16384:24576], AF1[:, 0:512])
                ph_proj_v(kb, xhat, io["wv1_pm"], brow2[0:1, 0:DH], 1, 128, io["cin1"], DH, sb_v, ps, consts)
                kb.barrier()
                ph_ag(io["cin1"], io["cout1"], "ag1")
                ph_attn(kb, 1, io["qbuf"], io["cout1"], io["abuf"], sb_at, ps, consts)
                kb.barrier()
                ph_wo(kb, io["abuf"], io["wo1"], bvec[:, 209:225], io["resid"], sb_wo, ps)
                kb.barrier()
                ph_ln_v2(kb, io["resid"], xhat, sb_ln, ps, consts)
                kb.barrier()
                sb_f1 = (AB1[:, 0:16384], AB1[:, 16384:18432], AB1[:, 18432:20480],
                         AB1[:, 24576:24704], AB2[0:8, 0:2048], AB1[:, 20480:22528],
                         AB1[:, 22528:24576], AF1[:, 0:2048], AF1[:, 2048:4096],
                         (AF2[:, 0:8], AF2[:, 16:17], AF2[:, 17:18], AF2[:, 18:19],
                          AF2[:, 19:20], AF2[:, 20:21], AF2[:, 21:22], AF2[:, 24:32],
                          AF2[:, 32:40], AF2[:, 40:48], AF2[:, 48:56], AF2[:, 56:64],
                          AF2[:, 64:192]))
                ph_ffn1(kb, xhat, io["wt1_pm"], io["g1_pm"], io["at_pm"], io["bs_pm"],
                        rep_sb, io["resid"], sb_f1, ps, consts)
                kb.barrier()
                ph_transpose_td(kb, io["resid"], io["y"], sb_tp, ps, FC, ident)
                kb.barrier()

        for name in dumps:
            if name == 'resid_mid':
                continue
            src = io[name]
            dst = dout(name + "_d", list(src.tensor.shape), src.tensor.dtype)

            DD = DS(kb, ("g0",))

            def f_dump(e, src=src, dst=dst, DD=DD):
                DD.issue(e, out=dst, in_=src)
            kb.sec("gpsimd", f_dump)
            kb.barrier()

        def f_final(e):
            for nm in DMASEMS:
                if kb.cnt[nm] > 0:
                    e.wait_ge(kb.sems[nm], kb.cnt[nm])
        kb.sec("gpsimd", f_final)

    return nc


def pm4(w, n_oc, ocw):
    return np.ascontiguousarray(
        w.reshape(FC, 128, n_oc, ocw).transpose(2, 1, 0, 3).reshape(n_oc * 128, FC * ocw))


def make_consts(core):
    hlf = core % 2
    qpos = (np.arange(TOK, dtype=np.float64) + hlf * TOK) * 1e6
    kpos = (np.arange(SEQ, dtype=np.float64)) * 1e6
    rep = np.zeros((8, 128), np.float32)
    for ee in range(8):
        rep[ee, ee * 16:(ee + 1) * 16] = 16.0
    return {
        "c_ident": np.eye(128, dtype=np.float32),
        "c_ones_bf": np.ones((128, 1), dtype=BF16NP),
        "c_ones_f32": np.ones((1, 128), dtype=np.float32),
        "c_onescol": np.ones((128, 1), dtype=np.float32),
        "c_isq": np.full((128, 1), ISQ, np.float32),
        "c_midbias": np.full((128, 1), 0.0 if hlf == 1 else NEG, np.float32),
        "c_kposB": kpos.reshape(NKT, 128).T.astype(np.float32).copy(),
        "c_qposB": qpos.reshape(1, TOK).astype(np.float32),
        "c_rep": rep.astype(BF16NP),
        "c_qw0": np.full((128, 1), 1.0 if hlf == 0 else 0.0, np.float32),
        "c_qw1": np.full((128, 1), 1.0 if hlf == 1 else 0.0, np.float32),
    }


def host_weights(ins):
    w = {}
    w["wq_pm"] = pm4(ins["L0_attn_Wq"], 16, 128)
    w["wk_pm"] = pm4(ins["L0_attn_Wk"], 16, 128)
    w["wv_pm"] = pm4(ins["L0_attn_Wv"], 4, 512)
    w["bq_pm"] = np.ascontiguousarray(ins["L0_attn_bq"].reshape(16, 128).T)
    w["bk_pm"] = np.ascontiguousarray(ins["L0_attn_bk"].reshape(16, 128).T)
    w["bv_row"] = ins["L0_attn_bv"].reshape(1, D)
    w["wo"] = ins["L0_attn_Wo"]
    w["bo_pm"] = np.ascontiguousarray(ins["L0_attn_bo"].reshape(16, 128).T)
    w["gate_pm"] = np.ascontiguousarray(
        ins["L0_ffn_gate_w"].reshape(16, 512, 16, 128).transpose(0, 3, 2, 1).reshape(2048, 8192))
    w["up_pm"] = np.ascontiguousarray(
        ins["L0_ffn_up_w"].reshape(16, 512, 16, 128).transpose(0, 3, 2, 1).reshape(2048, 8192))
    w["down_pm"] = np.ascontiguousarray(
        ins["L0_ffn_down_w"].reshape(16, 128, 64, 128).transpose(2, 3, 0, 1).reshape(8192, 2048))
    w["sg_pm"] = np.ascontiguousarray(ins["L0_ffn_gate_scale"].reshape(64, 128).T)
    w["su_pm"] = np.ascontiguousarray(ins["L0_ffn_up_scale"].reshape(64, 128).T)
    w["sd_pm"] = np.ascontiguousarray(ins["L0_ffn_down_scale"].reshape(16, 128).T)
    w["wq1_pm"] = pm4(ins["L1_attn_Wq"], 16, 128)
    w["wk1_pm"] = pm4(ins["L1_attn_Wk"], 1, 128)
    w["wv1_pm"] = pm4(ins["L1_attn_Wv"], 1, 128)
    w["bq1_pm"] = np.ascontiguousarray(ins["L1_attn_bq"].reshape(16, 128).T)
    w["bk1_pm"] = np.ascontiguousarray(ins["L1_attn_bk"].reshape(1, 128).T)
    w["bv1_row"] = ins["L1_attn_bv"].reshape(1, DH)
    w["wo1"] = ins["L1_attn_Wo"]
    w["bo1_pm"] = np.ascontiguousarray(ins["L1_attn_bo"].reshape(16, 128).T)
    w["wt1_pm"] = np.ascontiguousarray(
        ins["L1_ffn_weight"].reshape(4, 512, 16, 128).transpose(0, 3, 2, 1).reshape(512, 8192))
    w["g1_pm"] = np.ascontiguousarray(
        ins["L1_ffn_gate_w"].reshape(8, 16, 128).transpose(2, 1, 0).reshape(128, 128))
    w["at_pm"] = np.ascontiguousarray(
        ins["L1_ffn_A_w"].reshape(128, 16, 128).transpose(2, 1, 0).reshape(128, 2048))
    w["bs_pm"] = np.ascontiguousarray(
        ins["L1_ffn_B_w"].transpose(0, 2, 1).reshape(128, 2048))
    return {k: (v.astype(np.float32) if v.dtype == np.float64 else v) for k, v in w.items()}


_BUILD_CACHE = {}


def _get_prog(mode):
    if mode not in _BUILD_CACHE:
        _BUILD_CACHE[mode] = build_program(mode)
    return _BUILD_CACHE[mode]


def kernel(**inputs):
    x = np.ascontiguousarray(inputs["L0_attn_x"]).astype(np.float32, copy=False)
    w = host_weights(inputs)
    nc = _get_prog("fused")
    wkeys = ("wq_pm", "wk_pm", "wv_pm", "bq_pm", "bk_pm", "bv_row",
             "wo", "bo_pm", "gate_pm", "up_pm", "down_pm", "sg_pm", "su_pm", "sd_pm",
             "wq1_pm", "wk1_pm", "wv1_pm", "bq1_pm", "bk1_pm", "bv1_row",
             "wo1", "bo1_pm", "wt1_pm", "g1_pm", "at_pm", "bs_pm")
    maps = []
    for c in range(8):
        b, hlf = c // 2, c % 2
        m = {"x": np.ascontiguousarray(x[b, hlf * TOK:(hlf + 1) * TOK, :])}
        m.update({k: w[k] for k in wkeys})
        m.update(make_consts(c))
        maps.append(m)
    res = run_bass_kernel_spmd(nc, maps, core_ids=list(range(8)))
    y = np.zeros((4, SEQ, D), np.float32)
    for c in range(8):
        b, hlf = c // 2, c % 2
        y[b, hlf * TOK:(hlf + 1) * TOK, :] = res.results[c]["y"]
    return y


# version 08
# speedup vs baseline: 1.2460x; 1.0443x over previous
import numpy as np
import ml_dtypes
import concourse.bass as bass
import concourse.mybir as mybir
from concourse.bass_utils import run_bass_kernel_spmd

F32 = mybir.dt.float32
BF16 = mybir.dt.bfloat16
I32 = mybir.dt.int32
AF = mybir.ActivationFunctionType
OP = mybir.AluOpType
AX = mybir.AxisListType.X
BF16NP = ml_dtypes.bfloat16

D = 2048
FC = 16
TOK = 2048
SEQ = 4096
H = 16
DH = 128
IFF = 8192
TC = 512
NTC = TOK // TC
NTT = TOK // 128
NKT = SEQ // 128
E = 8
R = 16
EPS = 1e-5
ISQ = float(1.0 / np.sqrt(128.0))
NEG = -1.0e9

ENGS = ("sync", "tensor", "vector", "scalar", "gpsimd")
LSEMS = ("l0", "l1", "l2", "l3", "l4", "l5", "l6", "l7")
GSEMS = ("g0", "g1", "g2", "g3", "g4", "g5")
ALLSEMS = LSEMS + GSEMS + ("pe", "dve", "act", "cc")
DMASEMS = LSEMS + GSEMS + ("cc",)


class KB:

    def __init__(self, nc, blk, sems):
        self.nc = nc
        self.blk = blk
        self.sems = sems
        self.cnt = {k: 0 for k in sems}
        self.snap = {k: 0 for k in sems}
        self.waited = {e: {k: 0 for k in sems} for e in ENGS}

    def inc(self, name, k=1):
        self.cnt[name] += k
        return self.cnt[name]

    def barrier(self):
        self.snap = dict(self.cnt)

    def sec(self, eng, f):
        kb = self

        def wrapped(e):
            for k, v in kb.snap.items():
                if v > 0 and kb.waited[eng][k] < v:
                    e.wait_ge(kb.sems[k], v)
                    kb.waited[eng][k] = v
            f(e)

        getattr(self.blk, eng)(wrapped)


class DS:

    def __init__(self, kb, names):
        self.kb = kb
        self.names = tuple(names)
        self.k = 0
        self.base = {nm: kb.cnt[nm] for nm in self.names}

    def token(self, k):
        nm = self.names[k % len(self.names)]
        return (nm, self.base[nm] + 16 * (k // len(self.names) + 1))

    def issue(self, e, out=None, in_=None, **kw):
        nm, v = self.token(self.k)
        self.k += 1
        e.dma_start(out=out, in_=in_, **kw).then_inc(self.kb.sems[nm], 16)
        got = self.kb.inc(nm, 16)
        assert got == v, (nm, got, v)
        return (nm, v)

    def w(self, e, k):
        nm, v = self.token(k)
        e.wait_ge(self.kb.sems[nm], v)


def ph_transpose_td(kb, src, dst, sb, ps, nt_src, ident):
    nf = src.shape[1] // 128
    xin, tout = sb
    W = nf * 128
    LD = DS(kb, ("l0", "l1"))
    ST = DS(kb, ("g0", "g1"))
    pe0 = kb.cnt["pe"]
    dv0 = kb.cnt["dve"]

    def f_sync(e):
        for tt in range(nt_src):
            if tt >= 2:
                e.wait_ge(kb.sems["pe"], pe0 + nf * (tt - 1))
            LD.issue(e, out=xin[:, (tt % 2) * W:(tt % 2) * W + W],
                     in_=src[tt * 128:(tt + 1) * 128, :])
    kb.sec("sync", f_sync)

    def f_tensor(e):
        for tt in range(nt_src):
            LD.w(e, tt)
            for f in range(nf):
                k = tt * nf + f
                if k >= 2:
                    e.wait_ge(kb.sems["dve"], dv0 + (k - 1))
                e.transpose(out=ps[k % 2][:, 0:128],
                            in_=xin[:, (tt % 2) * W + f * 128:(tt % 2) * W + f * 128 + 128],
                            identity=ident[:]).then_inc(kb.sems["pe"], 1)
                kb.inc("pe")
    kb.sec("tensor", f_tensor)

    def f_vector(e):
        for tt in range(nt_src):
            if tt >= 2:
                ST.w(e, tt - 2)
            for f in range(nf):
                k = tt * nf + f
                e.wait_ge(kb.sems["pe"], pe0 + k + 1)
                e.tensor_copy(out=tout[:, (tt % 2) * W + f * 128:(tt % 2) * W + f * 128 + 128],
                              in_=ps[k % 2][:, 0:128]).then_inc(kb.sems["dve"], 1)
                kb.inc("dve")
    kb.sec("vector", f_vector)

    def f_gp(e):
        for tt in range(nt_src):
            e.wait_ge(kb.sems["dve"], dv0 + nf * (tt + 1))
            ST.issue(e, out=dst.rearrange("(f p) t -> p f t", p=128)[:, :, tt * 128:(tt + 1) * 128],
                     in_=tout[:, (tt % 2) * W:(tt % 2) * W + W].rearrange("p (f t) -> p f t", f=nf))
    kb.sec("gpsimd", f_gp)


def ph_ln_v2(kb, resid, xhat, sb, ps, consts):
    rfa, rb16, sq16, rowA, rowB, rowT, abc = sb
    ones_bf, ones_f32 = consts["ones_bf"], consts["ones_f32"]
    LD = DS(kb, ("l0", "l1"))
    ac0 = kb.cnt["act"]
    pe0 = kb.cnt["pe"]
    dv0 = kb.cnt["dve"]

    def f_sync(e):
        for tc in range(NTC):
            for fc in range(FC):
                k = tc * FC + fc
                if k >= 2:
                    e.wait_ge(kb.sems["act"], ac0 + 2 * (k - 1))
                LD.issue(e, out=rfa[:, (k % 2) * TC:(k % 2) * TC + TC],
                         in_=resid[fc * 128:(fc + 1) * 128, tc * TC:tc * TC + TC])
    kb.sec("sync", f_sync)

    def f_scalar(e):
        for tc in range(NTC):
            for fc in range(FC):
                k = tc * FC + fc
                LD.w(e, k)
                if k >= 2:
                    e.wait_ge(kb.sems["pe"], pe0 + 2 * (k - 1))
                e.activation(out=rb16[:, (k % 2) * TC:(k % 2) * TC + TC],
                             in_=rfa[:, (k % 2) * TC:(k % 2) * TC + TC],
                             func=AF.Copy).then_inc(kb.sems["act"], 1)
                e.activation(out=sq16[:, (k % 2) * TC:(k % 2) * TC + TC],
                             in_=rfa[:, (k % 2) * TC:(k % 2) * TC + TC],
                             func=AF.Square).then_inc(kb.sems["act"], 1)
                kb.inc("act", 2)
    kb.sec("scalar", f_scalar)

    def f_tensor(e):
        for tc in range(NTC):
            if tc >= 1:
                e.wait_ge(kb.sems["dve"], dv0 + 2 * tc)
            for fc in range(FC):
                k = tc * FC + fc
                e.wait_ge(kb.sems["act"], ac0 + 2 * (k + 1))
                st = (fc == 0)
                sp = (fc == FC - 1)
                e.matmul(out=ps[5][0:1, 0:TC], lhsT=ones_bf[:, 0:1],
                         rhs=rb16[:, (k % 2) * TC:(k % 2) * TC + TC],
                         start=st, stop=sp).then_inc(kb.sems["pe"], 1)
                kb.inc("pe")
                e.matmul(out=ps[6][0:1, 0:TC], lhsT=ones_bf[:, 0:1],
                         rhs=sq16[:, (k % 2) * TC:(k % 2) * TC + TC],
                         start=st, stop=sp).then_inc(kb.sems["pe"], 1)
                kb.inc("pe")
    kb.sec("tensor", f_tensor)

    def f_vector(e):
        for tc in range(NTC):
            e.wait_ge(kb.sems["pe"], pe0 + 2 * FC * (tc + 1))
            e.tensor_copy(out=rowA[0:1, tc * TC:tc * TC + TC],
                          in_=ps[5][0:1, 0:TC]).then_inc(kb.sems["dve"], 1)
            e.tensor_copy(out=rowB[0:1, tc * TC:tc * TC + TC],
                          in_=ps[6][0:1, 0:TC]).then_inc(kb.sems["dve"], 1)
            kb.inc("dve", 2)
    kb.sec("vector", f_vector)
    dv1 = kb.cnt["dve"]

    def f_vector2(e):
        def sw(inst):
            inst.then_inc(kb.sems["dve"], 1)
            e.wait_ge(kb.sems["dve"], kb.inc("dve"))
        e.wait_ge(kb.sems["dve"], dv1)
        sw(e.tensor_scalar(out=rowA[0:1, :], in0=rowA[0:1, :], scalar1=1.0 / D,
                           scalar2=None, op0=OP.mult))
        sw(e.tensor_scalar(out=rowB[0:1, :], in0=rowB[0:1, :], scalar1=1.0 / D,
                           scalar2=None, op0=OP.mult))
        sw(e.tensor_tensor(out=rowT[0:1, :], in0=rowA[0:1, :], in1=rowA[0:1, :],
                           op=OP.mult))
        sw(e.tensor_tensor(out=rowB[0:1, :], in0=rowB[0:1, :], in1=rowT[0:1, :],
                           op=OP.subtract))
        e.tensor_scalar(out=rowB[0:1, :], in0=rowB[0:1, :], scalar1=EPS,
                        scalar2=None, op0=OP.add).then_inc(kb.sems["dve"], 1)
        kb.inc("dve")
    kb.sec("vector", f_vector2)
    dv2 = kb.cnt["dve"]

    def f_scalar2(e):
        e.wait_ge(kb.sems["dve"], dv2)
        e.activation(out=rowT[0:1, :], in_=rowB[0:1, :],
                     func=AF.Sqrt).then_inc(kb.sems["act"], 1)
        kb.inc("act")
    kb.sec("scalar", f_scalar2)
    ac2 = kb.cnt["act"]

    def f_vector3(e):
        def sw(inst):
            inst.then_inc(kb.sems["dve"], 1)
            e.wait_ge(kb.sems["dve"], kb.inc("dve"))
        e.wait_ge(kb.sems["act"], ac2)
        sw(e.reciprocal(out=rowB[0:1, :], in_=rowT[0:1, :]))
        sw(e.tensor_tensor(out=rowA[0:1, :], in0=rowA[0:1, :], in1=rowB[0:1, :],
                           op=OP.mult))
        e.tensor_scalar(out=rowA[0:1, :], in0=rowA[0:1, :], scalar1=-1.0, scalar2=None,
                        op0=OP.mult).then_inc(kb.sems["dve"], 1)
        kb.inc("dve")
    kb.sec("vector", f_vector3)
    dv3 = kb.cnt["dve"]

    pe1 = kb.cnt["pe"]
    sched = []
    carry = None
    for tc in range(NTC):
        sched.append(("copyA", tc))
        if carry is not None:
            sched.append(("add", carry))
            carry = None
        sched.append(("copyB", tc))
        prev = None
        for fc in range(FC):
            k = tc * FC + fc
            sched.append(("mult", k))
            if prev is not None:
                sched.append(("add", prev))
            prev = k
        carry = prev
    sched.append(("add", carry))
    add_tick = {}
    copyB_tick = {}
    t = dv3
    for op, arg in sched:
        t += 1
        if op == "add":
            add_tick[arg] = t
        elif op == "copyB":
            copyB_tick[arg] = t

    def f_tensor2(e):
        e.wait_ge(kb.sems["dve"], dv3)
        for tc in range(NTC):
            if tc >= 1:
                e.wait_ge(kb.sems["dve"], copyB_tick[tc - 1])
            e.matmul(out=ps[5][:, 0:TC], lhsT=ones_f32[0:1, :],
                     rhs=rowB[0:1, tc * TC:tc * TC + TC], start=True, stop=True
                     ).then_inc(kb.sems["pe"], 1)
            kb.inc("pe")
            e.matmul(out=ps[6][:, 0:TC], lhsT=ones_f32[0:1, :],
                     rhs=rowA[0:1, tc * TC:tc * TC + TC], start=True, stop=True
                     ).then_inc(kb.sems["pe"], 1)
            kb.inc("pe")
    kb.sec("tensor", f_tensor2)

    def f_sync2(e):
        for tc in range(NTC):
            for fc in range(FC):
                k = tc * FC + fc
                if k >= 2:
                    e.wait_ge(kb.sems["dve"], add_tick[k - 2])
                elif k == 0:
                    e.wait_ge(kb.sems["dve"], dv3)
                LD.issue(e, out=rfa[:, (k % 2) * TC:(k % 2) * TC + TC],
                         in_=resid[fc * 128:(fc + 1) * 128, tc * TC:tc * TC + TC])
    kb.sec("sync", f_sync2)

    def f_vector4(e):
        for op, arg in sched:
            if op == "copyA":
                e.wait_ge(kb.sems["pe"], pe1 + 2 * (arg + 1))
                e.tensor_copy(out=abc[:, 0:TC], in_=ps[5][:, 0:TC]).then_inc(kb.sems["dve"], 1)
            elif op == "copyB":
                e.tensor_copy(out=abc[:, TC:2 * TC], in_=ps[6][:, 0:TC]).then_inc(kb.sems["dve"], 1)
            elif op == "mult":
                k = arg
                LD.w(e, 64 + k)
                e.tensor_tensor(out=rfa[:, (k % 2) * TC:(k % 2) * TC + TC],
                                in0=rfa[:, (k % 2) * TC:(k % 2) * TC + TC],
                                in1=abc[:, 0:TC], op=OP.mult).then_inc(kb.sems["dve"], 1)
            else:
                k = arg
                tc, fc = k // FC, k % FC
                e.tensor_tensor(out=xhat[:, fc * TOK + tc * TC:fc * TOK + tc * TC + TC],
                                in0=rfa[:, (k % 2) * TC:(k % 2) * TC + TC],
                                in1=abc[:, TC:2 * TC], op=OP.add).then_inc(kb.sems["dve"], 1)
            kb.inc("dve")
    kb.sec("vector", f_vector4)


def ph_proj_fm(kb, xhat, w_pm, bias_pm, n_oc, dest, dest_row0, sb, ps):
    wbf, stage = sb
    WL = DS(kb, ("g0", "g1"))
    SO = DS(kb, ("g2", "g3"))
    pe0 = kb.cnt["pe"]
    dv0 = kb.cnt["dve"]

    def mm_tick(oc, tc):
        return pe0 + oc * NTC + tc + 1
    kb.cnt["pe"] = pe0 + n_oc * NTC

    def ev_tick(oc, tc):
        return dv0 + oc * NTC + tc + 1
    kb.cnt["dve"] = dv0 + n_oc * NTC

    if n_oc == 1:
        ops = [("ld", 0), ("st", 0)]
    else:
        ops = [("ld", 0), ("ld", 1)]
        for oc in range(2, n_oc):
            ops.append(("st", oc - 2))
            ops.append(("ld", oc))
        ops += [("st", n_oc - 2), ("st", n_oc - 1)]

    def f_gp(e):
        for kind, oc in ops:
            if kind == "ld":
                if oc >= 2:
                    e.wait_ge(kb.sems["pe"], mm_tick(oc - 2, NTC - 1))
                WL.issue(e, out=wbf[:, (oc % 2) * FC * 128:(oc % 2) * FC * 128 + FC * 128],
                         in_=w_pm[oc * 128:(oc + 1) * 128, :])
            else:
                e.wait_ge(kb.sems["dve"], ev_tick(oc, NTC - 1))
                SO.issue(e, out=dest[dest_row0 + oc * 128:dest_row0 + (oc + 1) * 128, :],
                         in_=stage[:, (oc % 2) * TOK:(oc % 2) * TOK + TOK])
    kb.sec("gpsimd", f_gp)

    def f_tensor(e):
        for oc in range(n_oc):
            WL.w(e, oc)
            for tc in range(NTC):
                gidx = oc * NTC + tc
                if gidx >= 2:
                    e.wait_ge(kb.sems["dve"], ev_tick((gidx - 2) // NTC, (gidx - 2) % NTC))
                for fc in range(FC):
                    ins = e.matmul(out=ps[gidx % 2][:, 0:TC],
                                   lhsT=wbf[:, (oc % 2) * FC * 128 + fc * 128:(oc % 2) * FC * 128 + fc * 128 + 128],
                                   rhs=xhat[:, fc * TOK + tc * TC:fc * TOK + tc * TC + TC],
                                   start=(fc == 0), stop=(fc == FC - 1))
                ins.then_inc(kb.sems["pe"], 1)
    kb.sec("tensor", f_tensor)

    def f_vector(e):
        for oc in range(n_oc):
            for tc in range(NTC):
                gidx = oc * NTC + tc
                e.wait_ge(kb.sems["pe"], mm_tick(oc, tc))
                if oc >= 2 and tc == 0:
                    SO.w(e, oc - 2)
                if bias_pm is not None:
                    e.tensor_scalar(out=stage[:, (oc % 2) * TOK + tc * TC:(oc % 2) * TOK + tc * TC + TC],
                                    in0=ps[gidx % 2][:, 0:TC],
                                    scalar1=bias_pm[:, oc:oc + 1], scalar2=None,
                                    op0=OP.add).then_inc(kb.sems["dve"], 1)
                else:
                    e.tensor_copy(out=stage[:, (oc % 2) * TOK + tc * TC:(oc % 2) * TOK + tc * TC + TC],
                                  in_=ps[gidx % 2][:, 0:TC]).then_inc(kb.sems["dve"], 1)
    kb.sec("vector", f_vector)


def ph_proj_v(kb, xhat, wv_pm, bv_row, n_vg, VG, dest, dest_row0, sb, ps, consts):
    wbf, vgst, bvb = sb
    ones_f32 = consts["ones_f32"]
    HPG = VG // 128
    WL = DS(kb, ("g0", "g1"))
    SO = DS(kb, ("g2", "g3"))
    pe0 = kb.cnt["pe"]
    dv0 = kb.cnt["dve"]

    def bc_tick(vg):
        return pe0 + vg * (NTT + 1) + 1

    def mm_tick(vg, tt):
        return pe0 + vg * (NTT + 1) + 1 + tt + 1
    kb.cnt["pe"] = pe0 + n_vg * (NTT + 1)

    def bvb_tick(vg):
        return dv0 + vg * (NTT + 1) + 1

    def ev_tick(vg, tt):
        return dv0 + vg * (NTT + 1) + 1 + tt + 1
    kb.cnt["dve"] = dv0 + n_vg * (NTT + 1)

    ops = []
    for vg in range(n_vg):
        ops.append(("ld", vg))
        if vg >= 2:
            ops.append(("st", vg - 2))
    if n_vg >= 2:
        ops.append(("st", n_vg - 2))
    ops.append(("st", n_vg - 1))

    def f_gp(e):
        for kind, vg in ops:
            if kind == "ld":
                if vg >= 2:
                    e.wait_ge(kb.sems["pe"], mm_tick(vg - 2, NTT - 1))
                WL.issue(e, out=wbf[:, (vg % 2) * FC * VG:(vg % 2) * FC * VG + FC * VG],
                         in_=wv_pm[vg * 128:(vg + 1) * 128, :])
            else:
                e.wait_ge(kb.sems["dve"], ev_tick(vg, NTT - 1))
                for hh in range(HPG):
                    h = vg * HPG + hh
                    SO.issue(e, out=dest[dest_row0 + h * 128:dest_row0 + (h + 1) * 128, :],
                             in_=vgst[:, hh * TOK:(hh + 1) * TOK])
    kb.sec("gpsimd", f_gp)

    def f_tensor(e):
        for vg in range(n_vg):
            if vg >= 1:
                e.wait_ge(kb.sems["dve"], bvb_tick(vg - 1))
            e.matmul(out=ps[4][:, 0:VG], lhsT=ones_f32[0:1, :],
                     rhs=bv_row[0:1, vg * VG:(vg + 1) * VG],
                     start=True, stop=True).then_inc(kb.sems["pe"], 1)
            WL.w(e, vg)
            for tt in range(NTT):
                gidx = vg * NTT + tt
                if gidx >= 2:
                    e.wait_ge(kb.sems["dve"], ev_tick((gidx - 2) // NTT, (gidx - 2) % NTT))
                for fc in range(FC):
                    ins = e.matmul(out=ps[gidx % 2][:, 0:VG],
                                   lhsT=xhat[:, fc * TOK + tt * 128:fc * TOK + tt * 128 + 128],
                                   rhs=wbf[:, (vg % 2) * FC * VG + fc * VG:(vg % 2) * FC * VG + fc * VG + VG],
                                   start=(fc == 0), stop=(fc == FC - 1))
                ins.then_inc(kb.sems["pe"], 1)
    kb.sec("tensor", f_tensor)

    def f_vector(e):
        for vg in range(n_vg):
            e.wait_ge(kb.sems["pe"], bc_tick(vg))
            e.tensor_copy(out=bvb[:, 0:VG], in_=ps[4][:, 0:VG]).then_inc(kb.sems["dve"], 1)
            e.wait_ge(kb.sems["dve"], bvb_tick(vg))
            if vg >= 1:
                SO.w(e, vg * HPG - 1)
            for tt in range(NTT):
                gidx = vg * NTT + tt
                e.wait_ge(kb.sems["pe"], mm_tick(vg, tt))
                e.tensor_tensor(out=vgst[:].rearrange("p (hh t) -> p hh t", hh=HPG)[:, :, tt * 128:tt * 128 + 128],
                                in0=ps[gidx % 2][:, 0:VG].rearrange("p (hh dh) -> p hh dh", hh=HPG),
                                in1=bvb[:, 0:VG].rearrange("p (hh dh) -> p hh dh", hh=HPG),
                                op=OP.add).then_inc(kb.sems["dve"], 1)
    kb.sec("vector", f_vector)
def ph_attn(kb, layer, qbuf, kvfull, abuf, sb, ps, consts, qmode=None, deep=False,
            qgeom=None):
    (ksb, vsb, qsb, ptsb, msksb, tmpsb, asb, qbrow, recb, srw, qrhs, qtmp, accsb) = sb
    ones_f32 = consts["ones_f32"]
    ones_bf = consts["ones_bf"]
    onescol = consts["onescol"]
    isq_vec = consts["isq_vec"]
    midbias = consts["midbias"]
    kposB = consts["kposB"]
    qposB = consts["qposB"]
    mqa = (layer == 1)
    SEQH = SEQ // 2
    NPS = 4 if deep else 2
    sbanks = [0, 1, 5, 6] if deep else [0, 1]

    def sbank(kt):
        return ps[sbanks[kt % NPS]]

    def ptslot(kt):
        return ptsb[:, (kt % NPS) * TC:(kt % NPS) * TC + TC]

    def accs(qc, par):
        return accsb[:, (qc % 2) * 1024 + par * 512:(qc % 2) * 1024 + par * 512 + 512]

    if layer == 0:
        nkt = [20 + 4 * qc for qc in range(NTC)]
        masked = [list(range(4 * qc, 4 * qc + 4)) + list(range(16 + 4 * qc, 20 + 4 * qc))
                  for qc in range(NTC)]
        midt = [set(range(4 * qc + 4, 16 + 4 * qc)) for qc in range(NTC)]
        kv_row0 = [0, D]
    else:
        nkt = [NKT] * NTC
        masked = [[] for _ in range(NTC)]
        midt = [set() for _ in range(NTC)]
        kv_row0 = [0, DH]

    if layer == 0:
        pe0 = kb.cnt["pe"]
        dv0 = kb.cnt["dve"]
        QP = DS(kb, ("l4",))

        def f_syncq(e):
            QP.issue(e, out=qposB, in_=consts["qposB_dram"])
        kb.sec("sync", f_syncq)

        def cp_tick(qc):
            return dv0 + qc * 9 + 1

        def f_tensor_m(e):
            QP.w(e, 0)
            for qc in range(NTC):
                if qc >= 1:
                    e.wait_ge(kb.sems["dve"], cp_tick(qc - 1))
                e.matmul(out=ps[4][:, 0:TC], lhsT=ones_f32[0:1, :],
                         rhs=qposB[0:1, qc * TC:(qc + 1) * TC],
                         start=True, stop=True).then_inc(kb.sems["pe"], 1)
                kb.inc("pe")
        kb.sec("tensor", f_tensor_m)

        def f_vector_m(e):
            for qc in range(NTC):
                e.wait_ge(kb.sems["pe"], pe0 + qc + 1)
                e.tensor_copy(out=qbrow[:, 0:TC], in_=ps[4][:, 0:TC]).then_inc(kb.sems["dve"], 1)
                e.wait_ge(kb.sems["dve"], kb.inc("dve"))
                for j, t in enumerate(masked[qc]):
                    e.tensor_scalar(out=msksb[:, (qc * 8 + j) * TC:(qc * 8 + j) * TC + TC],
                                    in0=qbrow[:, 0:TC], scalar1=kposB[:, t:t + 1],
                                    scalar2=0.0, op0=OP.subtract, op1=OP.min
                                    ).then_inc(kb.sems["dve"], 1)
                    kb.inc("dve")
        kb.sec("vector", f_vector_m)
        kb.barrier()

    nh = H
    scram = (layer == 0)
    if qmode is None:
        qmode = "host" if scram else "plain"
    fusedq = (qmode == "fused")
    if qgeom is None:
        qgeom = (3 * D, 2 * D)
    KV = DS(kb, ("l0", "l1", "l2", "l3"))
    QL = DS(kb, ("l4", "l5", "l6", "l7") if fusedq else ("l4", "l5"))
    AS = DS(kb, ("g0", "g1"))
    QW = H * TOK
    if fusedq:
        QW = 16 * 256

    pe_t = kb.cnt["pe"]
    act_t = kb.cnt["act"]
    dve_t = kb.cnt["dve"]
    S_tick = {}
    PV_tick = {}
    RSUM_tick = {}
    BC_tick = {}
    EXP_tick = {}
    SC_tick = {}
    QS_tick = {}
    MADD_tick = {}
    ACC_tick = {}
    EV_tick = {}
    GA_tick = {}
    for h in range(nh):
        for qc in range(NTC):
            n = nkt[qc]
            mset = set(masked[qc])
            if scram:
                if fusedq:
                    act_t += 2
                    QS_tick[(h, qc)] = act_t
                dve_t += 1
                GA_tick[(h, qc)] = dve_t
            for kt in range(n):
                pe_t += 1
                S_tick[(h, qc, kt)] = pe_t
                if kt >= 1:
                    pe_t += 1
                    PV_tick[(h, qc, kt - 1)] = pe_t
                if kt in mset:
                    dve_t += 1
                    MADD_tick[(h, qc, kt)] = dve_t
                if kt >= 1:
                    dve_t += 1
                    ACC_tick[(h, qc, kt - 1)] = dve_t
                act_t += 1
                EXP_tick[(h, qc, kt)] = act_t
            pe_t += 1
            PV_tick[(h, qc, n - 1)] = pe_t
            dve_t += 1
            ACC_tick[(h, qc, n - 1)] = dve_t
            pe_t += 2
            RSUM_tick[(h, qc)] = pe_t
            act_t += 1
            SC_tick[(h, qc)] = act_t
            pe_t += 1
            BC_tick[(h, qc)] = pe_t
            dve_t += 1
            dve_t += 1
            EV_tick[(h, qc)] = dve_t
    kb.cnt["pe"] = pe_t
    kb.cnt["act"] = act_t
    kb.cnt["dve"] = dve_t

    def prev_hqc(h, qc):
        return (h, qc - 1) if qc > 0 else (h - 1, NTC - 1)

    def prev2_hqc(h, qc):
        if qc >= 2:
            return (h, qc - 2)
        if h >= 1:
            return (h - 1, NTC - 2 + qc)
        return None

    def bank_free_tick(h, qc, kt):
        if kt in set(masked[qc]):
            return ("dve", MADD_tick[(h, qc, kt)])
        return ("act", EXP_tick[(h, qc, kt)])

    def f_sync(e):
        for h in range(nh):
            if (not mqa) or h == 0:
                if (not mqa) and h >= 2:
                    e.wait_ge(kb.sems["pe"], PV_tick[(h - 2, NTC - 1, nkt[NTC - 1] - 1)])
                slot = 0 if mqa else (h % 2)
                krow = kv_row0[0] + (0 if mqa else h * 128)
                vrow = kv_row0[1] + (0 if mqa else h * 128)
                KV.issue(e, out=ksb[:, slot * SEQ:(slot + 1) * SEQ].rearrange("p (r t) -> p r t", r=2),
                         in_=kvfull.rearrange("(r z) t -> z r t", r=2)[krow:krow + 128, :, :])
                KV.issue(e, out=vsb[:, slot * SEQ:(slot + 1) * SEQ].rearrange("p (r t) -> p r t", r=2),
                         in_=kvfull.rearrange("(r z) t -> z r t", r=2)[vrow:vrow + 128, :, :])
            if h >= 2:
                if scram:
                    e.wait_ge(kb.sems["dve"], GA_tick[(h - 2, NTC - 1)])
                else:
                    e.wait_ge(kb.sems["pe"], S_tick[(h - 2, NTC - 1, nkt[NTC - 1] - 1)])
            if fusedq:
                r = h // 8
                qrow = r * qgeom[0] + qgeom[1]
                qcol = 256 * (h % 8)
                for fh in range(2):
                    QL.issue(e, out=qsb[:, (h % 2) * QW + fh * 8 * 256:(h % 2) * QW + (fh + 1) * 8 * 256
                                        ].rearrange("p (fc g) -> p fc g", fc=8),
                             in_=qbuf[qrow + fh * 8 * 128:qrow + (fh + 1) * 8 * 128,
                                      qcol:qcol + 256].rearrange("(fc p) g -> p fc g", p=128))
            elif scram:
                QL.issue(e, out=qsb[:, (h % 2) * TOK:(h % 2) * TOK + TOK],
                         in_=qbuf[:, h * TOK:(h + 1) * TOK])
            else:
                QL.issue(e, out=qsb[:, (h % 2) * TOK:(h % 2) * TOK + TOK],
                         in_=qbuf[h * 128:(h + 1) * 128, :])
    kb.sec("sync", f_sync)

    def f_tensor(e):
        for h in range(nh):
            kslot = 0 if mqa else (h % 2)
            if mqa:
                KV.w(e, 0)
                KV.w(e, 1)
            else:
                KV.w(e, 2 * h)
                KV.w(e, 2 * h + 1)
            QL.w(e, h)
            for qc in range(NTC):
                n = nkt[qc]
                for kt in range(n):
                    if kt >= NPS:
                        sem, v = bank_free_tick(h, qc, kt - NPS)
                        e.wait_ge(kb.sems[sem], v)
                    elif qc > 0 or h > 0:
                        ph_, pqc = prev_hqc(h, qc)
                        pn = nkt[pqc]
                        sem, v = bank_free_tick(ph_, pqc, pn - NPS + kt)
                        e.wait_ge(kb.sems[sem], v)
                    if kt == 0 and scram:
                        e.wait_ge(kb.sems["dve"], GA_tick[(h, qc)])
                    if scram:
                        s_rhs = qrhs[:, (qc % 2) * TC:(qc % 2) * TC + TC]
                    else:
                        s_rhs = qsb[:, (h % 2) * TOK + qc * TC:(h % 2) * TOK + qc * TC + TC]
                    e.matmul(out=sbank(kt)[:, 0:TC],
                             lhsT=ksb[:, kslot * SEQ + kt * 128:kslot * SEQ + kt * 128 + 128],
                             rhs=s_rhs,
                             start=True, stop=True).then_inc(kb.sems["pe"], 1)
                    if kt >= 1:
                        kp = kt - 1
                        e.wait_ge(kb.sems["act"], EXP_tick[(h, qc, kp)])
                        if kp == 0 and (qc > 0 or h > 0):
                            ph_, pqc = prev_hqc(h, qc)
                            e.wait_ge(kb.sems["dve"], EV_tick[(ph_, pqc)])
                        e.matmul(out=ps[2][:, 0:TC],
                                 lhsT=vsb[:, kslot * SEQ + kp * 128:kslot * SEQ + kp * 128 + 128],
                                 rhs=ptslot(kp),
                                 start=(kp == 0), stop=False).then_inc(kb.sems["pe"], 1)
                kp = n - 1
                e.wait_ge(kb.sems["act"], EXP_tick[(h, qc, kp)])
                e.matmul(out=ps[2][:, 0:TC],
                         lhsT=vsb[:, kslot * SEQ + kp * 128:kslot * SEQ + kp * 128 + 128],
                         rhs=ptslot(kp),
                         start=False, stop=True).then_inc(kb.sems["pe"], 1)
                e.wait_ge(kb.sems["dve"], ACC_tick[(h, qc, n - 1)])
                if qc > 0 or h > 0:
                    ph_, pqc = prev_hqc(h, qc)
                    e.wait_ge(kb.sems["act"], SC_tick[(ph_, pqc)])
                e.matmul(out=ps[3][0:1, 0:TC], lhsT=onescol[:],
                         rhs=accs(qc, 0), start=True, stop=False).then_inc(kb.sems["pe"], 1)
                e.matmul(out=ps[3][0:1, 0:TC], lhsT=onescol[:],
                         rhs=accs(qc, 1), start=False, stop=True).then_inc(kb.sems["pe"], 1)
                e.wait_ge(kb.sems["act"], SC_tick[(h, qc)])
                e.matmul(out=ps[4][:, 0:TC], lhsT=ones_f32[0:1, :],
                         rhs=srw[0:1, 0:TC], start=True, stop=True).then_inc(kb.sems["pe"], 1)
    kb.sec("tensor", f_tensor)

    def f_scalar(e):
        for h in range(nh):
            for qc in range(NTC):
                n = nkt[qc]
                mset = set(masked[qc])
                if scram and fusedq:
                    QL.w(e, 2 * h)
                    QL.w(e, 2 * h + 1)
                    if qc >= 2:
                        e.wait_ge(kb.sems["pe"], S_tick[(h, qc - 2, nkt[qc - 2] - 1)])
                    elif h >= 1:
                        e.wait_ge(kb.sems["pe"], S_tick[(h - 1, NTC - 2 + qc, nkt[NTC - 2 + qc] - 1)])
                    if qc > 0 or h > 0:
                        ph_, pqc = prev_hqc(h, qc)
                        e.wait_ge(kb.sems["dve"], GA_tick[(ph_, pqc)])
                    qv = qsb[:, (h % 2) * QW:(h % 2) * QW + QW].rearrange("p (m g) -> p g m", m=16)
                    e.activation(out=qtmp[:, 0:TC].rearrange("p (g m) -> p g m", m=16),
                                 in_=qv[:, 128 + qc * 32:128 + (qc + 1) * 32, :],
                                 func=AF.Copy,
                                 scale=consts["qw1"][:, 0:1]).then_inc(kb.sems["act"], 1)
                    e.activation(out=qrhs[:, (qc % 2) * TC:(qc % 2) * TC + TC].rearrange("p (g m) -> p g m", m=16),
                                 in_=qv[:, qc * 32:(qc + 1) * 32, :],
                                 func=AF.Copy,
                                 scale=consts["qw0"][:, 0:1]).then_inc(kb.sems["act"], 1)
                for kt in range(n):
                    if kt >= NPS:
                        e.wait_ge(kb.sems["pe"], PV_tick[(h, qc, kt - NPS)])
                        e.wait_ge(kb.sems["dve"], ACC_tick[(h, qc, kt - NPS)])
                    elif qc > 0 or h > 0:
                        ph_, pqc = prev_hqc(h, qc)
                        pn = nkt[pqc]
                        e.wait_ge(kb.sems["pe"], PV_tick[(ph_, pqc, pn - NPS + kt)])
                        e.wait_ge(kb.sems["dve"], ACC_tick[(ph_, pqc, pn - NPS + kt)])
                    if kt in mset:
                        e.wait_ge(kb.sems["dve"], MADD_tick[(h, qc, kt)])
                        e.activation(out=ptslot(kt),
                                     in_=tmpsb[:, (kt % 2) * TC:(kt % 2) * TC + TC],
                                     func=AF.Exp,
                                     scale=isq_vec[:, 0:1]).then_inc(kb.sems["act"], 1)
                    elif kt in midt[qc]:
                        e.wait_ge(kb.sems["pe"], S_tick[(h, qc, kt)])
                        e.activation(out=ptslot(kt),
                                     in_=sbank(kt)[:, 0:TC], func=AF.Exp,
                                     scale=isq_vec[:, 0:1],
                                     bias=midbias[:, 0:1]).then_inc(kb.sems["act"], 1)
                    else:
                        e.wait_ge(kb.sems["pe"], S_tick[(h, qc, kt)])
                        e.activation(out=ptslot(kt),
                                     in_=sbank(kt)[:, 0:TC], func=AF.Exp,
                                     scale=isq_vec[:, 0:1]).then_inc(kb.sems["act"], 1)
                e.wait_ge(kb.sems["pe"], RSUM_tick[(h, qc)])
                e.activation(out=srw[0:1, 0:TC], in_=ps[3][0:1, 0:TC],
                             func=AF.Copy).then_inc(kb.sems["act"], 1)
    kb.sec("scalar", f_scalar)

    def f_vector(e):
        for h in range(nh):
            for qc in range(NTC):
                n = nkt[qc]
                mset = set(masked[qc])
                if scram:
                    if fusedq:
                        e.wait_ge(kb.sems["act"], QS_tick[(h, qc)])
                        e.tensor_tensor(
                            out=qrhs[:, (qc % 2) * TC:(qc % 2) * TC + TC],
                            in0=qrhs[:, (qc % 2) * TC:(qc % 2) * TC + TC],
                            in1=qtmp[:, 0:TC],
                            op=OP.add).then_inc(kb.sems["dve"], 1)
                    else:
                        QL.w(e, h)
                        if qc >= 2:
                            e.wait_ge(kb.sems["pe"], S_tick[(h, qc - 2, nkt[qc - 2] - 1)])
                        elif h >= 1:
                            e.wait_ge(kb.sems["pe"], S_tick[(h - 1, NTC - 2 + qc, nkt[NTC - 2 + qc] - 1)])
                        e.tensor_copy(
                            out=qrhs[:, (qc % 2) * TC:(qc % 2) * TC + TC].rearrange("p (g m) -> p g m", m=16),
                            in_=qsb[:, (h % 2) * TOK:(h % 2) * TOK + TOK].rearrange(
                                "p (m g) -> p g m", m=16)[:, qc * 32:(qc + 1) * 32, :]
                        ).then_inc(kb.sems["dve"], 1)

                def acc_op(kt):
                    e.wait_ge(kb.sems["act"], EXP_tick[(h, qc, kt)])
                    dst = accs(qc, kt % 2)
                    if kt <= 1:
                        p2 = prev2_hqc(h, qc)
                        if p2 is not None:
                            e.wait_ge(kb.sems["pe"], RSUM_tick[p2])
                        e.tensor_copy(out=dst, in_=ptslot(kt)).then_inc(kb.sems["dve"], 1)
                    else:
                        e.tensor_tensor(out=dst, in0=dst, in1=ptslot(kt),
                                        op=OP.add).then_inc(kb.sems["dve"], 1)
                for kt in range(n):
                    if kt in mset:
                        e.wait_ge(kb.sems["pe"], S_tick[(h, qc, kt)])
                        if kt >= 2:
                            e.wait_ge(kb.sems["act"], EXP_tick[(h, qc, kt - 2)])
                        elif qc > 0 or h > 0:
                            ph_, pqc = prev_hqc(h, qc)
                            pn = nkt[pqc]
                            e.wait_ge(kb.sems["act"], EXP_tick[(ph_, pqc, pn - 2 + kt)])
                        j = masked[qc].index(kt)
                        e.tensor_tensor(out=tmpsb[:, (kt % 2) * TC:(kt % 2) * TC + TC],
                                        in0=sbank(kt)[:, 0:TC],
                                        in1=msksb[:, (qc * 8 + j) * TC:(qc * 8 + j) * TC + TC],
                                        op=OP.add).then_inc(kb.sems["dve"], 1)
                    if kt >= 1:
                        acc_op(kt - 1)
                acc_op(n - 1)
                e.wait_ge(kb.sems["pe"], BC_tick[(h, qc)])
                e.reciprocal(out=recb[:, 0:TC], in_=ps[4][:, 0:TC]).then_inc(kb.sems["dve"], 1)
                e.wait_ge(kb.sems["dve"], EV_tick[(h, qc)] - 1)
                if h >= 2 and qc == 0:
                    AS.w(e, h - 2)
                e.tensor_tensor(out=asb[:, (h % 2) * TOK + qc * TC:(h % 2) * TOK + qc * TC + TC],
                                in0=ps[2][:, 0:TC], in1=recb[:, 0:TC],
                                op=OP.mult).then_inc(kb.sems["dve"], 1)
    kb.sec("vector", f_vector)

    def f_gp(e):
        for h in range(nh):
            e.wait_ge(kb.sems["dve"], EV_tick[(h, NTC - 1)])
            AS.issue(e, out=abuf[h * 128:(h + 1) * 128, :],
                     in_=asb[:, (h % 2) * TOK:(h % 2) * TOK + TOK])
    kb.sec("gpsimd", f_gp)


def ph_wo(kb, abuf, wo_dram, bo_pm, resid, sb, ps):
    wosb, atsb, rsb, rosb = sb
    for half in range(2):
        pe0 = kb.cnt["pe"]
        dv0 = kb.cnt["dve"]
        nidx = NTC * 8
        AT = DS(kb, ("l0", "l1"))
        RS = DS(kb, ("l2", "l3"))
        WOL = DS(kb, ("g0", "g1"))
        WB = DS(kb, ("g2", "g3", "g4", "g5"))

        dsched = [("ar", 0), ("ar", 1)]
        for i in range(2, nidx):
            dsched.append(("ab", i - 2))
            dsched.append(("ar", i))
        dsched += [("ab", nidx - 2), ("ab", nidx - 1)]
        ar_tick = {}
        ab_tick = {}
        t = dv0
        for kind, i in dsched:
            t += 1
            (ar_tick if kind == "ar" else ab_tick)[i] = t
        kb.cnt["dve"] = t

        def mm_tick(i, pe0=pe0):
            return pe0 + i + 1
        kb.cnt["pe"] = pe0 + nidx

        def f_gp(e, half=half, pe0=pe0, ab_tick=ab_tick, WOL=WOL, WB=WB):
            e.wait_ge(kb.sems["pe"], pe0)
            WOL.issue(e, out=wosb[:],
                      in_=wo_dram.rearrange("(oc p) o2 -> p oc o2", p=128)[:, :, half * 1024:(half + 1) * 1024])
            for i in range(nidx):
                e.wait_ge(kb.sems["dve"], ab_tick[i])
                tc, j = i // 8, i % 8
                o2 = half * 8 + j
                WB.issue(e, out=resid[o2 * 128:(o2 + 1) * 128, tc * TC:(tc + 1) * TC],
                         in_=rosb[:, (i % 4) * TC:(i % 4) * TC + TC])
        kb.sec("gpsimd", f_gp)

        def f_sync(e, half=half, pe0=pe0, dv0=dv0, ar_tick=ar_tick, AT=AT, RS=RS, mm_tick=mm_tick):
            for tc in range(NTC):
                if tc >= 2:
                    e.wait_ge(kb.sems["pe"], mm_tick((tc - 1) * 8 - 1))
                else:
                    e.wait_ge(kb.sems["pe"], pe0)
                AT.issue(e, out=atsb[:, (tc % 2) * FC * TC:(tc % 2) * FC * TC + FC * TC],
                         in_=abuf.rearrange("(oc p) t -> p oc t", p=128)[:, :, tc * TC:(tc + 1) * TC])
                for j in range(8):
                    i = tc * 8 + j
                    if i >= 2:
                        e.wait_ge(kb.sems["dve"], ar_tick[i - 2])
                    elif half == 1:
                        e.wait_ge(kb.sems["dve"], dv0)
                    RS.issue(e, out=rsb[:, (i % 4) * TC:(i % 4) * TC + TC],
                             in_=resid[(half * 8 + j) * 128:(half * 8 + j + 1) * 128,
                                       tc * TC:(tc + 1) * TC])
        kb.sec("sync", f_sync)

        def f_tensor(e, half=half, AT=AT, WOL=WOL, ar_tick=ar_tick, mm_tick=mm_tick):
            WOL.w(e, 0)
            for tc in range(NTC):
                AT.w(e, tc)
                for j in range(8):
                    i = tc * 8 + j
                    if i >= 2:
                        e.wait_ge(kb.sems["dve"], ar_tick[i - 2])
                    for oc in range(FC):
                        ins = e.matmul(out=ps[i % 2][:, 0:TC],
                                       lhsT=wosb[:, oc * 1024 + j * 128:oc * 1024 + j * 128 + 128],
                                       rhs=atsb[:, (tc % 2) * FC * TC + oc * TC:(tc % 2) * FC * TC + oc * TC + TC],
                                       start=(oc == 0), stop=(oc == FC - 1))
                    ins.then_inc(kb.sems["pe"], 1)
        kb.sec("tensor", f_tensor)

        def f_vector(e, half=half, dsched=dsched, RS=RS, WB=WB, mm_tick=mm_tick):
            for kind, i in dsched:
                tc, j = i // 8, i % 8
                if kind == "ar":
                    e.wait_ge(kb.sems["pe"], mm_tick(i))
                    RS.w(e, i)
                    if i >= 4:
                        WB.w(e, i - 4)
                    e.tensor_tensor(out=rosb[:, (i % 4) * TC:(i % 4) * TC + TC],
                                    in0=ps[i % 2][:, 0:TC],
                                    in1=rsb[:, (i % 4) * TC:(i % 4) * TC + TC],
                                    op=OP.add).then_inc(kb.sems["dve"], 1)
                else:
                    o2 = half * 8 + j
                    e.tensor_scalar(out=rosb[:, (i % 4) * TC:(i % 4) * TC + TC],
                                    in0=rosb[:, (i % 4) * TC:(i % 4) * TC + TC],
                                    scalar1=bo_pm[:, o2:o2 + 1], scalar2=None,
                                    op0=OP.add).then_inc(kb.sems["dve"], 1)
        kb.sec("vector", f_vector)
def ph_down_prep(kb, down_pm, wd_scr, ai32, ab, ps):
    NI = IFF // 128
    DL = DS(kb, ("l0", "l1"))
    PS_ = DS(kb, ("g0", "g1"))
    ac0 = kb.cnt["act"]

    def f_sync(e):
        for ic in range(NI):
            if ic >= 2:
                e.wait_ge(kb.sems["act"], ac0 + ic - 1)
            DL.issue(e, out=ai32[:, (ic % 2) * D:(ic % 2) * D + D],
                     in_=down_pm[ic * 128:(ic + 1) * 128, :])
    kb.sec("sync", f_sync)

    def f_scalar(e):
        for ic in range(NI):
            DL.w(e, ic)
            if ic >= 2:
                PS_.w(e, ic - 2)
            e.activation(out=ab[:, (ic % 2) * D:(ic % 2) * D + D],
                         in_=ai32[:, (ic % 2) * D:(ic % 2) * D + D],
                         func=AF.Copy, bias=-128.0).then_inc(kb.sems["act"], 1)
            kb.inc("act")
    kb.sec("scalar", f_scalar)

    def f_gp(e):
        for ic in range(NI):
            e.wait_ge(kb.sems["act"], ac0 + ic + 1)
            PS_.issue(e, out=wd_scr[ic * 128:(ic + 1) * 128, :],
                      in_=ab[:, (ic % 2) * D:(ic % 2) * D + D])
    kb.sec("gpsimd", f_gp)


def ph_ffn0_a(kb, xhat, gate_pm, up_pm, h_scr, sgv, suv, sb, ps):
    ai32, wbuf, silu_sb, u_sb, h_stage = sb
    NIB = 16
    GL = DS(kb, ("l0", "l1"))
    HS = DS(kb, ("g0", "g1"))
    ac0 = kb.cnt["act"]
    pe0 = kb.cnt["pe"]
    dv0 = kb.cnt["dve"]

    def wgT(ib):
        return wbuf[:, (ib % 2) * 8192:(ib % 2) * 8192 + 8192]

    def wuT(ib):
        return wbuf[:, 16384 + (ib % 2) * 8192:16384 + (ib % 2) * 8192 + 8192]

    sched = [("c", 0, j) for j in range(8)]
    for ib in range(NIB):
        for k in range(16):
            sched.append(("s", ib, k))
            if ib + 1 < NIB and k % 2 == 0:
                sched.append(("c", ib + 1, k // 2))
    cast_tick = {}
    silu_tick = {}
    t = ac0
    for kind, ib, k in sched:
        t += 1
        (cast_tick if kind == "c" else silu_tick)[(ib, k)] = t
    kb.cnt["act"] = t

    def g_tick(ib, k):
        return pe0 + ib * 32 + 2 * k + 1

    def u_tick(ib, k):
        return pe0 + ib * 32 + 2 * k + 2

    def dsched_ib():
        s = [("ue", 0), ("ue", 1)]
        for k in range(2, 16):
            s.append(("h", k - 2))
            s.append(("ue", k))
        s += [("h", 14), ("h", 15)]
        return s
    ue_tick = {}
    h_tick = {}
    t = dv0
    for ib in range(NIB):
        for kind, k in dsched_ib():
            t += 1
            (ue_tick if kind == "ue" else h_tick)[(ib, k)] = t
    kb.cnt["dve"] = t
    kb.cnt["pe"] = pe0 + NIB * 32

    def f_sync(e):
        for ib in range(NIB):
            for j in range(8):
                if j >= 2:
                    e.wait_ge(kb.sems["act"], cast_tick[(ib, j - 2)])
                elif ib >= 1:
                    e.wait_ge(kb.sems["act"], cast_tick[(ib - 1, j + 6)])
                src = gate_pm if j < 4 else up_pm
                q = j % 4
                GL.issue(e, out=ai32[:, (j % 2) * 2048:(j % 2) * 2048 + 2048],
                         in_=src[ib * 128:(ib + 1) * 128, q * 2048:(q + 1) * 2048])
    kb.sec("sync", f_sync)

    def f_scalar(e):
        for kind, ib, k in sched:
            if kind == "c":
                j = k
                GL.w(e, ib * 8 + j)
                if ib >= 2 and j in (0, 4):
                    e.wait_ge(kb.sems["pe"], pe0 + (ib - 1) * 32)
                dst = wgT(ib) if j < 4 else wuT(ib)
                q = j % 4
                e.activation(out=dst[:, q * 2048:(q + 1) * 2048],
                             in_=ai32[:, (j % 2) * 2048:(j % 2) * 2048 + 2048],
                             func=AF.Copy, bias=-128.0).then_inc(kb.sems["act"], 1)
            else:
                icb, tc = k // 4, k % 4
                e.wait_ge(kb.sems["pe"], g_tick(ib, k))
                gi = ib * 16 + k
                if gi >= 2:
                    pib, pk = (gi - 2) // 16, (gi - 2) % 16
                    e.wait_ge(kb.sems["dve"], h_tick[(pib, pk)])
                ic = ib * 4 + icb
                e.activation(out=silu_sb[:, (k % 2) * TC:(k % 2) * TC + TC],
                             in_=ps[k % 2][:, 0:TC], func=AF.Silu,
                             scale=sgv[:, ic:ic + 1]).then_inc(kb.sems["act"], 1)
    kb.sec("scalar", f_scalar)

    def f_tensor(e):
        for ib in range(NIB):
            for j in range(8):
                e.wait_ge(kb.sems["act"], cast_tick[(ib, j)])
            for k in range(16):
                icb, tc = k // 4, k % 4
                gi = ib * 16 + k
                if gi >= 2:
                    pib, pk = (gi - 2) // 16, (gi - 2) % 16
                    e.wait_ge(kb.sems["act"], silu_tick[(pib, pk)])
                    e.wait_ge(kb.sems["dve"], ue_tick[(pib, pk)])
                for fc in range(FC):
                    ins = e.matmul(out=ps[k % 2][:, 0:TC],
                                   lhsT=wgT(ib)[:, fc * 512 + icb * 128:fc * 512 + icb * 128 + 128],
                                   rhs=xhat[:, fc * TOK + tc * TC:fc * TOK + tc * TC + TC],
                                   start=(fc == 0), stop=(fc == FC - 1))
                ins.then_inc(kb.sems["pe"], 1)
                for fc in range(FC):
                    ins = e.matmul(out=ps[2 + k % 2][:, 0:TC],
                                   lhsT=wuT(ib)[:, fc * 512 + icb * 128:fc * 512 + icb * 128 + 128],
                                   rhs=xhat[:, fc * TOK + tc * TC:fc * TOK + tc * TC + TC],
                                   start=(fc == 0), stop=(fc == FC - 1))
                ins.then_inc(kb.sems["pe"], 1)
    kb.sec("tensor", f_tensor)

    def f_vector(e):
        for ib in range(NIB):
            for kind, k in dsched_ib():
                icb, tc = k // 4, k % 4
                ic = ib * 4 + icb
                if kind == "ue":
                    e.wait_ge(kb.sems["pe"], u_tick(ib, k))
                    if ib > 0 or k >= 2:
                        pib, pk = (ib, k - 2) if k >= 2 else (ib - 1, 14 + k)
                        e.wait_ge(kb.sems["dve"], h_tick[(pib, pk)])
                    e.tensor_scalar(out=u_sb[:, (k % 2) * TC:(k % 2) * TC + TC],
                                    in0=ps[2 + k % 2][:, 0:TC],
                                    scalar1=suv[:, ic:ic + 1], scalar2=None,
                                    op0=OP.mult).then_inc(kb.sems["dve"], 1)
                else:
                    e.wait_ge(kb.sems["act"], silu_tick[(ib, k)])
                    if tc == 0 and ic >= 2:
                        HS.w(e, ic - 2)
                    e.tensor_tensor(out=h_stage[:, (icb % 2) * TOK + tc * TC:(icb % 2) * TOK + tc * TC + TC],
                                    in0=silu_sb[:, (k % 2) * TC:(k % 2) * TC + TC],
                                    in1=u_sb[:, (k % 2) * TC:(k % 2) * TC + TC],
                                    op=OP.mult).then_inc(kb.sems["dve"], 1)
    kb.sec("vector", f_vector)

    def f_gp(e):
        for ib in range(NIB):
            for icb in range(4):
                ic = ib * 4 + icb
                e.wait_ge(kb.sems["dve"], h_tick[(ib, icb * 4 + 3)])
                HS.issue(e, out=h_scr[:, ic * TOK:(ic + 1) * TOK],
                         in_=h_stage[:, (icb % 2) * TOK:(icb % 2) * TOK + TOK])
    kb.sec("gpsimd", f_gp)


def ph_ffn0_b(kb, h_scr, down_pm, sdv, resid, sb, ps, consts):
    htA, htB, wdt, rsb, rosb, hsb = sb
    ones_bf, ones_f32 = consts["ones_bf"], consts["ones_f32"]
    hsr = consts["hsr"]
    NI = IFF // 128
    NG = 8
    GSZ = NI // NG
    hts = (htA, htB)
    HT = DS(kb, ("l0", "l1"))
    RS = DS(kb, ("l2", "l3"))
    WDT = DS(kb, ("g0", "g1"))
    WB = DS(kb, ("g2", "g3", "g4", "g5"))
    pe0 = kb.cnt["pe"]
    dv0 = kb.cnt["dve"]

    def hs_tick(tc, ic):
        return pe0 + tc * (NI + 1 + 4 * NG) + ic + 1

    def bc_tick(tc):
        return pe0 + tc * (NI + 1 + 4 * NG) + NI + 1

    def mm_tick(tc, og, icg):
        return pe0 + tc * (NI + 1 + 4 * NG) + NI + 1 + og * NG + icg + 1
    kb.cnt["pe"] = pe0 + NTC * (NI + 1 + 4 * NG)

    seq12 = [("a", 0), ("a", 1), ("m", 0), ("a", 2), ("m", 1), ("a", 3),
             ("m", 2), ("r", 0), ("m", 3), ("r", 1), ("r", 2), ("r", 3)]
    a_tick = {}
    m_tick = {}
    r_tick = {}
    hrow_tick = {}
    t = dv0
    for tc in range(NTC):
        t += 3
        hrow_tick[tc] = t
        for og in range(4):
            for kind, j in seq12:
                t += 1
                {"a": a_tick, "m": m_tick, "r": r_tick}[kind][(tc, og, j)] = t
    kb.cnt["dve"] = t

    def f_sync(e):
        for tc in range(NTC):
            if tc >= 2:
                e.wait_ge(kb.sems["pe"], mm_tick(tc - 2, 3, NG - 1))
            HT.issue(e, out=hts[tc % 2].rearrange("p (ic t) -> p ic t", ic=NI),
                     in_=h_scr.rearrange("p (ic t) -> p ic t", ic=NI)[:, :, tc * TC:(tc + 1) * TC])
            for og in range(4):
                for j in range(4):
                    i = (tc * 4 + og) * 4 + j
                    if i >= 2:
                        pi = i - 2
                        ptc, pog, pj = pi // 16, (pi // 4) % 4, pi % 4
                        e.wait_ge(kb.sems["dve"], r_tick[(ptc, pog, pj)])
                    oc = og * 4 + j
                    RS.issue(e, out=rsb[:, (i % 4) * TC:(i % 4) * TC + TC],
                             in_=resid[oc * 128:(oc + 1) * 128, tc * TC:(tc + 1) * TC])
    kb.sec("sync", f_sync)

    def f_gp(e):
        for tc in range(NTC):
            for og in range(4):
                for icg in range(NG):
                    gi = (tc * 4 + og) * NG + icg
                    if gi >= 2:
                        pgi = gi - 2
                        ptc, pog, picg = pgi // (4 * NG), (pgi // NG) % 4, pgi % NG
                        e.wait_ge(kb.sems["pe"], mm_tick(ptc, pog, picg))
                    WDT.issue(e,
                              out=wdt[:, (icg % 2) * 4096:(icg % 2) * 4096 + 4096]
                              .rearrange("p (ic oc) -> p ic oc", ic=GSZ),
                              in_=down_pm.rearrange("(ic p) oc -> p ic oc", p=128)
                              [:, icg * GSZ:(icg + 1) * GSZ, og * 512:(og + 1) * 512])
    kb.sec("gpsimd", f_gp)

    def f_scalar(e):
        for tc in range(NTC):
            for og in range(4):
                for j in range(4):
                    e.wait_ge(kb.sems["dve"], r_tick[(tc, og, j)])
                    oc = og * 4 + j
                    WB.issue(e, out=resid[oc * 128:(oc + 1) * 128, tc * TC:(tc + 1) * TC],
                             in_=rosb[:, (j % 4) * TC:(j % 4) * TC + TC])
    kb.sec("scalar", f_scalar)

    def f_tensor(e):
        for tc in range(NTC):
            ht = hts[tc % 2]
            HT.w(e, tc)
            if tc >= 1:
                e.wait_ge(kb.sems["dve"], hrow_tick[tc - 1] - 2)
            for ic in range(NI):
                e.matmul(out=ps[5][0:1, 0:TC], lhsT=ones_bf[:, 0:1],
                         rhs=ht[:, ic * 512:(ic + 1) * 512],
                         start=(ic == 0), stop=(ic == NI - 1)).then_inc(kb.sems["pe"], 1)
            e.wait_ge(kb.sems["dve"], hrow_tick[tc] - 1)
            if tc >= 1:
                e.wait_ge(kb.sems["dve"], hrow_tick[tc - 1])
            e.matmul(out=ps[6][:, 0:TC], lhsT=ones_f32[0:1, :],
                     rhs=hsr[0:1, 0:TC], start=True, stop=True).then_inc(kb.sems["pe"], 1)
            for og in range(4):
                for icg in range(NG):
                    gi = (tc * 4 + og) * NG + icg
                    WDT.w(e, gi)
                    if icg == 0 and (tc > 0 or og > 0):
                        ptc, pog = (tc, og - 1) if og > 0 else (tc - 1, 3)
                        e.wait_ge(kb.sems["dve"], a_tick[(ptc, pog, 3)])
                    for i8 in range(GSZ):
                        ic = icg * GSZ + i8
                        for j in range(4):
                            base = (icg % 2) * 4096 + i8 * 512 + j * 128
                            ins = e.matmul(out=ps[j][:, 0:TC],
                                           lhsT=wdt[:, base:base + 128],
                                           rhs=ht[:, ic * 512:(ic + 1) * 512],
                                           start=(ic == 0), stop=(ic == NI - 1))
                    ins.then_inc(kb.sems["pe"], 1)
    kb.sec("tensor", f_tensor)

    def f_vector(e):
        for tc in range(NTC):
            e.wait_ge(kb.sems["pe"], hs_tick(tc, NI - 1))
            e.tensor_copy(out=hsr[0:1, 0:TC], in_=ps[5][0:1, 0:TC]).then_inc(kb.sems["dve"], 1)
            e.wait_ge(kb.sems["dve"], hrow_tick[tc] - 2)
            e.tensor_scalar(out=hsr[0:1, 0:TC], in0=hsr[0:1, 0:TC], scalar1=-128.0,
                            scalar2=None, op0=OP.mult).then_inc(kb.sems["dve"], 1)
            e.wait_ge(kb.sems["pe"], bc_tick(tc))
            e.tensor_copy(out=hsb[:, 0:TC], in_=ps[6][:, 0:TC]).then_inc(kb.sems["dve"], 1)
            e.wait_ge(kb.sems["dve"], hrow_tick[tc])
            for og in range(4):
                for kind, j in seq12:
                    oc = og * 4 + j
                    i = (tc * 4 + og) * 4 + j
                    if kind == "a":
                        e.wait_ge(kb.sems["pe"], mm_tick(tc, og, NG - 1))
                        if i >= 4:
                            WB.w(e, i - 4)
                        e.tensor_tensor(out=rosb[:, (j % 4) * TC:(j % 4) * TC + TC],
                                        in0=ps[j][:, 0:TC],
                                        in1=hsb[:, 0:TC],
                                        op=OP.add).then_inc(kb.sems["dve"], 1)
                    elif kind == "m":
                        e.tensor_scalar(out=rosb[:, (j % 4) * TC:(j % 4) * TC + TC],
                                        in0=rosb[:, (j % 4) * TC:(j % 4) * TC + TC],
                                        scalar1=sdv[:, oc:oc + 1], scalar2=None,
                                        op0=OP.mult).then_inc(kb.sems["dve"], 1)
                    else:
                        RS.w(e, i)
                        e.tensor_tensor(out=rosb[:, (j % 4) * TC:(j % 4) * TC + TC],
                                        in0=rosb[:, (j % 4) * TC:(j % 4) * TC + TC],
                                        in1=rsb[:, (i % 4) * TC:(i % 4) * TC + TC],
                                        op=OP.add).then_inc(kb.sems["dve"], 1)
    kb.sec("vector", f_vector)


def ph_ffn1(kb, xhat, wt1_pm, g1_pm, at_pm, bs_pm, rep_sb, resid, sb, ps, consts):
    (wtb, at_sb, bs_sb, g1_sb, cwT, cwt_sb, tc_sb, rsb, rosb, tk) = sb
    ident = consts["ident"]

    SL = DS(kb, ("g0", "g1"))

    def f_ld(e):
        SL.issue(e, out=g1_sb[:], in_=g1_pm)
        SL.issue(e, out=at_sb[:], in_=at_pm)
        SL.issue(e, out=bs_sb[:], in_=bs_pm)
    kb.sec("gpsimd", f_ld)
    kb.barrier()

    lg, t1, t2, t5, t7, t8, a2, t3, t4, t6, t9, a1, cwall = tk
    pe0 = kb.cnt["pe"]
    dv0 = kb.cnt["dve"]
    ac0 = kb.cnt["act"]
    ev_tick = {}
    dv = dv0
    ac = ac0
    for tt in range(NTT):
        ev_tick[tt] = dv + 1
        dv += 12
        ac += 2
    kb.cnt["dve"] = dv
    kb.cnt["act"] = ac

    def f_tensor_l(e):
        for tt in range(NTT):
            if tt >= 1:
                e.wait_ge(kb.sems["dve"], ev_tick[tt - 1])
            for fc in range(FC):
                ins = e.matmul(out=ps[0][:, 0:8],
                               lhsT=xhat[:, fc * TOK + tt * 128:fc * TOK + tt * 128 + 128],
                               rhs=g1_sb[:, fc * 8:fc * 8 + 8],
                               start=(fc == 0), stop=(fc == FC - 1))
            ins.then_inc(kb.sems["pe"], 1)
            kb.inc("pe")
    kb.sec("tensor", f_tensor_l)

    def f_scalar_l(e):
        for tt in range(NTT):
            e.wait_ge(kb.sems["dve"], dv0 + tt * 12 + 8)
            e.activation(out=a1[:], in_=lg[:], func=AF.Exp,
                         bias=t2[:, 0:1]).then_inc(kb.sems["act"], 1)
            e.activation(out=a2[:], in_=t5[:, 0:1], func=AF.Exp,
                         bias=t2[:, 0:1]).then_inc(kb.sems["act"], 1)
    kb.sec("scalar", f_scalar_l)

    def f_vector_l(e):
        for tt in range(NTT):
            base = dv0 + tt * 12
            e.wait_ge(kb.sems["pe"], pe0 + tt + 1)

            def swx(inst, k, base=base, e=e):
                inst.then_inc(kb.sems["dve"], 1)
                e.wait_ge(kb.sems["dve"], base + k)
            swx(e.tensor_copy(out=lg[:], in_=ps[0][:, 0:8]), 1)
            swx(e.tensor_reduce(out=t1[:, 0:1], in_=lg[:], axis=AX, op=OP.max), 2)
            swx(e.tensor_scalar(out=t2[:, 0:1], in0=t1[:, 0:1], scalar1=-1.0,
                                scalar2=None, op0=OP.mult), 3)
            swx(e.tensor_scalar(out=t3[:], in0=lg[:], scalar1=t1[:, 0:1],
                                scalar2=1.0e9, op0=OP.is_equal, op1=OP.mult), 4)
            swx(e.tensor_tensor(out=t4[:], in0=lg[:], in1=t3[:], op=OP.subtract), 5)
            swx(e.tensor_reduce(out=t5[:, 0:1], in_=t4[:], axis=AX, op=OP.max), 6)
            swx(e.tensor_scalar(out=t6[:], in0=lg[:], scalar1=t5[:, 0:1],
                                scalar2=None, op0=OP.is_ge), 7)
            e.nop().then_inc(kb.sems["dve"], 1)
            e.wait_ge(kb.sems["act"], ac0 + 2 * (tt + 1))
            swx(e.tensor_scalar(out=t7[:, 0:1], in0=a2[:, 0:1], scalar1=1.0,
                                scalar2=None, op0=OP.add), 9)
            swx(e.reciprocal(out=t8[:, 0:1], in_=t7[:, 0:1]), 10)
            swx(e.tensor_tensor(out=t9[:], in0=a1[:], in1=t6[:], op=OP.mult), 11)
            swx(e.tensor_scalar(out=cwall[:, tt * 8:(tt + 1) * 8], in0=t9[:],
                                scalar1=t8[:, 0:1], scalar2=None, op0=OP.mult), 12)
    kb.sec("vector", f_vector_l)
    kb.barrier()

    pe1 = kb.cnt["pe"]
    dv1 = kb.cnt["dve"]

    def f_tensor_c(e):
        for tt in range(NTT):
            if tt >= 2:
                e.wait_ge(kb.sems["dve"], dv1 + tt - 1)
            e.transpose(out=ps[4 + tt % 2][0:8, 0:128], in_=cwall[:, tt * 8:(tt + 1) * 8],
                        identity=ident[:]).then_inc(kb.sems["pe"], 1)
            kb.inc("pe")
    kb.sec("tensor", f_tensor_c)

    def f_vector_c(e):
        for tt in range(NTT):
            e.wait_ge(kb.sems["pe"], pe1 + tt + 1)
            e.tensor_copy(out=cwT[0:8, tt * 128:(tt + 1) * 128],
                          in_=ps[4 + tt % 2][0:8, 0:128]).then_inc(kb.sems["dve"], 1)
            kb.inc("dve")
    kb.sec("vector", f_vector_c)
    kb.barrier()

    pe2 = kb.cnt["pe"]
    dv2 = kb.cnt["dve"]

    def f_tensor_t(e):
        for tc in range(NTC):
            if tc >= 1:
                e.wait_ge(kb.sems["dve"], dv2 + 2 * tc)
            e.matmul(out=ps[4][:, 0:TC], lhsT=rep_sb[0:8, :],
                     rhs=cwT[0:8, tc * TC:(tc + 1) * TC], start=True, stop=True
                     ).then_inc(kb.sems["pe"], 1)
            kb.inc("pe")
            for fc in range(FC):
                ins = e.matmul(out=ps[tc % 2][:, 0:TC],
                               lhsT=at_sb[:, fc * 128:(fc + 1) * 128],
                               rhs=xhat[:, fc * TOK + tc * TC:fc * TOK + tc * TC + TC],
                               start=(fc == 0), stop=(fc == FC - 1))
            ins.then_inc(kb.sems["pe"], 1)
            kb.inc("pe")
    kb.sec("tensor", f_tensor_t)

    def f_vector_t(e):
        for tc in range(NTC):
            e.wait_ge(kb.sems["pe"], pe2 + 2 * (tc + 1))
            e.tensor_copy(out=cwt_sb[:, tc * TC:(tc + 1) * TC],
                          in_=ps[4][:, 0:TC]).then_inc(kb.sems["dve"], 1)
            e.wait_ge(kb.sems["dve"], kb.inc("dve"))
            e.tensor_tensor(out=tc_sb[:, tc * TC:(tc + 1) * TC],
                            in0=ps[tc % 2][:, 0:TC],
                            in1=cwt_sb[:, tc * TC:(tc + 1) * TC],
                            op=OP.mult).then_inc(kb.sems["dve"], 1)
            kb.inc("dve")
    kb.sec("vector", f_vector_t)
    kb.barrier()

    pe3 = kb.cnt["pe"]
    dv3 = kb.cnt["dve"]
    WT = DS(kb, ("g0", "g1"))
    WB = DS(kb, ("g2", "g3", "g4", "g5"))
    RS = DS(kb, ("l2", "l3"))

    def mm_tick(i):
        return pe3 + i + 1
    kb.cnt["pe"] = pe3 + 64

    def ad_tick(i):
        return dv3 + i + 1
    kb.cnt["dve"] = dv3 + 64

    def f_gp(e):
        ops = []
        for og in range(4):
            ops.append(("ld", og))
            if og >= 1:
                for i in range((og - 1) * 16, og * 16):
                    ops.append(("st", i))
        for i in range(48, 64):
            ops.append(("st", i))
        for kind, v in ops:
            if kind == "ld":
                if v >= 2:
                    e.wait_ge(kb.sems["pe"], mm_tick((v - 2) * 16 + 15))
                WT.issue(e, out=wtb[:, (v % 2) * 8192:(v % 2) * 8192 + 8192],
                         in_=wt1_pm[v * 128:(v + 1) * 128, :])
            else:
                e.wait_ge(kb.sems["dve"], ad_tick(v))
                og, rest = v // 16, v % 16
                oc = og * 4 + rest // 4
                tc = rest % 4
                WB.issue(e, out=resid[oc * 128:(oc + 1) * 128, tc * TC:(tc + 1) * TC],
                         in_=rosb[:, (v % 4) * TC:(v % 4) * TC + TC])
    kb.sec("gpsimd", f_gp)

    def f_sync(e):
        for i in range(64):
            if i >= 2:
                e.wait_ge(kb.sems["dve"], ad_tick(i - 2))
            og, rest = i // 16, i % 16
            oc = og * 4 + rest // 4
            tc = rest % 4
            RS.issue(e, out=rsb[:, (i % 4) * TC:(i % 4) * TC + TC],
                     in_=resid[oc * 128:(oc + 1) * 128, tc * TC:(tc + 1) * TC])
    kb.sec("sync", f_sync)

    def f_tensor_d(e):
        for og in range(4):
            WT.w(e, og)
            for ocr in range(4):
                for tc in range(NTC):
                    i = (og * 4 + ocr) * 4 + tc
                    oc = og * 4 + ocr
                    if i >= 2:
                        e.wait_ge(kb.sems["dve"], ad_tick(i - 2))
                    for fc in range(FC):
                        e.matmul(out=ps[i % 2][:, 0:TC],
                                 lhsT=wtb[:, (og % 2) * 8192 + fc * 512 + ocr * 128:(og % 2) * 8192 + fc * 512 + ocr * 128 + 128],
                                 rhs=xhat[:, fc * TOK + tc * TC:fc * TOK + tc * TC + TC],
                                 start=(fc == 0), stop=False)
                    ins = e.matmul(out=ps[i % 2][:, 0:TC],
                                   lhsT=bs_sb[:, oc * 128:(oc + 1) * 128],
                                   rhs=tc_sb[:, tc * TC:(tc + 1) * TC],
                                   start=False, stop=True)
                    ins.then_inc(kb.sems["pe"], 1)
    kb.sec("tensor", f_tensor_d)

    def f_vector_d(e):
        for i in range(64):
            e.wait_ge(kb.sems["pe"], mm_tick(i))
            RS.w(e, i)
            if i >= 4:
                WB.w(e, i - 4)
            e.tensor_tensor(out=rosb[:, (i % 4) * TC:(i % 4) * TC + TC],
                            in0=ps[i % 2][:, 0:TC],
                            in1=rsb[:, (i % 4) * TC:(i % 4) * TC + TC],
                            op=OP.add).then_inc(kb.sems["dve"], 1)
    kb.sec("vector", f_vector_d)
from contextlib import ExitStack


def build_program(mode, dumps=(), cc_mode="cc"):
    nc = bass.Bass("TRN2", target_bir_lowering=False, num_swdge_queues=4)
    io = {}

    def din(name, shape, dt):
        io[name] = nc.dram_tensor(name, shape, dt, kind="ExternalInput").ap()
        return io[name]

    def dout(name, shape, dt):
        io[name] = nc.dram_tensor(name, shape, dt, kind="ExternalOutput").ap()
        return io[name]

    def dint(name, shape, dt):
        io[name] = nc.dram_tensor(name, shape, dt, kind="Internal").ap()
        return io[name]

    din("c_ident", [128, 128], F32)
    din("c_ones_bf", [128, 1], BF16)
    din("c_ones_f32", [1, 128], F32)
    din("c_onescol", [128, 1], F32)
    din("c_isq", [128, 1], F32)
    din("c_midbias", [128, 1], F32)
    din("c_kposB", [128, NKT], F32)
    din("c_qposB", [1, TOK], F32)
    din("c_rep", [8, 128], BF16)
    din("c_qw0", [128, 1], F32)
    din("c_qw1", [128, 1], F32)

    if mode == "launch1":
        din("x", [TOK, D], F32)
        dint("resid", [D, TOK], F32)
        din("wq_pm", [D, D], F32)
        din("wk_pm", [D, D], F32)
        din("wv_pm", [4 * 128, FC * 512], F32)
        din("bq_pm", [128, FC], F32)
        din("bk_pm", [128, FC], F32)
        din("bv_row", [1, D], F32)
        dout("qbuf", [D, TOK], BF16)
        dout("cin", [2 * D, TOK], BF16)
    elif mode == "launch2":
        din("x", [TOK, D], F32)
        din("qx", [128, H * TOK], BF16)
        din("kvfull", [2 * 2 * D, TOK], BF16)
        din("wo", [D, D], F32)
        din("bo_pm", [128, FC], F32)
        din("gate_pm", [D, IFF], I32)
        din("up_pm", [D, IFF], I32)
        din("down_pm", [IFF, D], I32)
        din("sg_pm", [128, 64], F32)
        din("su_pm", [128, 64], F32)
        din("sd_pm", [128, 16], F32)
        din("wq1_pm", [D, D], F32)
        din("wk1_pm", [128, D], F32)
        din("wv1_pm", [128, D], F32)
        din("bq1_pm", [128, FC], F32)
        din("bk1_pm", [128, 1], F32)
        din("bv1_row", [1, DH], F32)
        dout("resid", [D, TOK], F32)
        dout("q1buf", [D, TOK], BF16)
        dout("cin1", [2 * DH, TOK], BF16)
        dint("abuf", [D, TOK], BF16)
        dint("h_scr", [128, (IFF // 128) * TOK], BF16)
    elif mode == "launch3":
        din("resid2", [D, TOK], F32)
        din("q1buf", [D, TOK], BF16)
        din("kvfull1", [2 * 2 * DH, TOK], BF16)
        din("wo1", [D, D], F32)
        din("bo1_pm", [128, FC], F32)
        din("wt1_pm", [512, 8192], F32)
        din("g1_pm", [128, 128], F32)
        din("at_pm", [128, 2048], F32)
        din("bs_pm", [128, 2048], F32)
        dout("y", [TOK, D], F32)
        dint("resid", [D, TOK], F32)
        dint("abuf", [D, TOK], BF16)
    elif mode == "fused":
        din("x", [TOK, D], F32)
        din("wq_pm", [D, D], F32)
        din("wk_pm", [D, D], F32)
        din("wv_pm", [4 * 128, FC * 512], F32)
        din("bq_pm", [128, FC], F32)
        din("bk_pm", [128, FC], F32)
        din("bv_row", [1, D], F32)
        din("wo", [D, D], F32)
        din("bo_pm", [128, FC], F32)
        din("gate_pm", [D, IFF], I32)
        din("up_pm", [D, IFF], I32)
        din("down_pm", [IFF, D], I32)
        din("sg_pm", [128, 64], F32)
        din("su_pm", [128, 64], F32)
        din("sd_pm", [128, 16], F32)
        din("wq1_pm", [D, D], F32)
        din("wk1_pm", [128, D], F32)
        din("wv1_pm", [128, D], F32)
        din("bq1_pm", [128, FC], F32)
        din("bk1_pm", [128, 1], F32)
        din("bv1_row", [1, DH], F32)
        din("wo1", [D, D], F32)
        din("bo1_pm", [128, FC], F32)
        din("wt1_pm", [512, 8192], F32)
        din("g1_pm", [128, 128], F32)
        din("at_pm", [128, 2048], F32)
        din("bs_pm", [128, 2048], F32)
        dout("y", [TOK, D], F32)
        dint("resid", [D, TOK], F32)
        dint("qbuf", [D, TOK], BF16)
        dint("abuf", [D, TOK], BF16)
        dint("h_scr", [128, (IFF // 128) * TOK], BF16)
        io["cin"] = nc.dram_tensor("cin", [3 * D, TOK], BF16).ap()
        io["cout"] = nc.dram_tensor("cout", [2 * 3 * D, TOK], BF16).ap()
        io["cin1"] = nc.dram_tensor("cin1", [2 * DH, TOK], BF16).ap()
        io["cout1"] = nc.dram_tensor("cout1", [2 * 2 * DH, TOK], BF16).ap()
    else:
        raise ValueError(mode)

    with ExitStack() as stack:
        ec = stack.enter_context
        xhat = ec(nc.sbuf_tensor("xhat", [128, FC * TOK], BF16))
        AB1 = ec(nc.sbuf_tensor("AB1", [128, 32768], BF16))
        AB2 = ec(nc.sbuf_tensor("AB2", [128, 8192], BF16))
        AF1 = ec(nc.sbuf_tensor("AF1", [128, 4096], F32))
        AF2 = ec(nc.sbuf_tensor("AF2", [128, 4096], F32))
        PT = ec(nc.sbuf_tensor("PT", [128, 2560], BF16))
        QS2 = ec(nc.sbuf_tensor("QS2", [128, 8192], BF16))
        AI32 = QS2[:].bitcast(I32)
        ident = ec(nc.sbuf_tensor("ident", [128, 128], F32))
        ones_bf = ec(nc.sbuf_tensor("ones_bf", [128, 1], BF16))
        ones_f32 = ec(nc.sbuf_tensor("ones_f32", [1, 128], F32))
        onescol = ec(nc.sbuf_tensor("onescol", [128, 1], F32))
        isq_vec = ec(nc.sbuf_tensor("isq_vec", [128, 1], F32))
        midbias = ec(nc.sbuf_tensor("midbias", [128, 1], F32))
        kposB = ec(nc.sbuf_tensor("kposB", [128, NKT], F32))
        rep_sb = ec(nc.sbuf_tensor("rep_sb", [8, 128], BF16))
        qw0 = ec(nc.sbuf_tensor("qw0", [128, 1], F32))
        qw1 = ec(nc.sbuf_tensor("qw1", [128, 1], F32))
        bvec = ec(nc.sbuf_tensor("bvec", [128, 256], F32))
        brow2 = AF2[0:1, 0:2048]
        ps = [ec(nc.psum_tensor(f"ps{i}", [128, 512], F32)) for i in range(7)]
        sems = {k: ec(nc.semaphore(f"s_{k}")) for k in ALLSEMS}
        blk = ec(nc.Block())

        kb = KB(nc, blk, sems)
        consts = {"ones_bf": ones_bf, "ones_f32": ones_f32, "onescol": onescol,
                  "isq_vec": isq_vec,
                  "midbias": midbias, "kposB": kposB, "qposB": AF2[0:1, 2048:4096],
                  "qposB_dram": io["c_qposB"], "ident": ident,
                  "qw0": qw0, "qw1": qw1}

        CL = DS(kb, ("l0", "l1"))

        def f_consts(e):
            pairs = [(ident[:], io["c_ident"]), (ones_bf[:], io["c_ones_bf"]),
                     (ones_f32[:], io["c_ones_f32"]), (onescol[:], io["c_onescol"]),
                     (isq_vec[:], io["c_isq"]),
                     (midbias[:], io["c_midbias"]), (kposB[:], io["c_kposB"]),
                     (rep_sb[:], io["c_rep"]), (qw0[:], io["c_qw0"]),
                     (qw1[:], io["c_qw1"])]
            bv_map = [("bq_pm", 0, FC), ("bk_pm", 16, FC), ("bo_pm", 32, FC),
                      ("sg_pm", 48, 64), ("su_pm", 112, 64), ("sd_pm", 176, 16),
                      ("bq1_pm", 192, FC), ("bk1_pm", 208, 1), ("bo1_pm", 209, FC)]
            for nm, off, wdt_ in bv_map:
                if nm in io:
                    pairs.append((bvec[:, off:off + wdt_], io[nm]))
            for dst, srcap in pairs:
                CL.issue(e, out=dst, in_=srcap, allow_slow_non_contiguous=True)
        kb.sec("sync", f_consts)
        kb.barrier()

        sb_tp = (AF1[:, 0:4096], AF2[:, 0:4096])
        sb_ln = (AF1[:, 0:1024], AB2[:, 0:1024], AB2[:, 1024:2048],
                 AF2[0:1, 0:2048], AF2[0:1, 2048:4096], AF1[0:1, 2048:4096],
                 AF1[:, 1024:2048])
        sb_proj = (AB1[:, 24576:28672], AB1[:, 28672:32768])
        sb_at = (AB1[:, 0:8192], AB1[:, 8192:16384], AB2[:, 0:4096], PT[:, 0:1024],
                 AB1[:, 16384:32768], AF1[:, 0:1024], AB2[:, 4096:8192],
                 AF1[:, 1536:2048], AF1[:, 1024:1536], AF2[0:1, 0:TOK],
                 PT[:, 1024:2048], PT[:, 2048:2560], AF1[:, 2048:4096])
        sb_atf = (AB1[:, 0:8192], AB1[:, 8192:16384], QS2[:, 0:8192], PT[:, 0:2048],
                  AB1[:, 16384:32768], AF1[:, 0:1024], AB2[:, 4096:8192],
                  AF1[:, 1536:2048], AF1[:, 1024:1536], AF2[0:1, 0:TOK],
                  AB2[:, 0:1024], AB2[:, 1024:1536], AF1[:, 2048:4096])
        sb_at1 = (AB1[:, 0:8192], AB1[:, 8192:16384], AB2[:, 0:4096], PT[:, 0:2048],
                  AB1[:, 16384:32768], AF1[:, 0:1024], AB2[:, 4096:8192],
                  AF1[:, 1536:2048], AF1[:, 1024:1536], AF2[0:1, 0:TOK],
                  PT[:, 2048:2560], PT[:, 2048:2560], AF1[:, 2048:4096])
        sb_wo = (AB1[:, 0:16384], AB1[:, 16384:32768], AF1[:, 0:2048], AF1[:, 2048:4096])

        def load_bias(pairs):
            BL = DS(kb, ("l0", "l1"))

            def f_b(e):
                for dst, srcap in pairs:
                    BL.issue(e, out=dst, in_=srcap)
            kb.sec("sync", f_b)
            kb.barrier()

        if mode == "launch1":
            ph_transpose_td(kb, io["x"], io["resid"], sb_tp, ps, NTT, ident)
            kb.barrier()
            ph_ln_v2(kb, io["resid"], xhat, sb_ln, ps, consts)
            kb.barrier()
            load_bias([(bvec[:, 0:FC], io["bq_pm"]), (bvec[:, FC:2 * FC], io["bk_pm"]),
                       (brow2[0:1, 0:D], io["bv_row"])])
            ph_proj_fm(kb, xhat, io["wq_pm"], bvec[:, 0:FC], FC, io["qbuf"], 0, sb_proj, ps)
            kb.barrier()
            ph_proj_fm(kb, xhat, io["wk_pm"], bvec[:, FC:2 * FC], FC, io["cin"], 0, sb_proj, ps)
            kb.barrier()
            sb_v = (AB1[:, 0:16384], AB1[:, 16384:24576], AF1[:, 0:512])
            ph_proj_v(kb, xhat, io["wv_pm"], brow2[0:1, 0:D], 4, 512, io["cin"], D, sb_v, ps, consts)
            kb.barrier()
        elif mode == "launch2":
            ph_transpose_td(kb, io["x"], io["resid"], sb_tp, ps, NTT, ident)
            kb.barrier()
            ph_attn(kb, 0, io["qx"], io["kvfull"], io["abuf"], sb_at, ps, consts)
            kb.barrier()
            load_bias([(bvec[:, 0:FC], io["bo_pm"])])
            ph_wo(kb, io["abuf"], io["wo"], bvec[:, 0:FC], io["resid"], sb_wo, ps)
            kb.barrier()
            if "resid_mid" in dumps:
                io["resid_mid"] = nc.dram_tensor("resid_mid_d", [D, TOK], F32,
                                                 kind="ExternalOutput").ap()

                DM = DS(kb, ("g0",))

                def f_dm(e):
                    DM.issue(e, out=io["resid_mid"], in_=io["resid"])
                kb.sec("gpsimd", f_dm)
                kb.barrier()
            ph_ln_v2(kb, io["resid"], xhat, sb_ln, ps, consts)
            kb.barrier()
            load_bias([(bvec[:, 0:64], io["sg_pm"]), (bvec[:, 64:128], io["su_pm"]),
                       (bvec[:, 128:144], io["sd_pm"])])
            sb_fa = (AI32[:, 0:4096], AB1[:, 0:32768],
                     AB2[:, 0:1024], AB2[:, 1024:2048], AB2[:, 2048:6144])
            ph_ffn0_a(kb, xhat, io["gate_pm"], io["up_pm"], io["h_scr"],
                      bvec[:, 0:64], bvec[:, 64:128], sb_fa, ps)
            kb.barrier()
            sb_fb = (AB1[:, 0:32768], xhat[:, 0:32768], AB2[:, 0:8192],
                     AF1[:, 0:2048], AF1[:, 2048:4096], AF2[:, 1024:1536])
            consts["hsr"] = AF2[0:1, 0:512]
            ph_ffn0_b(kb, io["h_scr"], io["down_pm"], bvec[:, 128:144], io["resid"], sb_fb, ps, consts)
            kb.barrier()
            ph_ln_v2(kb, io["resid"], xhat, sb_ln, ps, consts)
            kb.barrier()
            load_bias([(bvec[:, 0:FC], io["bq1_pm"]), (bvec[:, FC:FC + 1], io["bk1_pm"]),
                       (brow2[0:1, 0:DH], io["bv1_row"])])
            ph_proj_fm(kb, xhat, io["wq1_pm"], bvec[:, 0:FC], FC, io["q1buf"], 0, sb_proj, ps)
            kb.barrier()
            ph_proj_fm(kb, xhat, io["wk1_pm"], bvec[:, FC:FC + 1], 1, io["cin1"], 0, sb_proj, ps)
            kb.barrier()
            sb_v = (AB1[:, 0:16384], AB1[:, 16384:24576], AF1[:, 0:512])
            ph_proj_v(kb, xhat, io["wv1_pm"], brow2[0:1, 0:DH], 1, 128, io["cin1"], DH, sb_v, ps, consts)
            kb.barrier()
        elif mode == "launch3":
            CP = DS(kb, ("g0", "g1"))

            def f_cp(e):
                for fc in range(FC):
                    CP.issue(e, out=io["resid"][fc * 128:(fc + 1) * 128, :],
                             in_=io["resid2"][fc * 128:(fc + 1) * 128, :])
            kb.sec("gpsimd", f_cp)
            kb.barrier()
            ph_attn(kb, 1, io["q1buf"], io["kvfull1"], io["abuf"], sb_at, ps, consts)
            kb.barrier()
            load_bias([(bvec[:, 0:FC], io["bo1_pm"])])
            ph_wo(kb, io["abuf"], io["wo1"], bvec[:, 0:FC], io["resid"], sb_wo, ps)
            kb.barrier()
            ph_ln_v2(kb, io["resid"], xhat, sb_ln, ps, consts)
            kb.barrier()
            sb_f1 = (AB1[:, 0:16384], AB1[:, 16384:18432], AB1[:, 18432:20480],
                     AB1[:, 24576:24704], AB2[0:8, 0:2048], AB1[:, 20480:22528],
                     AB1[:, 22528:24576], AF1[:, 0:2048], AF1[:, 2048:4096],
                     (AF2[:, 0:8], AF2[:, 16:17], AF2[:, 17:18], AF2[:, 18:19],
                      AF2[:, 19:20], AF2[:, 20:21], AF2[:, 21:22], AF2[:, 24:32],
                      AF2[:, 32:40], AF2[:, 40:48], AF2[:, 48:56], AF2[:, 56:64],
                      AF2[:, 64:192]))
            ph_ffn1(kb, xhat, io["wt1_pm"], io["g1_pm"], io["at_pm"], io["bs_pm"],
                    rep_sb, io["resid"], sb_f1, ps, consts)
            kb.barrier()
            ph_transpose_td(kb, io["resid"], io["y"], sb_tp, ps, FC, ident)
            kb.barrier()

        if mode == "fused":
            RG = [[0, 1], [2, 3], [4, 5], [6, 7]]

            def ph_ag(cin_ap, cout_ap, which):
                def f_ag(e):
                    if cc_mode == "cc" or cc_mode == which or cc_mode.startswith("prefix"):
                        e.collective_compute("AllGather", OP.bypass, replica_groups=RG,
                                             ins=[cin_ap.opt()], outs=[cout_ap.opt()]
                                             ).then_inc(kb.sems["cc"], 1)
                        kb.inc("cc", 1)
                    else:
                        nrow = cin_ap.shape[0]
                        for rr in range(2):
                            e.dma_start(out=cout_ap[rr * nrow:(rr + 1) * nrow, :],
                                        in_=cin_ap[:, :]).then_inc(kb.sems["cc"], 16)
                            kb.inc("cc", 16)
                kb.sec("gpsimd", f_ag)
                kb.barrier()

            if cc_mode != "prefix0":
                ph_transpose_td(kb, io["x"], io["resid"], sb_tp, ps, NTT, ident)
                kb.barrier()
                ph_ln_v2(kb, io["resid"], xhat, sb_ln, ps, consts)
                kb.barrier()
                load_bias([(brow2[0:1, 0:D], io["bv_row"])])
                ph_proj_fm(kb, xhat, io["wq_pm"], bvec[:, 0:FC], FC, io["cin"], 2 * D, sb_proj, ps)
                kb.barrier()
                ph_proj_fm(kb, xhat, io["wk_pm"], bvec[:, 16:32], FC, io["cin"], 0, sb_proj, ps)
                kb.barrier()
                sb_v = (AB1[:, 0:16384], AB1[:, 16384:24576], AF1[:, 0:512])
                ph_proj_v(kb, xhat, io["wv_pm"], brow2[0:1, 0:D], 4, 512, io["cin"], D, sb_v, ps, consts)
                kb.barrier()
            ph_ag(io["cin"], io["cout"], "ag0")
            if cc_mode.startswith("prefix"):
                ph_transpose_td(kb, io["resid"], io["y"], sb_tp, ps, FC, ident)
                kb.barrier()
                fused_rest = False
            else:
                fused_rest = True
            if fused_rest and True:
                ph_attn(kb, 0, io["cout"], io["cout"], io["abuf"], sb_atf, ps, consts,
                        qmode="fused", deep=True)
                kb.barrier()
            if fused_rest:
                ph_wo(kb, io["abuf"], io["wo"], bvec[:, 32:48], io["resid"], sb_wo, ps)
                kb.barrier()
                ph_ln_v2(kb, io["resid"], xhat, sb_ln, ps, consts)
                kb.barrier()
                sb_fa = (AI32[:, 0:4096], AB1[:, 0:32768],
                         AB2[:, 0:1024], AB2[:, 1024:2048], AB2[:, 2048:6144])
                ph_ffn0_a(kb, xhat, io["gate_pm"], io["up_pm"], io["h_scr"],
                          bvec[:, 48:112], bvec[:, 112:176], sb_fa, ps)
                kb.barrier()
                sb_fb = (AB1[:, 0:32768], xhat[:, 0:32768], AB2[:, 0:8192],
                         AF1[:, 0:2048], AF1[:, 2048:4096], AF2[:, 1024:1536])
                consts["hsr"] = AF2[0:1, 0:512]
                ph_ffn0_b(kb, io["h_scr"], io["down_pm"], bvec[:, 176:192], io["resid"], sb_fb, ps, consts)
                kb.barrier()
                ph_ln_v2(kb, io["resid"], xhat, sb_ln, ps, consts)
                kb.barrier()
                load_bias([(brow2[0:1, 0:DH], io["bv1_row"])])
                ph_proj_fm(kb, xhat, io["wq1_pm"], bvec[:, 192:208], FC, io["qbuf"], 0, sb_proj, ps)
                kb.barrier()
                ph_proj_fm(kb, xhat, io["wk1_pm"], bvec[:, 208:209], 1, io["cin1"], 0, sb_proj, ps)
                kb.barrier()
                sb_v = (AB1[:, 0:16384], AB1[:, 16384:24576], AF1[:, 0:512])
                ph_proj_v(kb, xhat, io["wv1_pm"], brow2[0:1, 0:DH], 1, 128, io["cin1"], DH, sb_v, ps, consts)
                kb.barrier()
                ph_ag(io["cin1"], io["cout1"], "ag1")
                ph_attn(kb, 1, io["qbuf"], io["cout1"], io["abuf"], sb_at1, ps, consts, deep=True)
                kb.barrier()
                ph_wo(kb, io["abuf"], io["wo1"], bvec[:, 209:225], io["resid"], sb_wo, ps)
                kb.barrier()
                ph_ln_v2(kb, io["resid"], xhat, sb_ln, ps, consts)
                kb.barrier()
                sb_f1 = (AB1[:, 0:16384], AB1[:, 16384:18432], AB1[:, 18432:20480],
                         AB1[:, 24576:24704], AB2[0:8, 0:2048], AB1[:, 20480:22528],
                         AB1[:, 22528:24576], AF1[:, 0:2048], AF1[:, 2048:4096],
                         (AF2[:, 0:8], AF2[:, 16:17], AF2[:, 17:18], AF2[:, 18:19],
                          AF2[:, 19:20], AF2[:, 20:21], AF2[:, 21:22], AF2[:, 24:32],
                          AF2[:, 32:40], AF2[:, 40:48], AF2[:, 48:56], AF2[:, 56:64],
                          AF2[:, 64:192]))
                ph_ffn1(kb, xhat, io["wt1_pm"], io["g1_pm"], io["at_pm"], io["bs_pm"],
                        rep_sb, io["resid"], sb_f1, ps, consts)
                kb.barrier()
                ph_transpose_td(kb, io["resid"], io["y"], sb_tp, ps, FC, ident)
                kb.barrier()

        for name in dumps:
            if name == 'resid_mid':
                continue
            src = io[name]
            dst = dout(name + "_d", list(src.tensor.shape), src.tensor.dtype)

            DD = DS(kb, ("g0",))

            def f_dump(e, src=src, dst=dst, DD=DD):
                DD.issue(e, out=dst, in_=src)
            kb.sec("gpsimd", f_dump)
            kb.barrier()

        def f_final(e):
            for nm in DMASEMS:
                if kb.cnt[nm] > 0:
                    e.wait_ge(kb.sems[nm], kb.cnt[nm])
        kb.sec("gpsimd", f_final)

    return nc


def pm4(w, n_oc, ocw):
    return np.ascontiguousarray(
        w.reshape(FC, 128, n_oc, ocw).transpose(2, 1, 0, 3).reshape(n_oc * 128, FC * ocw))


def make_consts(core):
    hlf = core % 2
    qpos = (np.arange(TOK, dtype=np.float64) + hlf * TOK) * 1e6
    kpos = (np.arange(SEQ, dtype=np.float64)) * 1e6
    rep = np.zeros((8, 128), np.float32)
    for ee in range(8):
        rep[ee, ee * 16:(ee + 1) * 16] = 16.0
    return {
        "c_ident": np.eye(128, dtype=np.float32),
        "c_ones_bf": np.ones((128, 1), dtype=BF16NP),
        "c_ones_f32": np.ones((1, 128), dtype=np.float32),
        "c_onescol": np.ones((128, 1), dtype=np.float32),
        "c_isq": np.full((128, 1), ISQ, np.float32),
        "c_midbias": np.full((128, 1), 0.0 if hlf == 1 else NEG, np.float32),
        "c_kposB": kpos.reshape(NKT, 128).T.astype(np.float32).copy(),
        "c_qposB": qpos.reshape(1, TOK).astype(np.float32),
        "c_rep": rep.astype(BF16NP),
        "c_qw0": np.full((128, 1), 1.0 if hlf == 0 else 0.0, np.float32),
        "c_qw1": np.full((128, 1), 1.0 if hlf == 1 else 0.0, np.float32),
    }


def host_weights(ins):
    w = {}
    w["wq_pm"] = pm4(ins["L0_attn_Wq"], 16, 128)
    w["wk_pm"] = pm4(ins["L0_attn_Wk"], 16, 128)
    w["wv_pm"] = pm4(ins["L0_attn_Wv"], 4, 512)
    w["bq_pm"] = np.ascontiguousarray(ins["L0_attn_bq"].reshape(16, 128).T)
    w["bk_pm"] = np.ascontiguousarray(ins["L0_attn_bk"].reshape(16, 128).T)
    w["bv_row"] = ins["L0_attn_bv"].reshape(1, D)
    w["wo"] = ins["L0_attn_Wo"]
    w["bo_pm"] = np.ascontiguousarray(ins["L0_attn_bo"].reshape(16, 128).T)
    w["gate_pm"] = np.ascontiguousarray(
        ins["L0_ffn_gate_w"].reshape(16, 512, 16, 128).transpose(0, 3, 2, 1).reshape(2048, 8192))
    w["up_pm"] = np.ascontiguousarray(
        ins["L0_ffn_up_w"].reshape(16, 512, 16, 128).transpose(0, 3, 2, 1).reshape(2048, 8192))
    w["down_pm"] = np.ascontiguousarray(
        ins["L0_ffn_down_w"].reshape(16, 128, 64, 128).transpose(2, 3, 0, 1).reshape(8192, 2048))
    w["sg_pm"] = np.ascontiguousarray(ins["L0_ffn_gate_scale"].reshape(64, 128).T)
    w["su_pm"] = np.ascontiguousarray(ins["L0_ffn_up_scale"].reshape(64, 128).T)
    w["sd_pm"] = np.ascontiguousarray(ins["L0_ffn_down_scale"].reshape(16, 128).T)
    w["wq1_pm"] = pm4(ins["L1_attn_Wq"], 16, 128)
    w["wk1_pm"] = pm4(ins["L1_attn_Wk"], 1, 128)
    w["wv1_pm"] = pm4(ins["L1_attn_Wv"], 1, 128)
    w["bq1_pm"] = np.ascontiguousarray(ins["L1_attn_bq"].reshape(16, 128).T)
    w["bk1_pm"] = np.ascontiguousarray(ins["L1_attn_bk"].reshape(1, 128).T)
    w["bv1_row"] = ins["L1_attn_bv"].reshape(1, DH)
    w["wo1"] = ins["L1_attn_Wo"]
    w["bo1_pm"] = np.ascontiguousarray(ins["L1_attn_bo"].reshape(16, 128).T)
    w["wt1_pm"] = np.ascontiguousarray(
        ins["L1_ffn_weight"].reshape(4, 512, 16, 128).transpose(0, 3, 2, 1).reshape(512, 8192))
    w["g1_pm"] = np.ascontiguousarray(
        ins["L1_ffn_gate_w"].reshape(8, 16, 128).transpose(2, 1, 0).reshape(128, 128))
    w["at_pm"] = np.ascontiguousarray(
        ins["L1_ffn_A_w"].reshape(128, 16, 128).transpose(2, 1, 0).reshape(128, 2048))
    w["bs_pm"] = np.ascontiguousarray(
        ins["L1_ffn_B_w"].transpose(0, 2, 1).reshape(128, 2048))
    return {k: (v.astype(np.float32) if v.dtype == np.float64 else v) for k, v in w.items()}


_BUILD_CACHE = {}


def _get_prog(mode):
    if mode not in _BUILD_CACHE:
        _BUILD_CACHE[mode] = build_program(mode)
    return _BUILD_CACHE[mode]


def kernel(**inputs):
    x = np.ascontiguousarray(inputs["L0_attn_x"]).astype(np.float32, copy=False)
    w = host_weights(inputs)
    nc = _get_prog("fused")
    wkeys = ("wq_pm", "wk_pm", "wv_pm", "bq_pm", "bk_pm", "bv_row",
             "wo", "bo_pm", "gate_pm", "up_pm", "down_pm", "sg_pm", "su_pm", "sd_pm",
             "wq1_pm", "wk1_pm", "wv1_pm", "bq1_pm", "bk1_pm", "bv1_row",
             "wo1", "bo1_pm", "wt1_pm", "g1_pm", "at_pm", "bs_pm")
    maps = []
    for c in range(8):
        b, hlf = c // 2, c % 2
        m = {"x": np.ascontiguousarray(x[b, hlf * TOK:(hlf + 1) * TOK, :])}
        m.update({k: w[k] for k in wkeys})
        m.update(make_consts(c))
        maps.append(m)
    res = run_bass_kernel_spmd(nc, maps, core_ids=list(range(8)))
    y = np.zeros((4, SEQ, D), np.float32)
    for c in range(8):
        b, hlf = c // 2, c % 2
        y[b, hlf * TOK:(hlf + 1) * TOK, :] = res.results[c]["y"]
    return y
